# version 28
# baseline (speedup 1.0000x reference)
import numpy as np
import ml_dtypes
from contextlib import ExitStack

import concourse.bass as bass
import concourse.mybir as mybir
from concourse.bass_utils import run_bass_kernel_spmd

F32 = mybir.dt.float32
BF16 = mybir.dt.bfloat16
I32 = mybir.dt.int32
AF = mybir.ActivationFunctionType
ALU = mybir.AluOpType
AX = mybir.AxisListType

EPS_LN = 1e-5
EPS_PRIOR = 1e-12
SCALING = 2.0
N_CORES = 8


class Dims:
    def __init__(self, H=4096, D=1024, S_own=1024, S_all=2048, E_tot=16, E_act=4, R=8):
        self.H, self.D, self.S_own, self.S_all = H, D, S_own, S_all
        self.E_tot, self.E_act, self.R = E_tot, E_act, R
        self.ER = E_act * R
        self.HC = H // 128
        self.DC = D // 128
        self.FC = 3 * self.DC
        self.OC = self.HC
        self.TN = min(512, S_own)
        self.NCo = S_own // self.TN
        self.NCa = S_all // self.TN
        self.W1 = min(1024, S_all)
        self.N1 = S_all // self.W1
        self.QT = S_own // 128
        self.KC = S_all // 128
        self.scale = 1.0 / float(np.sqrt(float(D)))


class Sem:
    def __init__(self, nc, st, name):
        self.h = st.enter_context(nc.semaphore(name))
        self.n = 0

    def inc(self, inst, k=1):
        inst.then_inc(self.h, k)
        self.n += k
        return self.n


class DStream:

    def __init__(self, sems):
        self.sems = sems
        self.base = [s.n for s in sems]

    def begin(self):
        self.base = [s.n for s in self.sems]

    def issue(self, inst, i):
        s = self.sems[i % len(self.sems)]
        inst.then_inc(s.h, 16)
        s.n += 16

    def wait(self, eng, i):
        nb = len(self.sems)
        s = self.sems[i % nb]
        eng.wait_ge(s.h, self.base[i % nb] + 16 * (i // nb + 1))


def build_program(d: Dims):
    nc = bass.Bass("TRN2", target_bir_lowering=False)
    H, D, S_own, S_all = d.H, d.D, d.S_own, d.S_all
    HC, DC, FC, OC = d.HC, d.DC, d.FC, d.OC
    TN, NCo, NCa = d.TN, d.NCo, d.NCa
    QT, KC, ER = d.QT, d.KC, d.ER
    EA = d.E_act

    x_fm = nc.dram_tensor("x_fm", [H, S_own], F32, kind="ExternalInput")
    wqkv_t = nc.dram_tensor("wqkv_t", [FC, 128, HC, 128], F32, kind="ExternalInput")
    wo_t = nc.dram_tensor("wo_t", [OC, 128, DC, 128], F32, kind="ExternalInput")
    wb_t = nc.dram_tensor("wb_t", [OC, 128, HC, 128], F32, kind="ExternalInput")
    bqkv = nc.dram_tensor("bqkv", [3 * D], F32, kind="ExternalInput")
    bo = nc.dram_tensor("bo", [H], F32, kind="ExternalInput")
    bb = nc.dram_tensor("bb", [H], F32, kind="ExternalInput")
    loraA = nc.dram_tensor("loraA", [d.E_tot * d.R, H], F32, kind="ExternalInput")
    loraBrt = nc.dram_tensor("loraBrt", [d.E_tot * d.R, H], F32, kind="ExternalInput")
    routerW = nc.dram_tensor("routerW", [d.E_tot, H], F32, kind="ExternalInput")
    routerb = nc.dram_tensor("routerb", [d.E_tot, 1], F32, kind="ExternalInput")
    prior = nc.dram_tensor("prior", [d.E_tot, 1], F32, kind="ExternalInput")
    active = nc.dram_tensor("active", [EA], I32, kind="ExternalInput")
    out = nc.dram_tensor("out", [H, S_own], F32, kind="ExternalOutput")

    kvK_mine = nc.dram_tensor("kvK_mine", [DC * 128, S_own], BF16, kind="Internal")
    kvV_mine = nc.dram_tensor("kvV_mine", [DC * 128, S_own], BF16, kind="Internal")
    kvK_g = nc.dram_tensor("kvK_g", [2 * DC * 128, S_own], BF16, kind="Internal")
    kvV_g = nc.dram_tensor("kvV_g", [2 * DC * 128, S_own], BF16, kind="Internal")
    x2_spill = nc.dram_tensor("x2_spill", [HC, 128, S_own], F32, kind="Internal")

    eye_bf16_c = nc.inline_tensor(np.asarray(np.eye(128, dtype=ml_dtypes.bfloat16)), name="c_eye_b")
    eye_f32_c = nc.inline_tensor(np.eye(128, dtype=np.float32), name="c_eye_f")
    ones128_b_c = nc.inline_tensor(np.ones((128, 1), dtype=ml_dtypes.bfloat16), name="c_ones128b")
    ones_sq_c = nc.inline_tensor(np.ones((128, 128), dtype=ml_dtypes.bfloat16), name="c_ones_sq")
    ones_1x128_c = nc.inline_tensor(np.ones((1, 128), dtype=np.float32), name="c_ones1x128")
    quarter4_c = nc.inline_tensor(np.full((EA, 1), 1.0 / EA, dtype=np.float32), name="c_quart")
    ones4_c = nc.inline_tensor(np.ones((EA, 1), dtype=np.float32), name="c_ones4")
    ones_1x4_c = nc.inline_tensor(np.ones((1, EA), dtype=np.float32), name="c_ones1x4")
    rrep8 = np.zeros((EA, ER), dtype=np.float32)
    r2m = np.zeros((EA, ER), dtype=np.float32)
    for e in range(EA):
        rrep8[e, e * d.R:(e + 1) * d.R] = float(d.R)
        r2m[e, e * d.R:(e + 1) * d.R] = SCALING
    rrep8_c = nc.inline_tensor(rrep8, name="c_rrep8")
    r2m_c = nc.inline_tensor(r2m, name="c_r2m")
    modR_c = nc.inline_tensor((np.arange(ER, dtype=np.float32) % d.R).reshape(ER, 1), name="c_modR")
    cpool_c = nc.inline_tensor(np.tile(np.array([[EPS_LN, EPS_PRIOR]], dtype=np.float32), (128, 1)), name="c_cpool")

    G = ExitStack()
    with G:
        qd = Sem(nc, G, "qd")
        qg = Sem(nc, G, "qg")
        kld = Sem(nc, G, "kld")
        ccK = Sem(nc, G, "ccK")
        ccV = Sem(nc, G, "ccV")
        w01 = [Sem(nc, G, f"w{i}") for i in range(3)]
        l01 = [Sem(nc, G, f"l{i}") for i in range(6)]
        s01 = [Sem(nc, G, f"s{i}") for i in range(4)]
        x01 = [Sem(nc, G, f"x{i}") for i in range(8)]
        wS = DStream(w01)
        lS = DStream(l01)
        sS = DStream(s01)
        xS = DStream(x01)
        pe = Sem(nc, G, "pe")
        dv = Sem(nc, G, "dv")
        ac = Sem(nc, G, "ac")
        pp = Sem(nc, G, "pp")
        bar = Sem(nc, G, "bar")

        def barrier():
            with nc.Block() as blk:
                tgt = bar.n + 5

                def mk(fences):
                    def f(eng):
                        for (s, n) in fences:
                            eng.wait_ge(s.h, n)
                        eng.sem_inc(bar.h, 1)
                        eng.wait_ge(bar.h, tgt)
                    return f
                blk.sync(mk([(qd, qd.n), (kld, kld.n)] +
                            [(x, x.n) for x in l01] + [(x, x.n) for x in s01]))
                blk.tensor(mk([(pe, pe.n)]))
                blk.vector(mk([(dv, dv.n)]))
                blk.scalar(mk([(ac, ac.n)]))
                blk.gpsimd(mk([(qg, qg.n), (pp, pp.n)] +
                               [(x, x.n) for x in w01]))
            bar.n += 5

        sb = lambda *a, **k: nc.sbuf_tensor(*a, **k)
        ps = lambda *a, **k: nc.psum_tensor(*a, **k)

        AW = ER + EA
        AW_T = G.enter_context(sb("AW_T", [128, HC * AW], BF16))
        Bcat = G.enter_context(sb("Bcat", [ER, H], BF16))
        bqkv_pp = G.enter_context(sb("bqkv_pp", [128, FC], F32))
        bo_pp = G.enter_context(sb("bo_pp", [128, OC], F32))
        bb_pp = G.enter_context(sb("bb_pp", [128, OC], F32))
        eye_b = G.enter_context(sb("eye_b", [128, 128], BF16))
        eye_f = G.enter_context(sb("eye_f", [128, 128], F32))
        ones128b = G.enter_context(sb("ones128b", [128, 1], BF16))
        ones_sq = G.enter_context(sb("ones_sq", [128, 128], BF16))
        ones1x128 = G.enter_context(sb("ones1x128", [1, 128], F32))
        quart4 = G.enter_context(sb("quart4", [EA, 1], F32))
        ones4 = G.enter_context(sb("ones4", [EA, 1], F32))
        ones1x4 = G.enter_context(sb("ones1x4", [1, EA], F32))
        rrep8_s = G.enter_context(sb("rrep8_s", [EA, ER], F32))
        r2m_s = G.enter_context(sb("r2m_s", [EA, ER], F32))
        modR_s = G.enter_context(sb("modR_s", [ER, 1], F32))
        act_i = G.enter_context(sb("act_i", [EA, 1], I32))
        actf = G.enter_context(sb("actf", [EA, 1], F32))
        idxf = G.enter_context(sb("idxf", [ER, 1], F32))
        idx_i = G.enter_context(sb("idx_i", [ER, 1], I32))
        rb_g = G.enter_context(sb("rb_g", [EA, 1], F32))
        prior_g = G.enter_context(sb("prior_g", [EA, 1], F32))
        bias_r = G.enter_context(sb("bias_r", [EA, 1], F32))
        cpool = G.enter_context(sb("cpool", [128, 2], F32))
        nc.const_aps.register(F32, EPS_LN, cpool[:, 0:1])
        nc.const_aps.register(F32, EPS_PRIOR, cpool[:, 1:2])
        r2b = G.enter_context(sb("r2b", [128, S_own], F32))
        negmr2b = G.enter_context(sb("negmr2b", [128, S_own], F32))

        SQ = ExitStack()
        SQ.__enter__()
        Q_fm = SQ.enter_context(sb("Q_fm", [128, DC * S_own], BF16, side="right"))
        wcol = [SQ.enter_context(sb(f"wcol{i}", [128, HC * 128], BF16, side="right"))
                for i in range(2)]
        W1L = min(1024, S_own)
        if HC % 4 == 0 and S_own == W1L:
            CH1 = 4
        elif HC % 2 == 0 and S_own == W1L:
            CH1 = 2
        else:
            CH1 = 1
        wS.begin()
        xS.begin()
        col_order = list(range(DC, 3 * DC)) + list(range(DC))
        xv = x_fm.ap().rearrange("(c p) t -> p c t", p=128)
        K_fm = SQ.enter_context(sb("K_fm", [128, DC * S_all], BF16))
        SLn = ExitStack()
        SLn.__enter__()
        LNx = SLn.enter_context(sb("LNx", [128, HC * S_own], BF16))

        with ExitStack() as S0:
            Acat = S0.enter_context(sb("Acat", [ER, H], F32))
            Bcat_f = S0.enter_context(sb("Bcat_f", [ER, H], F32))
            Wr_g = S0.enter_context(sb("Wr_g", [EA, H], F32))
            brow_q = S0.enter_context(sb("brow_q", [FC, 128], F32))
            brow_o = S0.enter_context(sb("brow_o", [OC, 128], F32))
            brow_b = S0.enter_context(sb("brow_b", [OC, 128], F32))
            with ExitStack() as SP:
                p0x = [SP.enter_context(ps(f"p0x{i}", [128, 64], F32)) for i in range(4)]
                psi = SP.enter_context(ps("psi", [ER, 1], F32))

                with nc.Block() as blk:
                    qd0 = qd.n

                    def _sync(eng):
                        for dst, src in [
                            (eye_b[:], eye_bf16_c.ap()), (eye_f[:], eye_f32_c.ap()),
                            (ones128b[:], ones128_b_c.ap()), (ones_sq[:], ones_sq_c.ap()),
                            (ones1x128[:], ones_1x128_c.ap()),
                            (quart4[:], quarter4_c.ap()), (ones4[:], ones4_c.ap()),
                            (ones1x4[:], ones_1x4_c.ap()), (rrep8_s[:], rrep8_c.ap()),
                            (r2m_s[:], r2m_c.ap()), (modR_s[:], modR_c.ap()),
                            (act_i[:], active.rearrange("(p o) -> p o", o=1)),
                            (brow_q[:], bqkv.rearrange("(p j) -> p j", j=128)),
                            (brow_o[:], bo.rearrange("(p j) -> p j", j=128)),
                            (brow_b[:], bb.rearrange("(p j) -> p j", j=128)),
                            (cpool[:], cpool_c.ap()),
                        ]:
                            eng.dma_start(out=dst, in_=src).then_inc(qd.h, 16)
                    blk.sync(_sync)
                    qd.n = qd0 + 16 * 16
                    LOADS0 = qd.n

                    dv0 = dv.n
                    pe0 = pe.n

                    def _vector(eng):
                        eng.wait_ge(qd.h, LOADS0)
                        dv.inc(nc.vector.tensor_copy(out=actf[:], in_=act_i[:]))
                        eng.wait_ge(pe.h, pe0 + 1)
                        dv.inc(nc.vector.tensor_scalar(out=idxf[:], in0=psi[:],
                                                       scalar1=modR_s[:], scalar2=None,
                                                       op0=ALU.add))
                        eng.wait_ge(dv.h, dv0 + 2)
                        dv.inc(nc.vector.tensor_copy(out=idx_i[:], in_=idxf[:]))
                    blk.vector(_vector)
                    ACTF, IDXI = dv0 + 1, dv0 + 3

                    def _tensor(eng):
                        eng.wait_ge(dv.h, ACTF)
                        pe.inc(nc.tensor.matmul(psi[:], rrep8_s[:], actf[:],
                                                start=True, stop=True))
                    blk.tensor(_tensor)

                    def _gpsimd(eng):
                        for i in range(min(2, HC // CH1)):
                            xS.issue(nc.gpsimd.dma_start(
                                out=LNx[:, i * CH1 * S_own:(i + 1) * CH1 * S_own],
                                in_=xv[:, i * CH1:(i + 1) * CH1, :]), i)
                        eng.wait_ge(dv.h, IDXI)
                        qg.inc(nc.gpsimd.indirect_dma_start(
                            out=Acat[:], out_offset=None, in_=loraA.ap(),
                            in_offset=bass.IndirectOffsetOnAxis(ap=idx_i[:], axis=0)), 16)
                        qg.inc(nc.gpsimd.indirect_dma_start(
                            out=Bcat_f[:], out_offset=None, in_=loraBrt.ap(),
                            in_offset=bass.IndirectOffsetOnAxis(ap=idx_i[:], axis=0)), 16)
                        eng.wait_ge(qd.h, LOADS0)
                        qg.inc(nc.gpsimd.indirect_dma_start(
                            out=Wr_g[:], out_offset=None, in_=routerW.ap(),
                            in_offset=bass.IndirectOffsetOnAxis(ap=act_i[:], axis=0)), 16)
                        qg.inc(nc.gpsimd.indirect_dma_start(
                            out=rb_g[:], out_offset=None, in_=routerb.ap(),
                            in_offset=bass.IndirectOffsetOnAxis(ap=act_i[:], axis=0)), 16)
                        qg.inc(nc.gpsimd.indirect_dma_start(
                            out=prior_g[:], out_offset=None, in_=prior.ap(),
                            in_offset=bass.IndirectOffsetOnAxis(ap=act_i[:], axis=0)), 16)
                        for i in range(2, HC // CH1):
                            xS.issue(nc.gpsimd.dma_start(
                                out=LNx[:, i * CH1 * S_own:(i + 1) * CH1 * S_own],
                                in_=xv[:, i * CH1:(i + 1) * CH1, :]), i)
                    blk.gpsimd(_gpsimd)
                    GATH = qg.n

                jobs = [(brow_q[:, :], FC, bqkv_pp[:, :], F32),
                        (brow_o[:, :], OC, bo_pp[:, :], F32),
                        (brow_b[:, :], OC, bb_pp[:, :], F32)]
                for hcb in range(HC):
                    jobs.append((Acat[:, hcb * 128:(hcb + 1) * 128], ER,
                                 AW_T[:, hcb * AW:hcb * AW + ER], BF16))
                for hcb in range(HC):
                    jobs.append((Wr_g[:, hcb * 128:(hcb + 1) * 128], EA,
                                 AW_T[:, hcb * AW + ER:(hcb + 1) * AW], BF16))

                with nc.Block() as blk:
                    pe0 = pe.n
                    dv0 = dv.n
                    ac0 = ac.n

                    def _tensor(eng):
                        eng.wait_ge(qd.h, LOADS0)
                        for j, (src, npart, dst, dt_) in enumerate(jobs):
                            if j == 3:
                                eng.wait_ge(qg.h, GATH)
                            dstp = p0x[j % 4]
                            if j >= 4:
                                eng.wait_ge(dv.h, dv0 + j - 3)
                            pe.inc(nc.tensor.transpose(dstp[:, 0:npart], src,
                                                       eye_f[0:npart, 0:npart]))
                    blk.tensor(_tensor)

                    def _scalar(eng):
                        eng.wait_ge(qd.h, LOADS0)
                        eng.wait_ge(qg.h, GATH)
                        ac.inc(nc.scalar.activation(out=prior_g[:], in_=prior_g[:],
                                                    func=AF.Ln, bias=EPS_PRIOR, scale=1.0))
                    blk.scalar(_scalar)
                    LN_AC = ac.n

                    def _vector(eng):
                        for j, (src, npart, dst, dt_) in enumerate(jobs):
                            dstp = p0x[j % 4]
                            eng.wait_ge(pe.h, pe0 + j + 1)
                            dv.inc(nc.vector.tensor_copy(out=dst, in_=dstp[:, 0:npart]))
                        eng.wait_ge(qg.h, GATH)
                        dv.inc(nc.vector.tensor_copy(out=Bcat[:], in_=Bcat_f[:]))
                        eng.wait_ge(ac.h, LN_AC)
                        dv.inc(nc.vector.tensor_add(out=bias_r[:], in0=rb_g[:], in1=prior_g[:]))
                    blk.vector(_vector)

            barrier()

        if True:
            with ExitStack() as SL:
                r1b = SL.enter_context(sb("r1b", [128, S_own], F32))
                with ExitStack() as SR:
                    negmr1b = SL.enter_context(sb("negmr1b", [128, S_own], F32))

                    N1L = S_own // W1L
                    with ExitStack() as S1:
                        xsq = [S1.enter_context(sb(f"xsq{i}", [128, CH1 * W1L], BF16,
                                                   side="right"))
                               for i in range(2)]
                        rowA = S1.enter_context(sb("rowA", [1, S_own], F32))
                        rowB = S1.enter_context(sb("rowB", [1, S_own], F32))
                        rowC = S1.enter_context(sb("rowC", [1, S_own], F32))
                        rowD = S1.enter_context(sb("rowD", [1, S_own], F32))

                        NSTAT = S_own // TN
                        with ExitStack() as SP:
                            st_sum = [SP.enter_context(ps(f"st_s{i}", [1, TN], F32))
                                      for i in range(NSTAT)]
                            st_sq = [SP.enter_context(ps(f"st_q{i}", [1, TN], F32))
                                     for i in range(NSTAT)]
                            chunks = list(range(HC // CH1))
                            NCHU = len(chunks)
                            TPC = (CH1 * W1L) // TN
                            ac0, pe0 = ac.n, pe.n

                            with nc.Block() as blk:
                                def _gpsimd(eng):
                                    for ci in range(2):
                                        wS.issue(nc.gpsimd.dma_start(
                                            out=wcol[ci % 2][:, 0:HC * 128],
                                            in_=wqkv_t[col_order[ci]].rearrange(
                                                "p hc j -> p (hc j)")), ci)
                                blk.gpsimd(_gpsimd)

                                def _scalar(eng):
                                    for i in chunks:
                                        xS.wait(eng, i)
                                        if i >= 2:
                                            eng.wait_ge(pe.h, pe0 + (i - 2) + 1)
                                        sl = LNx[:, i * CH1 * S_own:(i + 1) * CH1 * S_own]
                                        ac.inc(nc.scalar.activation(out=xsq[i % 2][:],
                                                                    in_=sl, func=AF.Square))

                                def _tensor(eng):
                                    for i in chunks:
                                        xS.wait(eng, i)
                                        eng.wait_ge(ac.h, ac0 + i + 1)
                                        sl = LNx[:, i * CH1 * S_own:(i + 1) * CH1 * S_own]
                                        last = None
                                        for t in range(TPC):
                                            nidx = t % NSTAT
                                            nc.tensor.matmul(
                                                st_sum[nidx][:], ones128b[:],
                                                sl[:, t * TN:(t + 1) * TN],
                                                start=(i == 0 and t < NSTAT),
                                                stop=(i == NCHU - 1 and t >= TPC - NSTAT))
                                            last = nc.tensor.matmul(
                                                st_sq[nidx][:], ones128b[:],
                                                xsq[i % 2][:, t * TN:(t + 1) * TN],
                                                start=(i == 0 and t < NSTAT),
                                                stop=(i == NCHU - 1 and t >= TPC - NSTAT))
                                        pe.inc(last)

                                blk.scalar(_scalar)
                                blk.tensor(_tensor)
                            PE_STATS = pe.n

                            with nc.Block() as blk:
                                def _vector(eng):
                                    eng.wait_ge(pe.h, PE_STATS)
                                    for nidx in range(NSTAT):
                                        dv.inc(nc.vector.tensor_scalar(
                                            out=rowA[0:1, nidx * TN:(nidx + 1) * TN],
                                            in0=st_sum[nidx][:], scalar1=1.0 / H,
                                            scalar2=None, op0=ALU.mult))
                                        dv.inc(nc.vector.tensor_scalar(
                                            out=rowB[0:1, nidx * TN:(nidx + 1) * TN],
                                            in0=st_sq[nidx][:], scalar1=1.0 / H,
                                            scalar2=None, op0=ALU.mult))
                                blk.vector(_vector)
                            ROWS_DONE = dv.n

                        def ln_finalize(rowA_, rowB_, rowC_, rowD_, dstR, dstM, width):
                            nslice = width // TN
                            with ExitStack() as SPb:
                                pbcR = [SPb.enter_context(ps(f"pbR_{id(rowA_)}_{i}",
                                                             [128, TN], F32))
                                        for i in range(2)]
                                pbcM = [SPb.enter_context(ps(f"pbM_{id(rowA_)}_{i}",
                                                             [128, TN], F32))
                                        for i in range(2)]
                                ac0, dv0, pe0 = ac.n, dv.n, pe.n
                                with nc.Block() as blk:
                                    def _scalar(eng):
                                        ac.inc(nc.scalar.activation(
                                            out=rowC_[0:1, 0:width], in_=rowA_[0:1, 0:width],
                                            func=AF.Square))
                                        eng.wait_ge(dv.h, dv0 + 1)
                                        ac.inc(nc.scalar.activation(
                                            out=rowC_[0:1, 0:width], in_=rowD_[0:1, 0:width],
                                            func=AF.Ln, bias=EPS_LN))
                                        eng.wait_ge(ac.h, ac0 + 2)
                                        ac.inc(nc.scalar.activation(
                                            out=rowD_[0:1, 0:width], in_=rowC_[0:1, 0:width],
                                            func=AF.Exp, scale=-0.5))

                                    def _vector(eng):
                                        eng.wait_ge(ac.h, ac0 + 1)
                                        dv.inc(nc.vector.tensor_sub(
                                            out=rowD_[0:1, 0:width], in0=rowB_[0:1, 0:width],
                                            in1=rowC_[0:1, 0:width]))
                                        for ss in range(nslice):
                                            eng.wait_ge(pe.h, pe0 + 2 * ss + 1)
                                            dv.inc(nc.vector.tensor_copy(
                                                out=dstR[:, ss * TN:(ss + 1) * TN],
                                                in_=pbcR[ss % 2][:]))
                                            eng.wait_ge(pe.h, pe0 + 2 * ss + 2)
                                            dv.inc(nc.vector.tensor_copy(
                                                out=dstM[:, ss * TN:(ss + 1) * TN],
                                                in_=pbcM[ss % 2][:]))

                                    def _tensor(eng):
                                        for ss in range(nslice):
                                            eng.wait_ge(ac.h, ac0 + 3)
                                            if ss >= 2:
                                                eng.wait_ge(dv.h, dv0 + 2 * (ss - 2) + 2)
                                            pe.inc(nc.tensor.matmul(
                                                pbcR[ss % 2][:], ones1x128[:],
                                                rowD_[0:1, ss * TN:(ss + 1) * TN],
                                                start=True, stop=True))
                                            if ss >= 2:
                                                eng.wait_ge(dv.h, dv0 + 2 * (ss - 2) + 3)
                                            pe.inc(nc.tensor.matmul(
                                                pbcM[ss % 2][:], ones1x128[:],
                                                rowA_[0:1, ss * TN:(ss + 1) * TN],
                                                start=True, stop=True))

                                    blk.scalar(_scalar)
                                    blk.vector(_vector)
                                    blk.tensor(_tensor)

                        ln_finalize(rowA, rowB, rowC, rowD, r1b, negmr1b, S_own)
                        barrier()

                    APL = [0, 0]

                with ExitStack() as S2b:
                    kvbuf = [S2b.enter_context(sb(f"kvbuf{i}", [128, S_own], BF16,
                                                  side="right")) for i in range(2)]
                    with ExitStack() as SP:
                        pqk = [SP.enter_context(ps(f"pqk{i}", [128, TN], F32))
                               for i in range(2)]
                        cols = []
                        g = 0
                        for fc in col_order:
                            is_q = fc < DC
                            cols.append({"fc": fc, "is_q": is_q, "ncs": NCo, "g0": g})
                            g += NCo
                        kv_cols = [c for c in cols if not c["is_q"]]
                        for k, c in enumerate(kv_cols):
                            c["kv_idx"] = k
                        pe0, ac0, dvb0 = pe.n, ac.n, dv.n
                        sS.begin()

                        if NCo == 1:
                            gseq = [(ci, 0) for ci in range(len(cols))]
                        else:
                            gseq = [(0, 0), (1, 0), (0, 1), (1, 1)]
                            for ci in range(2, len(cols)):
                                for ncb in range(NCo):
                                    gseq.append((ci, ncb))
                        lastpos = {}
                        for p, (ci, ncb) in enumerate(gseq):
                            lastpos[ci] = p
                        multmark = [0] * len(gseq)

                        def pe_mark(ci):
                            return pe0 + lastpos[ci] + 1

                        with nc.Block() as blk:
                            rgroups = [[2 * i, 2 * i + 1] for i in range(N_CORES // 2)]

                            def _gpsimd(eng):
                                for ci, c in enumerate(cols):
                                    if ci >= 2:
                                        eng.wait_ge(pe.h, pe_mark(ci - 2))
                                        wS.issue(nc.gpsimd.dma_start(
                                            out=wcol[ci % 2][:, 0:HC * 128],
                                            in_=wqkv_t[c["fc"]].rearrange(
                                                "p hc j -> p (hc j)")), ci)
                                    if ci == DC + 2:
                                        for k in range(DC):
                                            sS.wait(eng, k)
                                        ccK.inc(nc.gpsimd.collective_compute(
                                            "AllGather", ALU.bypass,
                                            replica_groups=rgroups,
                                            ins=[kvK_mine.ap()], outs=[kvK_g.ap()]))
                                    if ci == min(2 * DC + 3, len(cols) - 1):
                                        for k in range(DC, 2 * DC):
                                            sS.wait(eng, k)
                                        ccV.inc(nc.gpsimd.collective_compute(
                                            "AllGather", ALU.bypass,
                                            replica_groups=rgroups,
                                            ins=[kvV_mine.ap()], outs=[kvV_g.ap()]))

                            def _vector(eng):
                                def center(ts_, h0, h1):
                                    for hcb in range(h0, h1):
                                        sl = LNx[:, hcb * S_own + ts_ * TN:
                                                 hcb * S_own + (ts_ + 1) * TN]
                                        dv.inc(nc.vector.tensor_sub(
                                            out=sl, in0=sl,
                                            in1=negmr1b[:, ts_ * TN:(ts_ + 1) * TN]))

                                def mult(p):
                                    ci, ncb = gseq[p]
                                    eng.wait_ge(pe.h, pe0 + p + 1)
                                    dv.inc(nc.vector.tensor_mul(
                                        out=pqk[p % 2][:], in0=pqk[p % 2][:],
                                        in1=r1b[:, ncb * TN:(ncb + 1) * TN]))
                                    multmark[p] = dv.n

                                center(0, 0, HC)
                                APL[0] = dv.n
                                p_start = 0
                                if NCo > 1:
                                    center(1, 0, HC // 2)
                                    mult(0)
                                    center(1, HC // 2, HC)
                                    APL[1] = dv.n
                                    mult(1)
                                    p_start = 2
                                for p in range(p_start, len(gseq)):
                                    mult(p)

                            def _tensor(eng):
                                for p, (ci, ncb) in enumerate(gseq):
                                    wS.wait(eng, ci)
                                    eng.wait_ge(dv.h, APL[ncb])
                                    if p >= 2:
                                        eng.wait_ge(ac.h, ac0 + p - 1)
                                    last = None
                                    for hcb in range(HC):
                                        last = nc.tensor.matmul(
                                            pqk[p % 2][:],
                                            wcol[ci % 2][:, hcb * 128:(hcb + 1) * 128],
                                            LNx[:, hcb * S_own + ncb * TN:
                                                hcb * S_own + (ncb + 1) * TN],
                                            start=(hcb == 0), stop=(hcb == HC - 1))
                                    pe.inc(last)

                            def _scalar(eng):
                                for p, (ci, ncb) in enumerate(gseq):
                                    c = cols[ci]
                                    eng.wait_ge(dv.h, multmark[p])
                                    if c["is_q"]:
                                        dst = Q_fm[:, c["fc"] * S_own + ncb * TN:
                                                   c["fc"] * S_own + (ncb + 1) * TN]
                                    else:
                                        k = c["kv_idx"]
                                        if ncb == 0 and k >= 2:
                                            sS.wait(eng, k - 2)
                                        dst = kvbuf[k % 2][:, ncb * TN:(ncb + 1) * TN]
                                    ac.inc(nc.scalar.activation(
                                        out=dst, in_=pqk[p % 2][:], func=AF.Identity,
                                        bias=bqkv_pp[:, c["fc"]:c["fc"] + 1], scale=1.0))

                            def _sync(eng):
                                for k, c in enumerate(kv_cols):
                                    eng.wait_ge(ac.h, ac0 + lastpos[k] + 1)
                                    if c["fc"] < 2 * DC:
                                        dst = kvK_mine[(c["fc"] - DC) * 128:
                                                       (c["fc"] - DC + 1) * 128, :]
                                    else:
                                        dst = kvV_mine[(c["fc"] - 2 * DC) * 128:
                                                       (c["fc"] - 2 * DC + 1) * 128, :]
                                    sS.issue(eng.dma_start(out=dst, in_=kvbuf[k % 2][:]),
                                             k)
                                eng.wait_ge(ccK.h, ccK.n)
                                for dc in range(DC):
                                    for r in range(2):
                                        kld.inc(eng.dma_start(
                                            out=K_fm[:, dc * S_all + r * S_own:
                                                     dc * S_all + (r + 1) * S_own],
                                            in_=kvK_g[r * DC * 128 + dc * 128:
                                                      r * DC * 128 + (dc + 1) * 128,
                                                      :]), 16)

                            blk.gpsimd(_gpsimd)
                            blk.vector(_vector)
                            blk.tensor(_tensor)
                            blk.scalar(_scalar)
                            blk.sync(_sync)
                        barrier()
            SLn.close()

            with ExitStack() as S3:
                attn_out = S3.enter_context(sb("attn_out", [128, DC * S_own], BF16))
                wcol4 = [S3.enter_context(sb(f"wcol4_{i}", [128, DC * 128], BF16))
                         for i in range(3)]
                with ExitStack() as S3t:
                    V_tm = S3t.enter_context(sb("V_tm", [128, KC * D], BF16))
                    PnT = S3t.enter_context(sb("PnT", [128, KC * S_own], BF16))
                    Pf32 = [S3t.enter_context(sb(f"Pf32_{i}", [128, S_all], F32))
                            for i in range(2)]
                    Pn = [S3t.enter_context(sb(f"Pn{i}", [128, S_all], BF16))
                          for i in range(2)]
                    vstage = [S3t.enter_context(sb(f"vstage{i}", [128, S_all], BF16))
                              for i in range(2)]
                    mx4 = S3t.enter_context(sb("mx4", [128, NCa], F32))
                    negmx = S3t.enter_context(sb("negmx", [128, 1], F32))
                    bias_q = S3t.enter_context(sb("bias_q", [128, 1], F32))
                    sums = S3t.enter_context(sb("sums", [128, NCa], F32))
                    ssum = S3t.enter_context(sb("ssum", [128, 1], F32))
                    rinv = S3t.enter_context(sb("rinv", [128, 1], F32))

                    with ExitStack() as SP:
                        CCV = ccV.n
                        KLOAD = kld.n
                        wS.begin()
                        with nc.Block() as blk:
                            def _gpsimd(eng):
                                for ci in range(2):
                                    wS.issue(nc.gpsimd.dma_start(
                                        out=wcol4[ci % 3][:],
                                        in_=wo_t[ci].rearrange("p dc j -> p (dc j)")), ci)
                            blk.gpsimd(_gpsimd)

                        BSZ = 4 if KC % 4 == 0 else 1
                        NB = KC // BSZ
                        pe0, dv0 = pe.n, dv.n
                        lS.begin()
                        SV = ExitStack()
                        SV.__enter__()
                        ptpV = [SV.enter_context(ps(f"ptpV{i}", [128, 512], BF16))
                                for i in range(4)]
                        with nc.Block() as blk:
                            def _sync(eng):
                                eng.wait_ge(ccV.h, CCV)
                                for dc in range(DC):
                                    if dc >= 2:
                                        eng.wait_ge(pe.h, pe0 + (dc - 1) * NB)
                                    for r in range(2):
                                        lS.issue(eng.dma_start(
                                            out=vstage[dc % 2][:, r * S_own:
                                                       (r + 1) * S_own],
                                            in_=kvV_g[r * DC * 128 + dc * 128:
                                                      r * DC * 128 + (dc + 1) * 128,
                                                      :]), dc * 2 + r)

                            def _tensor(eng):
                                for dc in range(DC):
                                    lS.wait(eng, dc * 2)
                                    lS.wait(eng, dc * 2 + 1)
                                    for b in range(NB):
                                        gi = dc * NB + b
                                        if gi >= 4:
                                            eng.wait_ge(dv.h, dv0 + gi - 3)
                                        last = None
                                        for t in range(BSZ):
                                            kc = b * BSZ + t
                                            last = nc.tensor.transpose(
                                                ptpV[gi % 4][:, t * 128:(t + 1) * 128],
                                                vstage[dc % 2][:, kc * 128:(kc + 1) * 128],
                                                eye_b[:, :])
                                        pe.inc(last)

                            def _vector(eng):
                                for dc in range(DC):
                                    for b in range(NB):
                                        gi = dc * NB + b
                                        eng.wait_ge(pe.h, pe0 + gi + 1)
                                        src = ptpV[gi % 4].ap().rearrange(
                                            "p (t j) -> p t j", t=4)[:, 0:BSZ, :]
                                        dst = V_tm.ap().rearrange(
                                            "p (kc dd) -> p kc dd", kc=KC)[
                                            :, b * BSZ:(b + 1) * BSZ,
                                            dc * 128:(dc + 1) * 128]
                                        dv.inc(nc.vector.tensor_copy(out=dst, in_=src))

                            blk.sync(_sync)
                            blk.tensor(_tensor)
                            blk.vector(_vector)
                        VT_DV = dv.n
                        SV.close()
                        pscore = SP.enter_context(ps("pscore", [128, S_all], F32))
                        ptp = [SP.enter_context(ps(f"ptp{i}", [128, 512], BF16))
                               for i in range(2)]
                        ppv = SP.enter_context(ps("ppv", [128, DC * 128], F32))

                        pe0, dv0, ac0 = pe.n, dv.n, ac.n
                        DVQ = 8 + NB

                        def dv_q(qt):
                            return dv0 + qt * DVQ

                        pm = {}
                        cnt = 0
                        for qt in range(QT):
                            cnt += 1
                            pm[("S", qt)] = pe0 + cnt
                            if qt >= 1:
                                cnt += 1
                                pm[("PV", qt - 1)] = pe0 + cnt
                            for b in range(NB):
                                cnt += 1
                                pm[("T", qt, b)] = pe0 + cnt
                        cnt += 1
                        pm[("PV", QT - 1)] = pe0 + cnt

                        am = {}
                        acnt = 0
                        for qt in range(QT):
                            acnt += 1
                            am[("bias", qt)] = ac0 + acnt
                            acnt += NCa
                            am[("exp", qt)] = ac0 + acnt
                            if qt >= 1:
                                acnt += 1
                                am[("pvev", qt - 1)] = ac0 + acnt
                        acnt += 1
                        am[("pvev", QT - 1)] = ac0 + acnt

                        def emit_pv(eng, qt):
                            if qt >= 1:
                                eng.wait_ge(ac.h, am[("pvev", qt - 1)])
                            if qt == 0:
                                eng.wait_ge(dv.h, VT_DV)
                            last = None
                            for kc in range(KC):
                                eng.wait_ge(dv.h, dv_q(qt) + 8 + (kc // BSZ) + 1)
                                for dc in range(DC):
                                    last = nc.tensor.matmul(
                                        ppv[:, dc * 128:(dc + 1) * 128],
                                        V_tm[:, kc * D + dc * 128:kc * D + (dc + 1) * 128],
                                        PnT[:, kc * S_own + qt * 128:
                                            kc * S_own + (qt + 1) * 128],
                                        start=(kc == 0 and dc % 4 == 0),
                                        stop=(kc == KC - 1 and
                                              (dc % 4 == 3 or dc == DC - 1)))
                            pe.inc(last)

                        with nc.Block() as blk:
                            def _tensor(eng):
                                for qt in range(QT):
                                    eng.wait_ge(kld.h, KLOAD)
                                    if qt > 0:
                                        eng.wait_ge(ac.h, am[("exp", qt - 1)])
                                    last = None
                                    for ncb in range(NCa):
                                        for dc in range(DC):
                                            last = nc.tensor.matmul(
                                                pscore[:, ncb * TN:(ncb + 1) * TN],
                                                Q_fm[:, dc * S_own + qt * 128:
                                                     dc * S_own + (qt + 1) * 128],
                                                K_fm[:, dc * S_all + ncb * TN:
                                                     dc * S_all + (ncb + 1) * TN],
                                                start=(dc == 0), stop=(dc == DC - 1))
                                    pe.inc(last)
                                    if qt >= 1:
                                        emit_pv(eng, qt - 1)
                                    eng.wait_ge(dv.h, dv_q(qt) + 8)
                                    for b in range(NB):
                                        gi = qt * NB + b
                                        if gi >= 2:
                                            pq, pb = (gi - 2) // NB, (gi - 2) % NB
                                            eng.wait_ge(dv.h, dv_q(pq) + 8 + pb + 1)
                                        else:
                                            eng.wait_ge(dv.h, VT_DV)
                                        last = None
                                        for t in range(BSZ):
                                            kc = b * BSZ + t
                                            last = nc.tensor.transpose(
                                                ptp[gi % 2][:, t * 128:(t + 1) * 128],
                                                Pn[qt % 2][:, kc * 128:(kc + 1) * 128],
                                                eye_b[:, :])
                                        pe.inc(last)
                                emit_pv(eng, QT - 1)

                            def _vector(eng):
                                for qt in range(QT):
                                    eng.wait_ge(pe.h, pm[("S", qt)])
                                    if qt >= 2:
                                        eng.wait_ge(pe.h, pm[("T", qt - 2, NB - 1)])
                                    if qt >= 1:
                                        eng.wait_ge(ac.h, am[("bias", qt - 1)])
                                    for ncb in range(NCa):
                                        dv.inc(nc.vector.reduce_max(
                                            out=mx4[:, ncb:ncb + 1],
                                            in_=pscore[:, ncb * TN:(ncb + 1) * TN],
                                            axis=AX.X, negate=True))
                                    eng.wait_ge(dv.h, dv_q(qt) + NCa)
                                    dv.inc(nc.vector.tensor_reduce(
                                        out=negmx[:], in_=mx4[:], axis=AX.X, op=ALU.min))
                                    while dv.n < dv_q(qt) + 5:
                                        dv.inc(nc.vector.engine_nop())
                                    eng.wait_ge(ac.h, am[("exp", qt)])
                                    dv.inc(nc.vector.reduce_sum(out=ssum[:], in_=sums[:],
                                                                axis=AX.X))
                                    eng.wait_ge(dv.h, dv_q(qt) + 6)
                                    dv.inc(nc.vector.reciprocal(out=rinv[:], in_=ssum[:]))
                                    eng.wait_ge(dv.h, dv_q(qt) + 7)
                                    dv.inc(nc.vector.tensor_scalar(
                                        out=Pn[qt % 2][:], in0=Pf32[qt % 2][:],
                                        scalar1=rinv[:], scalar2=None, op0=ALU.mult))
                                    for b in range(NB):
                                        eng.wait_ge(pe.h, pm[("T", qt, b)])
                                        src = ptp[(qt * NB + b) % 2].ap().rearrange(
                                            "p (t j) -> p t j", t=4)[:, 0:BSZ, :]
                                        dst = PnT.ap().rearrange(
                                            "p (kc q) -> p kc q", kc=KC)[
                                            :, b * BSZ:(b + 1) * BSZ,
                                            qt * 128:(qt + 1) * 128]
                                        dv.inc(nc.vector.tensor_copy(out=dst, in_=src))

                            def _scalar(eng):
                                for qt in range(QT):
                                    eng.wait_ge(dv.h, dv_q(qt) + 5)
                                    ac.inc(nc.scalar.mul(out=bias_q[:], in_=negmx[:],
                                                         mul=d.scale))
                                    if qt >= 2:
                                        eng.wait_ge(dv.h, dv_q(qt - 2) + 8)
                                    if qt >= 1:
                                        eng.wait_ge(dv.h, dv_q(qt - 1) + 6)
                                    eng.wait_ge(ac.h, am[("bias", qt)])
                                    for ncb in range(NCa):
                                        ac.inc(nc.scalar.activation(
                                            out=Pf32[qt % 2][:, ncb * TN:(ncb + 1) * TN],
                                            in_=pscore[:, ncb * TN:(ncb + 1) * TN],
                                            func=AF.Exp, bias=bias_q[:], scale=d.scale,
                                            accum_out=sums[:, ncb:ncb + 1]))
                                    if qt >= 1:
                                        eng.wait_ge(pe.h, pm[("PV", qt - 1)])
                                        dst = attn_out.ap().rearrange(
                                            "p (dc s) -> p dc s", dc=DC)[
                                            :, :, (qt - 1) * 128:qt * 128]
                                        ac.inc(nc.scalar.copy(
                                            out=dst,
                                            in_=ppv.ap().rearrange(
                                                "p (dc j) -> p dc j", dc=DC)))
                                eng.wait_ge(pe.h, pm[("PV", QT - 1)])
                                dst = attn_out.ap().rearrange(
                                    "p (dc s) -> p dc s", dc=DC)[
                                    :, :, (QT - 1) * 128:QT * 128]
                                ac.inc(nc.scalar.copy(
                                    out=dst,
                                    in_=ppv.ap().rearrange("p (dc j) -> p dc j", dc=DC)))

                            blk.tensor(_tensor)
                            blk.vector(_vector)
                            blk.scalar(_scalar)
                            assert pe.n == pe0 + cnt, (pe.n, pe0, cnt)
                            assert ac.n == ac0 + acnt, (ac.n, ac0, acnt)
                            assert dv.n == dv0 + QT * DVQ
                        barrier()

                with ExitStack() as S4:
                    xo = [S4.enter_context(sb(f"xo{i}", [128, S_own], F32)) for i in range(6)]
                    x2t = [S4.enter_context(sb(f"x2t{i}", [128, S_own], F32)) for i in range(4)]
                    sq4 = [S4.enter_context(sb(f"sq4_{i}", [128, S_own], BF16)) for i in range(4)]
                    x2b = [S4.enter_context(sb(f"x2b_{i}", [128, S_own], BF16)) for i in range(4)]
                    rowA2 = S4.enter_context(sb("rowA2", [1, S_own], F32))
                    rowB2 = S4.enter_context(sb("rowB2", [1, S_own], F32))
                    rowC2 = S4.enter_context(sb("rowC2", [1, S_own], F32))
                    rowD2 = S4.enter_context(sb("rowD2", [1, S_own], F32))

                    with ExitStack() as SP:
                        po4 = [SP.enter_context(ps(f"po4_{i}", [128, TN], F32))
                               for i in range(3)]
                        st2_s = [SP.enter_context(ps(f"st2s{i}", [1, TN], F32))
                                 for i in range(NCo)]
                        st2_q = [SP.enter_context(ps(f"st2q{i}", [1, TN], F32))
                                 for i in range(NCo)]

                        groups = [(fc, tc) for fc in range(OC) for tc in range(NCo)]
                        NG4 = len(groups)
                        pe0, dv0, ac0 = pe.n, dv.n, ac.n
                        lS.begin()
                        sS.begin()

                        def op_mark(g):
                            return pe0 + (g + 1 if g < 2 else 2 * g - 1)

                        def stats_mark(h):
                            if h < NG4 - 2:
                                return pe0 + 2 * h + 4
                            return pe0 + 2 * NG4 - 2 + (h - (NG4 - 2)) + 1

                        with nc.Block() as blk:
                            def _gpsimd(eng):
                                for fc in range(2, OC):
                                    if fc >= 3:
                                        eng.wait_ge(pe.h,
                                                    op_mark((fc - 3) * NCo + NCo - 1))
                                    wS.issue(nc.gpsimd.dma_start(
                                        out=wcol4[fc % 3][:],
                                        in_=wo_t[fc].rearrange("p dc j -> p (dc j)")), fc)

                            def _sync(eng):
                                for fc in range(OC):
                                    if fc >= 6:
                                        eng.wait_ge(dv.h,
                                                    dv0 + 2 * ((fc - 6) * NCo + NCo - 1) + 1)
                                    lS.issue(eng.dma_start(
                                        out=xo[fc % 6][:],
                                        in_=x_fm[fc * 128:(fc + 1) * 128, :]), fc)

                            def _tensor(eng):
                                for g, (fc, tc) in enumerate(groups):
                                    if tc == 0:
                                        wS.wait(eng, fc)
                                    if g >= 3:
                                        eng.wait_ge(dv.h, dv0 + 2 * (g - 3) + 1)
                                    last = None
                                    for dc in range(DC):
                                        last = nc.tensor.matmul(
                                            po4[g % 3][:],
                                            wcol4[fc % 3][:, dc * 128:(dc + 1) * 128],
                                            attn_out[:, dc * S_own + tc * TN:
                                                     dc * S_own + (tc + 1) * TN],
                                            start=(dc == 0), stop=(dc == DC - 1))
                                    pe.inc(last)
                                    if g >= 2:
                                        h = g - 2
                                        hfc, htc = groups[h]
                                        eng.wait_ge(ac.h, ac0 + h + 1)
                                        eng.wait_ge(dv.h, dv0 + 2 * h + 2)
                                        nc.tensor.matmul(
                                            st2_s[htc][:], ones128b[:],
                                            x2b[hfc % 4][:, htc * TN:(htc + 1) * TN],
                                            start=(hfc == 0), stop=(hfc == OC - 1))
                                        last = nc.tensor.matmul(
                                            st2_q[htc][:], ones128b[:],
                                            sq4[hfc % 4][:, htc * TN:(htc + 1) * TN],
                                            start=(hfc == 0), stop=(hfc == OC - 1))
                                        pe.inc(last)
                                for h in range(NG4 - 2, NG4):
                                    hfc, htc = groups[h]
                                    eng.wait_ge(ac.h, ac0 + h + 1)
                                    eng.wait_ge(dv.h, dv0 + 2 * h + 2)
                                    nc.tensor.matmul(
                                        st2_s[htc][:], ones128b[:],
                                        x2b[hfc % 4][:, htc * TN:(htc + 1) * TN],
                                        start=(hfc == 0), stop=(hfc == OC - 1))
                                    last = nc.tensor.matmul(
                                        st2_q[htc][:], ones128b[:],
                                        sq4[hfc % 4][:, htc * TN:(htc + 1) * TN],
                                        start=(hfc == 0), stop=(hfc == OC - 1))
                                    pe.inc(last)

                            def _vector(eng):
                                for g, (fc, tc) in enumerate(groups):
                                    eng.wait_ge(pe.h, op_mark(g))
                                    if tc == 0:
                                        lS.wait(eng, fc)
                                    if tc == 0 and fc >= 4:
                                        hlast = (fc - 4) * NCo + NCo - 1
                                        eng.wait_ge(ac.h, ac0 + hlast + 1)
                                        sS.wait(eng, fc - 4)
                                        eng.wait_ge(pe.h, stats_mark(hlast))
                                    dv.inc(nc.vector.scalar_tensor_tensor(
                                        out=x2t[fc % 4][:, tc * TN:(tc + 1) * TN],
                                        in0=po4[g % 3][:], scalar=bo_pp[:, fc:fc + 1],
                                        in1=xo[fc % 6][:, tc * TN:(tc + 1) * TN],
                                        op0=ALU.add, op1=ALU.add))
                                    eng.wait_ge(dv.h, dv0 + 2 * g + 1)
                                    dv.inc(nc.vector.tensor_copy(
                                        out=x2b[fc % 4][:, tc * TN:(tc + 1) * TN],
                                        in_=x2t[fc % 4][:, tc * TN:(tc + 1) * TN]))

                            def _scalar(eng):
                                for g, (fc, tc) in enumerate(groups):
                                    eng.wait_ge(dv.h, dv0 + 2 * g + 1)
                                    if tc == 0 and fc >= 4:
                                        eng.wait_ge(pe.h,
                                                    stats_mark((fc - 4) * NCo + NCo - 1))
                                    ac.inc(nc.scalar.activation(
                                        out=sq4[fc % 4][:, tc * TN:(tc + 1) * TN],
                                        in_=x2t[fc % 4][:, tc * TN:(tc + 1) * TN],
                                        func=AF.Square))
                                    if tc == NCo - 1:
                                        sS.issue(nc.scalar.dma_start(
                                            out=x2_spill[fc], in_=x2t[fc % 4][:]), fc)

                            blk.gpsimd(_gpsimd)
                            blk.sync(_sync)
                            blk.tensor(_tensor)
                            blk.vector(_vector)
                            blk.scalar(_scalar)
                        P4_PE = pe.n


                        with nc.Block() as blk:
                            def _vector(eng):
                                eng.wait_ge(pe.h, P4_PE)
                                for t in range(NCo):
                                    dv.inc(nc.vector.tensor_scalar(
                                        out=rowA2[0:1, t * TN:(t + 1) * TN],
                                        in0=st2_s[t][:], scalar1=1.0 / H, scalar2=None,
                                        op0=ALU.mult))
                                    dv.inc(nc.vector.tensor_scalar(
                                        out=rowB2[0:1, t * TN:(t + 1) * TN],
                                        in0=st2_q[t][:], scalar1=1.0 / H, scalar2=None,
                                        op0=ALU.mult))
                            blk.vector(_vector)

                    ln_finalize(rowA2, rowB2, rowC2, rowD2, r2b, negmr2b, S_own)
                    barrier()
        SQ.close()

        with ExitStack() as S6:
            x2n = S6.enter_context(sb("x2n", [128, HC * S_own], BF16))
            wcol8 = [S6.enter_context(sb(f"wcol8_{i}", [128, HC * 128], BF16))
                     for i in range(2)]
            z_sb = S6.enter_context(sb("z_sb", [ER, S_own], F32))
            zs = S6.enter_context(sb("zs", [ER, S_own], BF16))
            a_r = S6.enter_context(sb("a_r", [EA, S_own], F32))
            e_r = S6.enter_context(sb("e_r", [EA, S_own], F32))
            t_r = S6.enter_context(sb("t_r", [EA, S_own], F32))
            alpha = S6.enter_context(sb("alpha", [EA, S_own], F32))
            mrow = S6.enter_context(sb("mrow", [1, S_own], F32))
            rrow = S6.enter_context(sb("rrow", [1, S_own], F32))

            with ExitStack() as S6a:
                CH6 = 2 if HC % 2 == 0 else 1
                x2s = [S6a.enter_context(sb(f"x2s{i}", [128, CH6 * S_own], F32))
                       for i in range(3)]
                with ExitStack() as SPz:
                    paw = SPz.enter_context(ps("paw", [AW, S_own], F32))
                    dv0, pe0 = dv.n, pe.n
                    NCH6 = HC // CH6
                    lS.begin()
                    wS.begin()
                    xS.begin()
                    x2v = x2_spill.ap().rearrange("h p t -> p h t")
                    with nc.Block() as blk:
                        def _gpsimd(eng):
                            for ci in range(2):
                                wS.issue(nc.gpsimd.dma_start(
                                    out=wcol8[ci % 2][:, 0:HC * 128],
                                    in_=wb_t[ci].rearrange("p hc j -> p (hc j)")), ci)
                            for i in range(NCH6):
                                if i % 2 == 1:
                                    if i >= 3:
                                        eng.wait_ge(dv.h, dv0 + CH6 * (i - 3) + CH6)
                                    xS.issue(nc.gpsimd.dma_start(
                                        out=x2s[i % 3][:],
                                        in_=x2v[:, i * CH6:(i + 1) * CH6, :]), i)
                        blk.gpsimd(_gpsimd)

                        def _sync(eng):
                            for i in range(NCH6):
                                if i % 2 == 0:
                                    if i >= 3:
                                        eng.wait_ge(dv.h, dv0 + CH6 * (i - 3) + CH6)
                                    lS.issue(eng.dma_start(
                                        out=x2s[i % 3][:],
                                        in_=x2v[:, i * CH6:(i + 1) * CH6, :]), i)

                        def _vector(eng):
                            for i in range(NCH6):
                                if i % 2 == 0:
                                    lS.wait(eng, i)
                                else:
                                    xS.wait(eng, i)
                                for c in range(CH6):
                                    hcb = i * CH6 + c
                                    dv.inc(nc.vector.tensor_sub(
                                        out=x2n[:, hcb * S_own:(hcb + 1) * S_own],
                                        in0=x2s[i % 3][:, c * S_own:(c + 1) * S_own],
                                        in1=negmr2b[:]))

                        def _tensor(eng):
                            for hcb in range(HC):
                                eng.wait_ge(dv.h, dv0 + hcb + 1)
                                last = None
                                for t in range(NCo):
                                    last = nc.tensor.matmul(
                                        paw[:, t * TN:(t + 1) * TN],
                                        AW_T[:, hcb * AW:(hcb + 1) * AW],
                                        x2n[:, hcb * S_own + t * TN:
                                            hcb * S_own + (t + 1) * TN],
                                        start=(hcb == 0), stop=(hcb == HC - 1))
                                pe.inc(last)

                        blk.sync(_sync)
                        blk.vector(_vector)
                        blk.tensor(_tensor)
                    P6_PE = pe.n

                    with nc.Block() as blk:
                        def _vector(eng):
                            eng.wait_ge(pe.h, P6_PE)
                            dv.inc(nc.vector.tensor_copy(out=z_sb[:],
                                                          in_=paw[0:ER, :]))
                        blk.vector(_vector)
                    ZEV = dv.n

                    with nc.Block() as blk:
                        def _vector(eng):
                            eng.wait_ge(pe.h, P6_PE)
                            dv.inc(nc.vector.tensor_mul(out=paw[ER:AW, :],
                                                        in0=paw[ER:AW, :],
                                                        in1=r2b[0:EA, :]))
                        blk.vector(_vector)
                        LSC = dv.n

                        def _scalar(eng):
                            eng.wait_ge(dv.h, LSC)
                            ac.inc(nc.scalar.activation(out=a_r[:], in_=paw[ER:AW, :],
                                                        func=AF.Identity, bias=bias_r[:],
                                                        scale=1.0))
                        blk.scalar(_scalar)
                    AEV = ac.n

                LSC_DV = dv.n
                with ExitStack() as S8:
                    x2r = [S8.enter_context(sb(f"x2r{i}", [128, S_own], F32))
                           for i in range(2)]
                    out_t = [S8.enter_context(sb(f"out_t{i}", [128, S_own], F32))
                             for i in range(2)]
                    with ExitStack() as SP:
                        pys = [SP.enter_context(ps(f"py{i}", [128, TN], F32))
                               for i in range(4)]
                        pr1 = SP.enter_context(ps("pr1", [1, S_own], F32))
                        pr4 = SP.enter_context(ps("pr4", [ER, S_own], F32))

                        pe0, dvr0, ac0 = pe.n, dv.n, ac.n
                        dv8 = dvr0 + 5
                        NHOC = min(2, OC)
                        NGH = NHOC * NCo
                        lS.begin()
                        sS.begin()
                        lfence = [(x, x.n) for x in l01]
                        sfence = [(x, x.n) for x in s01]

                        NFILL = 9

                        def m_lora(g):
                            if g < NGH:
                                return pe0 + NFILL + g + 1
                            return pe0 + NFILL + NGH + (g - NGH) + 1

                        def emit_w_part(eng, oc, t, tile):
                            last = None
                            for hcb in range(HC):
                                last = nc.tensor.matmul(
                                    tile[:],
                                    wcol8[oc % 2][:, hcb * 128:(hcb + 1) * 128],
                                    x2n[:, hcb * S_own + t * TN:
                                        hcb * S_own + (t + 1) * TN],
                                    start=(hcb == 0), stop=False)
                            return last

                        with nc.Block() as blk:
                            def _tensor(eng):
                                eng.wait_ge(dv.h, LSC_DV)
                                eng.wait_ge(ac.h, AEV)
                                heads = [(oc, t) for oc in range(NHOC)
                                         for t in range(NCo)]
                                hi = 0

                                def fill(eng):
                                    nonlocal hi
                                    if hi < len(heads):
                                        oc, t = heads[hi]
                                        if t == 0:
                                            wS.wait(eng, oc)
                                        pe.inc(emit_w_part(eng, oc, t, pys[hi]))
                                    else:
                                        pe.inc(nc.tensor.matmul(
                                            pys[3][0:1, 0:1], ones128b[0:1, 0:1],
                                            ones128b[0:1, 0:1],
                                            start=True, stop=True))
                                    hi += 1

                                fill(eng)
                                for t in range(NCo):
                                    last = nc.tensor.matmul(
                                        pr1[0:1, t * TN:(t + 1) * TN], quart4[:],
                                        a_r[:, t * TN:(t + 1) * TN],
                                        start=True, stop=True)
                                pe.inc(last)
                                fill(eng)
                                eng.wait_ge(dv.h, dvr0 + 1)
                                for t in range(NCo):
                                    last = nc.tensor.matmul(
                                        pr4[0:EA, t * TN:(t + 1) * TN], ones1x4[:],
                                        mrow[0:1, t * TN:(t + 1) * TN],
                                        start=True, stop=True)
                                pe.inc(last)
                                fill(eng)
                                eng.wait_ge(ac.h, ac0 + 1)
                                for t in range(NCo):
                                    last = nc.tensor.matmul(
                                        pr1[0:1, t * TN:(t + 1) * TN], ones4[:],
                                        e_r[:, t * TN:(t + 1) * TN],
                                        start=True, stop=True)
                                pe.inc(last)
                                fill(eng)
                                eng.wait_ge(dv.h, dvr0 + 3)
                                for t in range(NCo):
                                    last = nc.tensor.matmul(
                                        pr4[0:EA, t * TN:(t + 1) * TN], ones1x4[:],
                                        rrow[0:1, t * TN:(t + 1) * TN],
                                        start=True, stop=True)
                                pe.inc(last)
                                eng.wait_ge(dv.h, dvr0 + 4)
                                for t in range(NCo):
                                    last = nc.tensor.matmul(
                                        pr4[0:ER, t * TN:(t + 1) * TN], r2m_s[:],
                                        alpha[:, t * TN:(t + 1) * TN],
                                        start=True, stop=True)
                                pe.inc(last)
                                eng.wait_ge(dv.h, dvr0 + 5)
                                for g in range(NGH):
                                    oc, t = heads[g]
                                    pe.inc(nc.tensor.matmul(
                                        pys[g][:], Bcat[:, oc * 128:(oc + 1) * 128],
                                        zs[:, t * TN:(t + 1) * TN],
                                        start=False, stop=True))
                                for oc in range(NHOC, OC):
                                    wS.wait(eng, oc)
                                    for t in range(NCo):
                                        g = oc * NCo + t
                                        if g >= 4:
                                            eng.wait_ge(dv.h, dv8 + 2 * (g - 4) + 2)
                                        for hcb in range(HC):
                                            nc.tensor.matmul(
                                                pys[g % 4][:],
                                                wcol8[oc % 2][:, hcb * 128:(hcb + 1) * 128],
                                                x2n[:, hcb * S_own + t * TN:
                                                    hcb * S_own + (t + 1) * TN],
                                                start=(hcb == 0), stop=False)
                                        last = nc.tensor.matmul(
                                            pys[g % 4][:], Bcat[:, oc * 128:(oc + 1) * 128],
                                            zs[:, t * TN:(t + 1) * TN],
                                            start=False, stop=True)
                                        pe.inc(last)

                            def _vector(eng):
                                eng.wait_ge(pe.h, pe0 + 2)
                                dv.inc(nc.vector.tensor_copy(out=mrow[:],
                                                             in_=pr1[0:1, :]))
                                eng.wait_ge(pe.h, pe0 + 4)
                                dv.inc(nc.vector.tensor_sub(out=t_r[:], in0=a_r[:],
                                                            in1=pr4[0:EA, :]))
                                eng.wait_ge(pe.h, pe0 + 6)
                                eng.wait_ge(dv.h, dvr0 + 2)
                                dv.inc(nc.vector.reciprocal(out=rrow[:],
                                                            in_=pr1[0:1, :]))
                                eng.wait_ge(pe.h, pe0 + 8)
                                dv.inc(nc.vector.tensor_mul(out=alpha[:], in0=e_r[:],
                                                            in1=pr4[0:EA, :]))
                                eng.wait_ge(pe.h, pe0 + 9)
                                eng.wait_ge(dv.h, ZEV)
                                dv.inc(nc.vector.tensor_mul(out=zs[:], in0=z_sb[:],
                                                            in1=pr4[0:ER, :]))
                                for oc in range(OC):
                                    for t in range(NCo):
                                        g = oc * NCo + t
                                        eng.wait_ge(pe.h, m_lora(g))
                                        if t == 0:
                                            lS.wait(eng, oc)
                                        if oc >= 2 and t == 0:
                                            sS.wait(eng, oc - 2)
                                        dv.inc(nc.vector.tensor_mul(
                                            out=pys[g % 4][:], in0=pys[g % 4][:],
                                            in1=r2b[:, t * TN:(t + 1) * TN]))
                                        eng.wait_ge(dv.h, dv8 + 2 * g + 1)
                                        dv.inc(nc.vector.scalar_tensor_tensor(
                                            out=out_t[oc % 2][:, t * TN:(t + 1) * TN],
                                            in0=pys[g % 4][:],
                                            scalar=bb_pp[:, oc:oc + 1],
                                            in1=x2r[oc % 2][:, t * TN:(t + 1) * TN],
                                            op0=ALU.add, op1=ALU.add))

                            def _scalar(eng):
                                eng.wait_ge(dv.h, dvr0 + 2)
                                ac.inc(nc.scalar.activation(out=e_r[:], in_=t_r[:],
                                                            func=AF.Exp))

                            def _sync(eng):
                                for (x, n) in lfence:
                                    eng.wait_ge(x.h, n)
                                for (x, n) in sfence:
                                    eng.wait_ge(x.h, n)
                                for oc in range(OC):
                                    if oc >= 2:
                                        eng.wait_ge(dv.h,
                                                    dv8 + 2 * ((oc - 2) * NCo + NCo))
                                    lS.issue(eng.dma_start(out=x2r[oc % 2][:],
                                                           in_=x2_spill[oc]), oc)
                                    eng.wait_ge(dv.h, dv8 + 2 * (oc * NCo + NCo))
                                    sS.issue(eng.dma_start(
                                        out=out[oc * 128:(oc + 1) * 128, :],
                                        in_=out_t[oc % 2][:]), oc)

                            def _gpsimd(eng):
                                for oc in range(2, OC):
                                    if oc == 2:
                                        eng.wait_ge(pe.h, pe0 + 3)
                                    elif oc == 3:
                                        eng.wait_ge(pe.h, pe0 + 7)
                                    else:
                                        eng.wait_ge(pe.h,
                                                    m_lora((oc - 2) * NCo + NCo - 1))
                                    wS.issue(nc.gpsimd.dma_start(
                                        out=wcol8[oc % 2][:, 0:HC * 128],
                                        in_=wb_t[oc].rearrange("p hc j -> p (hc j)")), oc)

                            blk.tensor(_tensor)
                            blk.vector(_vector)
                            blk.scalar(_scalar)
                            blk.sync(_sync)
                            blk.gpsimd(_gpsimd)

                    barrier()

    return nc


def prep_inputs(d: Dims, ins):
    x = np.asarray(ins["L0_attn_x"])
    Wqkv = np.asarray(ins["L0_attn_Wqkv"])
    bqkv = np.ascontiguousarray(np.asarray(ins["L0_attn_bqkv"]))
    Wo = np.asarray(ins["L0_attn_Wo"])
    bo = np.ascontiguousarray(np.asarray(ins["L0_attn_bo"]))
    Wb = np.asarray(ins["L0_ffn_W_base"])
    bb = np.ascontiguousarray(np.asarray(ins["L0_ffn_b_base"]))
    loraA = np.asarray(ins["L0_ffn_lora_A"])
    loraB = np.asarray(ins["L0_ffn_lora_B"])
    routerW = np.ascontiguousarray(np.asarray(ins["L0_ffn_router_W"]))
    routerb = np.asarray(ins["L0_ffn_router_b"])
    prior = np.asarray(ins["L0_ffn_domain_prior"])
    active = np.ascontiguousarray(np.asarray(ins["L0_ffn_active"]).astype(np.int32))

    H, S_own, S_all = d.H, d.S_own, d.S_all
    HC, DC, FC, OC = d.HC, d.DC, d.FC, d.OC

    wqkv_t = np.ascontiguousarray(Wqkv.reshape(HC, 128, FC, 128).transpose(2, 1, 0, 3))
    wo_t = np.ascontiguousarray(Wo.reshape(DC, 128, OC, 128).transpose(2, 1, 0, 3))
    wb_t = np.ascontiguousarray(Wb.reshape(OC, 128, HC, 128).transpose(0, 3, 2, 1))
    loraA_f = np.ascontiguousarray(loraA.reshape(d.E_tot * d.R, H))
    loraBrt_f = np.ascontiguousarray(loraB.transpose(0, 2, 1).reshape(d.E_tot * d.R, H))
    routerb_c = np.ascontiguousarray(routerb.reshape(d.E_tot, 1))
    prior_c = np.ascontiguousarray(prior.reshape(d.E_tot, 1))

    shared = {
        "wqkv_t": wqkv_t, "wo_t": wo_t, "wb_t": wb_t,
        "bqkv": bqkv, "bo": bo, "bb": bb,
        "loraA": loraA_f, "loraBrt": loraBrt_f,
        "routerW": routerW, "routerb": routerb_c, "prior": prior_c,
        "active": active,
    }
    in_maps = []
    halves = S_all // S_own
    n_cores = x.shape[0] * halves
    for core in range(n_cores):
        b = core // halves
        h = core % halves
        m = dict(shared)
        m["x_fm"] = np.ascontiguousarray(x[b][h * S_own:(h + 1) * S_own].T)
        in_maps.append(m)
    return in_maps


def assemble_output(d: Dims, results):
    halves = d.S_all // d.S_own
    B = len(results) // halves
    out = np.empty((B, d.S_all, d.H), dtype=np.float32)
    for core in range(len(results)):
        b = core // halves
        h = core % halves
        out[b, h * d.S_own:(h + 1) * d.S_own, :] = results[core]["out"].T
    return out


def kernel(**inputs):
    d = Dims()
    nc = build_program(d)
    in_maps = prep_inputs(d, inputs)
    res = run_bass_kernel_spmd(nc, in_maps, core_ids=list(range(N_CORES)))
    return assemble_output(d, res.results)


# version 29
# speedup vs baseline: 1.0032x; 1.0032x over previous
import numpy as np
import ml_dtypes
from contextlib import ExitStack

import concourse.bass as bass
import concourse.mybir as mybir
from concourse.bass_utils import run_bass_kernel_spmd

F32 = mybir.dt.float32
BF16 = mybir.dt.bfloat16
I32 = mybir.dt.int32
AF = mybir.ActivationFunctionType
ALU = mybir.AluOpType
AX = mybir.AxisListType

EPS_LN = 1e-5
EPS_PRIOR = 1e-12
SCALING = 2.0
N_CORES = 8


class Dims:
    def __init__(self, H=4096, D=1024, S_own=1024, S_all=2048, E_tot=16, E_act=4, R=8):
        self.H, self.D, self.S_own, self.S_all = H, D, S_own, S_all
        self.E_tot, self.E_act, self.R = E_tot, E_act, R
        self.ER = E_act * R
        self.HC = H // 128
        self.DC = D // 128
        self.FC = 3 * self.DC
        self.OC = self.HC
        self.TN = min(512, S_own)
        self.NCo = S_own // self.TN
        self.NCa = S_all // self.TN
        self.W1 = min(1024, S_all)
        self.N1 = S_all // self.W1
        self.QT = S_own // 128
        self.KC = S_all // 128
        self.scale = 1.0 / float(np.sqrt(float(D)))


class Sem:
    def __init__(self, nc, st, name):
        self.h = st.enter_context(nc.semaphore(name))
        self.n = 0

    def inc(self, inst, k=1):
        inst.then_inc(self.h, k)
        self.n += k
        return self.n


class DStream:

    def __init__(self, sems):
        self.sems = sems
        self.base = [s.n for s in sems]

    def begin(self):
        self.base = [s.n for s in self.sems]

    def issue(self, inst, i):
        s = self.sems[i % len(self.sems)]
        inst.then_inc(s.h, 16)
        s.n += 16

    def wait(self, eng, i):
        nb = len(self.sems)
        s = self.sems[i % nb]
        eng.wait_ge(s.h, self.base[i % nb] + 16 * (i // nb + 1))


def build_program(d: Dims):
    nc = bass.Bass("TRN2", target_bir_lowering=False)
    H, D, S_own, S_all = d.H, d.D, d.S_own, d.S_all
    HC, DC, FC, OC = d.HC, d.DC, d.FC, d.OC
    TN, NCo, NCa = d.TN, d.NCo, d.NCa
    QT, KC, ER = d.QT, d.KC, d.ER
    EA = d.E_act

    x_fm = nc.dram_tensor("x_fm", [H, S_own], F32, kind="ExternalInput")
    wqkv_t = nc.dram_tensor("wqkv_t", [FC, 128, HC, 128], F32, kind="ExternalInput")
    wo_t = nc.dram_tensor("wo_t", [OC, 128, DC, 128], F32, kind="ExternalInput")
    wb_t = nc.dram_tensor("wb_t", [OC, 128, HC, 128], F32, kind="ExternalInput")
    bqkv = nc.dram_tensor("bqkv", [3 * D], F32, kind="ExternalInput")
    bo = nc.dram_tensor("bo", [H], F32, kind="ExternalInput")
    bb = nc.dram_tensor("bb", [H], F32, kind="ExternalInput")
    loraA = nc.dram_tensor("loraA", [d.E_tot * d.R, H], F32, kind="ExternalInput")
    loraBrt = nc.dram_tensor("loraBrt", [d.E_tot * d.R, H], F32, kind="ExternalInput")
    routerW = nc.dram_tensor("routerW", [d.E_tot, H], F32, kind="ExternalInput")
    routerb = nc.dram_tensor("routerb", [d.E_tot, 1], F32, kind="ExternalInput")
    prior = nc.dram_tensor("prior", [d.E_tot, 1], F32, kind="ExternalInput")
    active = nc.dram_tensor("active", [EA], I32, kind="ExternalInput")
    out = nc.dram_tensor("out", [H, S_own], F32, kind="ExternalOutput")

    kvK_mine = nc.dram_tensor("kvK_mine", [DC * 128, S_own], BF16, kind="Internal")
    kvV_mine = nc.dram_tensor("kvV_mine", [DC * 128, S_own], BF16, kind="Internal")
    kvK_g = nc.dram_tensor("kvK_g", [2 * DC * 128, S_own], BF16, kind="Internal")
    kvV_g = nc.dram_tensor("kvV_g", [2 * DC * 128, S_own], BF16, kind="Internal")
    x2_spill = nc.dram_tensor("x2_spill", [HC, 128, S_own], F32, kind="Internal")

    eye_bf16_c = nc.inline_tensor(np.asarray(np.eye(128, dtype=ml_dtypes.bfloat16)), name="c_eye_b")
    eye_f32_c = nc.inline_tensor(np.eye(128, dtype=np.float32), name="c_eye_f")
    ones128_b_c = nc.inline_tensor(np.ones((128, 1), dtype=ml_dtypes.bfloat16), name="c_ones128b")
    ones_sq_c = nc.inline_tensor(np.ones((128, 128), dtype=ml_dtypes.bfloat16), name="c_ones_sq")
    ones_1x128_c = nc.inline_tensor(np.ones((1, 128), dtype=np.float32), name="c_ones1x128")
    quarter4_c = nc.inline_tensor(np.full((EA, 1), 1.0 / EA, dtype=np.float32), name="c_quart")
    ones4_c = nc.inline_tensor(np.ones((EA, 1), dtype=np.float32), name="c_ones4")
    ones_1x4_c = nc.inline_tensor(np.ones((1, EA), dtype=np.float32), name="c_ones1x4")
    rrep8 = np.zeros((EA, ER), dtype=np.float32)
    r2m = np.zeros((EA, ER), dtype=np.float32)
    for e in range(EA):
        rrep8[e, e * d.R:(e + 1) * d.R] = float(d.R)
        r2m[e, e * d.R:(e + 1) * d.R] = SCALING
    rrep8_c = nc.inline_tensor(rrep8, name="c_rrep8")
    r2m_c = nc.inline_tensor(r2m, name="c_r2m")
    modR_c = nc.inline_tensor((np.arange(ER, dtype=np.float32) % d.R).reshape(ER, 1), name="c_modR")
    cpool_c = nc.inline_tensor(np.tile(np.array([[EPS_LN, EPS_PRIOR]], dtype=np.float32), (128, 1)), name="c_cpool")

    G = ExitStack()
    with G:
        qd = Sem(nc, G, "qd")
        qg = Sem(nc, G, "qg")
        kld = Sem(nc, G, "kld")
        ccK = Sem(nc, G, "ccK")
        ccV = Sem(nc, G, "ccV")
        w01 = [Sem(nc, G, f"w{i}") for i in range(3)]
        l01 = [Sem(nc, G, f"l{i}") for i in range(6)]
        s01 = [Sem(nc, G, f"s{i}") for i in range(4)]
        x01 = [Sem(nc, G, f"x{i}") for i in range(8)]
        wS = DStream(w01)
        lS = DStream(l01)
        sS = DStream(s01)
        xS = DStream(x01)
        pe = Sem(nc, G, "pe")
        dv = Sem(nc, G, "dv")
        ac = Sem(nc, G, "ac")
        pp = Sem(nc, G, "pp")
        bar = Sem(nc, G, "bar")

        def barrier():
            with nc.Block() as blk:
                tgt = bar.n + 5

                def mk(fences):
                    def f(eng):
                        for (s, n) in fences:
                            eng.wait_ge(s.h, n)
                        eng.sem_inc(bar.h, 1)
                        eng.wait_ge(bar.h, tgt)
                    return f
                blk.sync(mk([(qd, qd.n), (kld, kld.n)] +
                            [(x, x.n) for x in l01] + [(x, x.n) for x in s01]))
                blk.tensor(mk([(pe, pe.n)]))
                blk.vector(mk([(dv, dv.n)]))
                blk.scalar(mk([(ac, ac.n)]))
                blk.gpsimd(mk([(qg, qg.n), (pp, pp.n)] +
                               [(x, x.n) for x in w01]))
            bar.n += 5

        sb = lambda *a, **k: nc.sbuf_tensor(*a, **k)
        ps = lambda *a, **k: nc.psum_tensor(*a, **k)

        AW = ER + EA
        AW_T = G.enter_context(sb("AW_T", [128, HC * AW], BF16))
        Bcat = G.enter_context(sb("Bcat", [ER, H], BF16))
        bqkv_pp = G.enter_context(sb("bqkv_pp", [128, FC], F32))
        bo_pp = G.enter_context(sb("bo_pp", [128, OC], F32))
        bb_pp = G.enter_context(sb("bb_pp", [128, OC], F32))
        eye_b = G.enter_context(sb("eye_b", [128, 128], BF16))
        eye_f = G.enter_context(sb("eye_f", [128, 128], F32))
        ones128b = G.enter_context(sb("ones128b", [128, 1], BF16))
        ones_sq = G.enter_context(sb("ones_sq", [128, 128], BF16))
        ones1x128 = G.enter_context(sb("ones1x128", [1, 128], F32))
        quart4 = G.enter_context(sb("quart4", [EA, 1], F32))
        ones4 = G.enter_context(sb("ones4", [EA, 1], F32))
        ones1x4 = G.enter_context(sb("ones1x4", [1, EA], F32))
        rrep8_s = G.enter_context(sb("rrep8_s", [EA, ER], F32))
        r2m_s = G.enter_context(sb("r2m_s", [EA, ER], F32))
        modR_s = G.enter_context(sb("modR_s", [ER, 1], F32))
        act_i = G.enter_context(sb("act_i", [EA, 1], I32))
        actf = G.enter_context(sb("actf", [EA, 1], F32))
        idxf = G.enter_context(sb("idxf", [ER, 1], F32))
        idx_i = G.enter_context(sb("idx_i", [ER, 1], I32))
        rb_g = G.enter_context(sb("rb_g", [EA, 1], F32))
        prior_g = G.enter_context(sb("prior_g", [EA, 1], F32))
        bias_r = G.enter_context(sb("bias_r", [EA, 1], F32))
        cpool = G.enter_context(sb("cpool", [128, 2], F32))
        nc.const_aps.register(F32, EPS_LN, cpool[:, 0:1])
        nc.const_aps.register(F32, EPS_PRIOR, cpool[:, 1:2])
        r2b = G.enter_context(sb("r2b", [128, S_own], F32))
        negmr2b = G.enter_context(sb("negmr2b", [128, S_own], F32))

        SQ = ExitStack()
        SQ.__enter__()
        Q_fm = SQ.enter_context(sb("Q_fm", [128, DC * S_own], BF16, side="right"))
        wcol = [SQ.enter_context(sb(f"wcol{i}", [128, HC * 128], BF16, side="right"))
                for i in range(2)]
        W1L = min(1024, S_own)
        if HC % 4 == 0 and S_own == W1L:
            CH1 = 4
        elif HC % 2 == 0 and S_own == W1L:
            CH1 = 2
        else:
            CH1 = 1
        wS.begin()
        xS.begin()
        col_order = list(range(DC, 3 * DC)) + list(range(DC))
        xv = x_fm.ap().rearrange("(c p) t -> p c t", p=128)
        K_fm = SQ.enter_context(sb("K_fm", [128, DC * S_all], BF16))
        SLn = ExitStack()
        SLn.__enter__()
        LNx = SLn.enter_context(sb("LNx", [128, HC * S_own], BF16))

        with ExitStack() as S0:
            Acat = S0.enter_context(sb("Acat", [ER, H], F32))
            Bcat_f = S0.enter_context(sb("Bcat_f", [ER, H], F32))
            Wr_g = S0.enter_context(sb("Wr_g", [EA, H], F32))
            brow_q = S0.enter_context(sb("brow_q", [FC, 128], F32))
            brow_o = S0.enter_context(sb("brow_o", [OC, 128], F32))
            brow_b = S0.enter_context(sb("brow_b", [OC, 128], F32))
            with ExitStack() as SP:
                p0x = [SP.enter_context(ps(f"p0x{i}", [128, 64], F32)) for i in range(4)]
                psi = SP.enter_context(ps("psi", [ER, 1], F32))

                with nc.Block() as blk:
                    qd0 = qd.n

                    def _sync(eng):
                        for dst, src in [
                            (eye_b[:], eye_bf16_c.ap()), (eye_f[:], eye_f32_c.ap()),
                            (ones128b[:], ones128_b_c.ap()), (ones_sq[:], ones_sq_c.ap()),
                            (ones1x128[:], ones_1x128_c.ap()),
                            (quart4[:], quarter4_c.ap()), (ones4[:], ones4_c.ap()),
                            (ones1x4[:], ones_1x4_c.ap()), (rrep8_s[:], rrep8_c.ap()),
                            (r2m_s[:], r2m_c.ap()), (modR_s[:], modR_c.ap()),
                            (act_i[:], active.rearrange("(p o) -> p o", o=1)),
                            (brow_q[:], bqkv.rearrange("(p j) -> p j", j=128)),
                            (brow_o[:], bo.rearrange("(p j) -> p j", j=128)),
                            (brow_b[:], bb.rearrange("(p j) -> p j", j=128)),
                            (cpool[:], cpool_c.ap()),
                        ]:
                            eng.dma_start(out=dst, in_=src).then_inc(qd.h, 16)
                    blk.sync(_sync)
                    qd.n = qd0 + 16 * 16
                    LOADS0 = qd.n

                    dv0 = dv.n
                    pe0 = pe.n

                    def _vector(eng):
                        eng.wait_ge(qd.h, LOADS0)
                        dv.inc(nc.vector.tensor_copy(out=actf[:], in_=act_i[:]))
                        eng.wait_ge(pe.h, pe0 + 1)
                        dv.inc(nc.vector.tensor_scalar(out=idxf[:], in0=psi[:],
                                                       scalar1=modR_s[:], scalar2=None,
                                                       op0=ALU.add))
                        eng.wait_ge(dv.h, dv0 + 2)
                        dv.inc(nc.vector.tensor_copy(out=idx_i[:], in_=idxf[:]))
                    blk.vector(_vector)
                    ACTF, IDXI = dv0 + 1, dv0 + 3

                    def _tensor(eng):
                        eng.wait_ge(dv.h, ACTF)
                        pe.inc(nc.tensor.matmul(psi[:], rrep8_s[:], actf[:],
                                                start=True, stop=True))
                    blk.tensor(_tensor)

                    def _gpsimd(eng):
                        for i in range(min(4, HC // CH1)):
                            xS.issue(nc.gpsimd.dma_start(
                                out=LNx[:, i * CH1 * S_own:(i + 1) * CH1 * S_own],
                                in_=xv[:, i * CH1:(i + 1) * CH1, :]), i)
                        eng.wait_ge(dv.h, IDXI)
                        qg.inc(nc.gpsimd.indirect_dma_start(
                            out=Acat[:], out_offset=None, in_=loraA.ap(),
                            in_offset=bass.IndirectOffsetOnAxis(ap=idx_i[:], axis=0)), 16)
                        qg.inc(nc.gpsimd.indirect_dma_start(
                            out=Bcat_f[:], out_offset=None, in_=loraBrt.ap(),
                            in_offset=bass.IndirectOffsetOnAxis(ap=idx_i[:], axis=0)), 16)
                        eng.wait_ge(qd.h, LOADS0)
                        qg.inc(nc.gpsimd.indirect_dma_start(
                            out=Wr_g[:], out_offset=None, in_=routerW.ap(),
                            in_offset=bass.IndirectOffsetOnAxis(ap=act_i[:], axis=0)), 16)
                        qg.inc(nc.gpsimd.indirect_dma_start(
                            out=rb_g[:], out_offset=None, in_=routerb.ap(),
                            in_offset=bass.IndirectOffsetOnAxis(ap=act_i[:], axis=0)), 16)
                        qg.inc(nc.gpsimd.indirect_dma_start(
                            out=prior_g[:], out_offset=None, in_=prior.ap(),
                            in_offset=bass.IndirectOffsetOnAxis(ap=act_i[:], axis=0)), 16)
                        for i in range(4, HC // CH1):
                            xS.issue(nc.gpsimd.dma_start(
                                out=LNx[:, i * CH1 * S_own:(i + 1) * CH1 * S_own],
                                in_=xv[:, i * CH1:(i + 1) * CH1, :]), i)
                    blk.gpsimd(_gpsimd)
                    GATH = qg.n

                jobs = [(brow_q[:, :], FC, bqkv_pp[:, :], F32),
                        (brow_o[:, :], OC, bo_pp[:, :], F32),
                        (brow_b[:, :], OC, bb_pp[:, :], F32)]
                for hcb in range(HC):
                    jobs.append((Acat[:, hcb * 128:(hcb + 1) * 128], ER,
                                 AW_T[:, hcb * AW:hcb * AW + ER], BF16))
                for hcb in range(HC):
                    jobs.append((Wr_g[:, hcb * 128:(hcb + 1) * 128], EA,
                                 AW_T[:, hcb * AW + ER:(hcb + 1) * AW], BF16))

                with nc.Block() as blk:
                    pe0 = pe.n
                    dv0 = dv.n
                    ac0 = ac.n

                    def _tensor(eng):
                        eng.wait_ge(qd.h, LOADS0)
                        for j, (src, npart, dst, dt_) in enumerate(jobs):
                            if j == 3:
                                eng.wait_ge(qg.h, GATH)
                            dstp = p0x[j % 4]
                            if j >= 4:
                                eng.wait_ge(dv.h, dv0 + j - 3)
                            pe.inc(nc.tensor.transpose(dstp[:, 0:npart], src,
                                                       eye_f[0:npart, 0:npart]))
                    blk.tensor(_tensor)

                    def _scalar(eng):
                        eng.wait_ge(qd.h, LOADS0)
                        eng.wait_ge(qg.h, GATH)
                        ac.inc(nc.scalar.activation(out=prior_g[:], in_=prior_g[:],
                                                    func=AF.Ln, bias=EPS_PRIOR, scale=1.0))
                    blk.scalar(_scalar)
                    LN_AC = ac.n

                    def _vector(eng):
                        for j, (src, npart, dst, dt_) in enumerate(jobs):
                            dstp = p0x[j % 4]
                            eng.wait_ge(pe.h, pe0 + j + 1)
                            dv.inc(nc.vector.tensor_copy(out=dst, in_=dstp[:, 0:npart]))
                        eng.wait_ge(qg.h, GATH)
                        dv.inc(nc.vector.tensor_copy(out=Bcat[:], in_=Bcat_f[:]))
                        eng.wait_ge(ac.h, LN_AC)
                        dv.inc(nc.vector.tensor_add(out=bias_r[:], in0=rb_g[:], in1=prior_g[:]))
                    blk.vector(_vector)

            barrier()

        if True:
            with ExitStack() as SL:
                r1b = SL.enter_context(sb("r1b", [128, S_own], F32))
                with ExitStack() as SR:
                    negmr1b = SL.enter_context(sb("negmr1b", [128, S_own], F32))

                    N1L = S_own // W1L
                    with ExitStack() as S1:
                        xsq = [S1.enter_context(sb(f"xsq{i}", [128, CH1 * W1L], BF16,
                                                   side="right"))
                               for i in range(2)]
                        rowA = S1.enter_context(sb("rowA", [1, S_own], F32))
                        rowB = S1.enter_context(sb("rowB", [1, S_own], F32))
                        rowC = S1.enter_context(sb("rowC", [1, S_own], F32))
                        rowD = S1.enter_context(sb("rowD", [1, S_own], F32))

                        NSTAT = S_own // TN
                        with ExitStack() as SP:
                            st_sum = [SP.enter_context(ps(f"st_s{i}", [1, TN], F32))
                                      for i in range(NSTAT)]
                            st_sq = [SP.enter_context(ps(f"st_q{i}", [1, TN], F32))
                                     for i in range(NSTAT)]
                            chunks = list(range(HC // CH1))
                            NCHU = len(chunks)
                            TPC = (CH1 * W1L) // TN
                            ac0, pe0 = ac.n, pe.n

                            with nc.Block() as blk:
                                def _gpsimd(eng):
                                    for ci in range(2):
                                        wS.issue(nc.gpsimd.dma_start(
                                            out=wcol[ci % 2][:, 0:HC * 128],
                                            in_=wqkv_t[col_order[ci]].rearrange(
                                                "p hc j -> p (hc j)")), ci)
                                blk.gpsimd(_gpsimd)

                                def _scalar(eng):
                                    for i in chunks:
                                        xS.wait(eng, i)
                                        if i >= 2:
                                            eng.wait_ge(pe.h, pe0 + (i - 2) + 1)
                                        sl = LNx[:, i * CH1 * S_own:(i + 1) * CH1 * S_own]
                                        ac.inc(nc.scalar.activation(out=xsq[i % 2][:],
                                                                    in_=sl, func=AF.Square))

                                def _tensor(eng):
                                    for i in chunks:
                                        xS.wait(eng, i)
                                        eng.wait_ge(ac.h, ac0 + i + 1)
                                        sl = LNx[:, i * CH1 * S_own:(i + 1) * CH1 * S_own]
                                        last = None
                                        for t in range(TPC):
                                            nidx = t % NSTAT
                                            nc.tensor.matmul(
                                                st_sum[nidx][:], ones128b[:],
                                                sl[:, t * TN:(t + 1) * TN],
                                                start=(i == 0 and t < NSTAT),
                                                stop=(i == NCHU - 1 and t >= TPC - NSTAT))
                                            last = nc.tensor.matmul(
                                                st_sq[nidx][:], ones128b[:],
                                                xsq[i % 2][:, t * TN:(t + 1) * TN],
                                                start=(i == 0 and t < NSTAT),
                                                stop=(i == NCHU - 1 and t >= TPC - NSTAT))
                                        pe.inc(last)

                                blk.scalar(_scalar)
                                blk.tensor(_tensor)
                            PE_STATS = pe.n

                            with nc.Block() as blk:
                                def _vector(eng):
                                    eng.wait_ge(pe.h, PE_STATS)
                                    for nidx in range(NSTAT):
                                        dv.inc(nc.vector.tensor_scalar(
                                            out=rowA[0:1, nidx * TN:(nidx + 1) * TN],
                                            in0=st_sum[nidx][:], scalar1=1.0 / H,
                                            scalar2=None, op0=ALU.mult))
                                        dv.inc(nc.vector.tensor_scalar(
                                            out=rowB[0:1, nidx * TN:(nidx + 1) * TN],
                                            in0=st_sq[nidx][:], scalar1=1.0 / H,
                                            scalar2=None, op0=ALU.mult))
                                blk.vector(_vector)
                            ROWS_DONE = dv.n

                        def ln_finalize(rowA_, rowB_, rowC_, rowD_, dstR, dstM, width):
                            nslice = width // TN
                            with ExitStack() as SPb:
                                pbcR = [SPb.enter_context(ps(f"pbR_{id(rowA_)}_{i}",
                                                             [128, TN], F32))
                                        for i in range(2)]
                                pbcM = [SPb.enter_context(ps(f"pbM_{id(rowA_)}_{i}",
                                                             [128, TN], F32))
                                        for i in range(2)]
                                ac0, dv0, pe0 = ac.n, dv.n, pe.n
                                with nc.Block() as blk:
                                    def _scalar(eng):
                                        ac.inc(nc.scalar.activation(
                                            out=rowC_[0:1, 0:width], in_=rowA_[0:1, 0:width],
                                            func=AF.Square))
                                        eng.wait_ge(dv.h, dv0 + 1)
                                        ac.inc(nc.scalar.activation(
                                            out=rowC_[0:1, 0:width], in_=rowD_[0:1, 0:width],
                                            func=AF.Ln, bias=EPS_LN))
                                        eng.wait_ge(ac.h, ac0 + 2)
                                        ac.inc(nc.scalar.activation(
                                            out=rowD_[0:1, 0:width], in_=rowC_[0:1, 0:width],
                                            func=AF.Exp, scale=-0.5))

                                    def _vector(eng):
                                        eng.wait_ge(ac.h, ac0 + 1)
                                        dv.inc(nc.vector.tensor_sub(
                                            out=rowD_[0:1, 0:width], in0=rowB_[0:1, 0:width],
                                            in1=rowC_[0:1, 0:width]))
                                        for ss in range(nslice):
                                            eng.wait_ge(pe.h, pe0 + 2 * ss + 1)
                                            dv.inc(nc.vector.tensor_copy(
                                                out=dstR[:, ss * TN:(ss + 1) * TN],
                                                in_=pbcR[ss % 2][:]))
                                            eng.wait_ge(pe.h, pe0 + 2 * ss + 2)
                                            dv.inc(nc.vector.tensor_copy(
                                                out=dstM[:, ss * TN:(ss + 1) * TN],
                                                in_=pbcM[ss % 2][:]))

                                    def _tensor(eng):
                                        for ss in range(nslice):
                                            eng.wait_ge(ac.h, ac0 + 3)
                                            if ss >= 2:
                                                eng.wait_ge(dv.h, dv0 + 2 * (ss - 2) + 2)
                                            pe.inc(nc.tensor.matmul(
                                                pbcR[ss % 2][:], ones1x128[:],
                                                rowD_[0:1, ss * TN:(ss + 1) * TN],
                                                start=True, stop=True))
                                            if ss >= 2:
                                                eng.wait_ge(dv.h, dv0 + 2 * (ss - 2) + 3)
                                            pe.inc(nc.tensor.matmul(
                                                pbcM[ss % 2][:], ones1x128[:],
                                                rowA_[0:1, ss * TN:(ss + 1) * TN],
                                                start=True, stop=True))

                                    blk.scalar(_scalar)
                                    blk.vector(_vector)
                                    blk.tensor(_tensor)

                        ln_finalize(rowA, rowB, rowC, rowD, r1b, negmr1b, S_own)
                        barrier()

                    APL = [0, 0]

                with ExitStack() as S2b:
                    kvbuf = [S2b.enter_context(sb(f"kvbuf{i}", [128, S_own], BF16,
                                                  side="right")) for i in range(2)]
                    with ExitStack() as SP:
                        pqk = [SP.enter_context(ps(f"pqk{i}", [128, TN], F32))
                               for i in range(2)]
                        cols = []
                        g = 0
                        for fc in col_order:
                            is_q = fc < DC
                            cols.append({"fc": fc, "is_q": is_q, "ncs": NCo, "g0": g})
                            g += NCo
                        kv_cols = [c for c in cols if not c["is_q"]]
                        for k, c in enumerate(kv_cols):
                            c["kv_idx"] = k
                        pe0, ac0, dvb0 = pe.n, ac.n, dv.n
                        sS.begin()

                        if NCo == 1:
                            gseq = [(ci, 0) for ci in range(len(cols))]
                        else:
                            gseq = [(0, 0), (1, 0), (0, 1), (1, 1)]
                            for ci in range(2, len(cols)):
                                for ncb in range(NCo):
                                    gseq.append((ci, ncb))
                        lastpos = {}
                        for p, (ci, ncb) in enumerate(gseq):
                            lastpos[ci] = p
                        multmark = [0] * len(gseq)

                        def pe_mark(ci):
                            return pe0 + lastpos[ci] + 1

                        with nc.Block() as blk:
                            rgroups = [[2 * i, 2 * i + 1] for i in range(N_CORES // 2)]

                            def _gpsimd(eng):
                                for ci, c in enumerate(cols):
                                    if ci >= 2:
                                        eng.wait_ge(pe.h, pe_mark(ci - 2))
                                        wS.issue(nc.gpsimd.dma_start(
                                            out=wcol[ci % 2][:, 0:HC * 128],
                                            in_=wqkv_t[c["fc"]].rearrange(
                                                "p hc j -> p (hc j)")), ci)
                                    if ci == DC + 2:
                                        for k in range(DC):
                                            sS.wait(eng, k)
                                        ccK.inc(nc.gpsimd.collective_compute(
                                            "AllGather", ALU.bypass,
                                            replica_groups=rgroups,
                                            ins=[kvK_mine.ap()], outs=[kvK_g.ap()]))
                                    if ci == min(2 * DC + 3, len(cols) - 1):
                                        for k in range(DC, 2 * DC):
                                            sS.wait(eng, k)
                                        ccV.inc(nc.gpsimd.collective_compute(
                                            "AllGather", ALU.bypass,
                                            replica_groups=rgroups,
                                            ins=[kvV_mine.ap()], outs=[kvV_g.ap()]))

                            def _vector(eng):
                                def center(ts_, h0, h1):
                                    for hcb in range(h0, h1):
                                        sl = LNx[:, hcb * S_own + ts_ * TN:
                                                 hcb * S_own + (ts_ + 1) * TN]
                                        dv.inc(nc.vector.tensor_sub(
                                            out=sl, in0=sl,
                                            in1=negmr1b[:, ts_ * TN:(ts_ + 1) * TN]))

                                def mult(p):
                                    ci, ncb = gseq[p]
                                    eng.wait_ge(pe.h, pe0 + p + 1)
                                    dv.inc(nc.vector.tensor_mul(
                                        out=pqk[p % 2][:], in0=pqk[p % 2][:],
                                        in1=r1b[:, ncb * TN:(ncb + 1) * TN]))
                                    multmark[p] = dv.n

                                center(0, 0, HC)
                                APL[0] = dv.n
                                p_start = 0
                                if NCo > 1:
                                    center(1, 0, HC // 2)
                                    mult(0)
                                    center(1, HC // 2, HC)
                                    APL[1] = dv.n
                                    mult(1)
                                    p_start = 2
                                for p in range(p_start, len(gseq)):
                                    mult(p)

                            def _tensor(eng):
                                for p, (ci, ncb) in enumerate(gseq):
                                    wS.wait(eng, ci)
                                    eng.wait_ge(dv.h, APL[ncb])
                                    if p >= 2:
                                        eng.wait_ge(ac.h, ac0 + p - 1)
                                    last = None
                                    for hcb in range(HC):
                                        last = nc.tensor.matmul(
                                            pqk[p % 2][:],
                                            wcol[ci % 2][:, hcb * 128:(hcb + 1) * 128],
                                            LNx[:, hcb * S_own + ncb * TN:
                                                hcb * S_own + (ncb + 1) * TN],
                                            start=(hcb == 0), stop=(hcb == HC - 1))
                                    pe.inc(last)

                            def _scalar(eng):
                                for p, (ci, ncb) in enumerate(gseq):
                                    c = cols[ci]
                                    eng.wait_ge(dv.h, multmark[p])
                                    if c["is_q"]:
                                        dst = Q_fm[:, c["fc"] * S_own + ncb * TN:
                                                   c["fc"] * S_own + (ncb + 1) * TN]
                                    else:
                                        k = c["kv_idx"]
                                        if ncb == 0 and k >= 2:
                                            sS.wait(eng, k - 2)
                                        dst = kvbuf[k % 2][:, ncb * TN:(ncb + 1) * TN]
                                    ac.inc(nc.scalar.activation(
                                        out=dst, in_=pqk[p % 2][:], func=AF.Identity,
                                        bias=bqkv_pp[:, c["fc"]:c["fc"] + 1], scale=1.0))

                            def _sync(eng):
                                for k, c in enumerate(kv_cols):
                                    eng.wait_ge(ac.h, ac0 + lastpos[k] + 1)
                                    if c["fc"] < 2 * DC:
                                        dst = kvK_mine[(c["fc"] - DC) * 128:
                                                       (c["fc"] - DC + 1) * 128, :]
                                    else:
                                        dst = kvV_mine[(c["fc"] - 2 * DC) * 128:
                                                       (c["fc"] - 2 * DC + 1) * 128, :]
                                    sS.issue(eng.dma_start(out=dst, in_=kvbuf[k % 2][:]),
                                             k)
                                eng.wait_ge(ccK.h, ccK.n)
                                for dc in range(DC):
                                    for r in range(2):
                                        kld.inc(eng.dma_start(
                                            out=K_fm[:, dc * S_all + r * S_own:
                                                     dc * S_all + (r + 1) * S_own],
                                            in_=kvK_g[r * DC * 128 + dc * 128:
                                                      r * DC * 128 + (dc + 1) * 128,
                                                      :]), 16)

                            blk.gpsimd(_gpsimd)
                            blk.vector(_vector)
                            blk.tensor(_tensor)
                            blk.scalar(_scalar)
                            blk.sync(_sync)
                        barrier()
            SLn.close()

            with ExitStack() as S3:
                attn_out = S3.enter_context(sb("attn_out", [128, DC * S_own], BF16))
                wcol4 = [S3.enter_context(sb(f"wcol4_{i}", [128, DC * 128], BF16))
                         for i in range(3)]
                with ExitStack() as S3t:
                    V_tm = S3t.enter_context(sb("V_tm", [128, KC * D], BF16))
                    PnT = S3t.enter_context(sb("PnT", [128, KC * S_own], BF16))
                    Pf32 = [S3t.enter_context(sb(f"Pf32_{i}", [128, S_all], F32))
                            for i in range(2)]
                    Pn = [S3t.enter_context(sb(f"Pn{i}", [128, S_all], BF16))
                          for i in range(2)]
                    vstage = [S3t.enter_context(sb(f"vstage{i}", [128, S_all], BF16))
                              for i in range(2)]
                    mx4 = S3t.enter_context(sb("mx4", [128, NCa], F32))
                    negmx = S3t.enter_context(sb("negmx", [128, 1], F32))
                    bias_q = S3t.enter_context(sb("bias_q", [128, 1], F32))
                    sums = S3t.enter_context(sb("sums", [128, NCa], F32))
                    ssum = S3t.enter_context(sb("ssum", [128, 1], F32))
                    rinv = S3t.enter_context(sb("rinv", [128, 1], F32))

                    with ExitStack() as SP:
                        CCV = ccV.n
                        KLOAD = kld.n
                        wS.begin()
                        with nc.Block() as blk:
                            def _gpsimd(eng):
                                for ci in range(2):
                                    wS.issue(nc.gpsimd.dma_start(
                                        out=wcol4[ci % 3][:],
                                        in_=wo_t[ci].rearrange("p dc j -> p (dc j)")), ci)
                            blk.gpsimd(_gpsimd)

                        BSZ = 4 if KC % 4 == 0 else 1
                        NB = KC // BSZ
                        pe0, dv0 = pe.n, dv.n
                        lS.begin()
                        SV = ExitStack()
                        SV.__enter__()
                        ptpV = [SV.enter_context(ps(f"ptpV{i}", [128, 512], BF16))
                                for i in range(4)]
                        with nc.Block() as blk:
                            def _sync(eng):
                                eng.wait_ge(ccV.h, CCV)
                                for dc in range(DC):
                                    if dc >= 2:
                                        eng.wait_ge(pe.h, pe0 + (dc - 1) * NB)
                                    for r in range(2):
                                        lS.issue(eng.dma_start(
                                            out=vstage[dc % 2][:, r * S_own:
                                                       (r + 1) * S_own],
                                            in_=kvV_g[r * DC * 128 + dc * 128:
                                                      r * DC * 128 + (dc + 1) * 128,
                                                      :]), dc * 2 + r)

                            def _tensor(eng):
                                for dc in range(DC):
                                    lS.wait(eng, dc * 2)
                                    lS.wait(eng, dc * 2 + 1)
                                    for b in range(NB):
                                        gi = dc * NB + b
                                        if gi >= 4:
                                            eng.wait_ge(dv.h, dv0 + gi - 3)
                                        last = None
                                        for t in range(BSZ):
                                            kc = b * BSZ + t
                                            last = nc.tensor.transpose(
                                                ptpV[gi % 4][:, t * 128:(t + 1) * 128],
                                                vstage[dc % 2][:, kc * 128:(kc + 1) * 128],
                                                eye_b[:, :])
                                        pe.inc(last)

                            def _vector(eng):
                                for dc in range(DC):
                                    for b in range(NB):
                                        gi = dc * NB + b
                                        eng.wait_ge(pe.h, pe0 + gi + 1)
                                        src = ptpV[gi % 4].ap().rearrange(
                                            "p (t j) -> p t j", t=4)[:, 0:BSZ, :]
                                        dst = V_tm.ap().rearrange(
                                            "p (kc dd) -> p kc dd", kc=KC)[
                                            :, b * BSZ:(b + 1) * BSZ,
                                            dc * 128:(dc + 1) * 128]
                                        dv.inc(nc.vector.tensor_copy(out=dst, in_=src))

                            blk.sync(_sync)
                            blk.tensor(_tensor)
                            blk.vector(_vector)
                        VT_DV = dv.n
                        SV.close()
                        pscore = SP.enter_context(ps("pscore", [128, S_all], F32))
                        ptp = [SP.enter_context(ps(f"ptp{i}", [128, 512], BF16))
                               for i in range(2)]
                        ppv = SP.enter_context(ps("ppv", [128, DC * 128], F32))

                        pe0, dv0, ac0 = pe.n, dv.n, ac.n
                        DVQ = 8 + NB

                        def dv_q(qt):
                            return dv0 + qt * DVQ

                        pm = {}
                        cnt = 0
                        for qt in range(QT):
                            cnt += 1
                            pm[("S", qt)] = pe0 + cnt
                            if qt >= 1:
                                cnt += 1
                                pm[("PV", qt - 1)] = pe0 + cnt
                            for b in range(NB):
                                cnt += 1
                                pm[("T", qt, b)] = pe0 + cnt
                        cnt += 1
                        pm[("PV", QT - 1)] = pe0 + cnt

                        am = {}
                        acnt = 0
                        for qt in range(QT):
                            acnt += 1
                            am[("bias", qt)] = ac0 + acnt
                            acnt += NCa
                            am[("exp", qt)] = ac0 + acnt
                            if qt >= 1:
                                acnt += 1
                                am[("pvev", qt - 1)] = ac0 + acnt
                        acnt += 1
                        am[("pvev", QT - 1)] = ac0 + acnt

                        def emit_pv(eng, qt):
                            if qt >= 1:
                                eng.wait_ge(ac.h, am[("pvev", qt - 1)])
                            if qt == 0:
                                eng.wait_ge(dv.h, VT_DV)
                            last = None
                            for kc in range(KC):
                                eng.wait_ge(dv.h, dv_q(qt) + 8 + (kc // BSZ) + 1)
                                for dc in range(DC):
                                    last = nc.tensor.matmul(
                                        ppv[:, dc * 128:(dc + 1) * 128],
                                        V_tm[:, kc * D + dc * 128:kc * D + (dc + 1) * 128],
                                        PnT[:, kc * S_own + qt * 128:
                                            kc * S_own + (qt + 1) * 128],
                                        start=(kc == 0 and dc % 4 == 0),
                                        stop=(kc == KC - 1 and
                                              (dc % 4 == 3 or dc == DC - 1)))
                            pe.inc(last)

                        with nc.Block() as blk:
                            def _tensor(eng):
                                for qt in range(QT):
                                    eng.wait_ge(kld.h, KLOAD)
                                    if qt > 0:
                                        eng.wait_ge(ac.h, am[("exp", qt - 1)])
                                    last = None
                                    for ncb in range(NCa):
                                        for dc in range(DC):
                                            last = nc.tensor.matmul(
                                                pscore[:, ncb * TN:(ncb + 1) * TN],
                                                Q_fm[:, dc * S_own + qt * 128:
                                                     dc * S_own + (qt + 1) * 128],
                                                K_fm[:, dc * S_all + ncb * TN:
                                                     dc * S_all + (ncb + 1) * TN],
                                                start=(dc == 0), stop=(dc == DC - 1))
                                    pe.inc(last)
                                    if qt >= 1:
                                        emit_pv(eng, qt - 1)
                                    eng.wait_ge(dv.h, dv_q(qt) + 8)
                                    for b in range(NB):
                                        gi = qt * NB + b
                                        if gi >= 2:
                                            pq, pb = (gi - 2) // NB, (gi - 2) % NB
                                            eng.wait_ge(dv.h, dv_q(pq) + 8 + pb + 1)
                                        else:
                                            eng.wait_ge(dv.h, VT_DV)
                                        last = None
                                        for t in range(BSZ):
                                            kc = b * BSZ + t
                                            last = nc.tensor.transpose(
                                                ptp[gi % 2][:, t * 128:(t + 1) * 128],
                                                Pn[qt % 2][:, kc * 128:(kc + 1) * 128],
                                                eye_b[:, :])
                                        pe.inc(last)
                                emit_pv(eng, QT - 1)

                            def _vector(eng):
                                for qt in range(QT):
                                    eng.wait_ge(pe.h, pm[("S", qt)])
                                    if qt >= 2:
                                        eng.wait_ge(pe.h, pm[("T", qt - 2, NB - 1)])
                                    if qt >= 1:
                                        eng.wait_ge(ac.h, am[("bias", qt - 1)])
                                    for ncb in range(NCa):
                                        dv.inc(nc.vector.reduce_max(
                                            out=mx4[:, ncb:ncb + 1],
                                            in_=pscore[:, ncb * TN:(ncb + 1) * TN],
                                            axis=AX.X, negate=True))
                                    eng.wait_ge(dv.h, dv_q(qt) + NCa)
                                    dv.inc(nc.vector.tensor_reduce(
                                        out=negmx[:], in_=mx4[:], axis=AX.X, op=ALU.min))
                                    while dv.n < dv_q(qt) + 5:
                                        dv.inc(nc.vector.engine_nop())
                                    eng.wait_ge(ac.h, am[("exp", qt)])
                                    dv.inc(nc.vector.reduce_sum(out=ssum[:], in_=sums[:],
                                                                axis=AX.X))
                                    eng.wait_ge(dv.h, dv_q(qt) + 6)
                                    dv.inc(nc.vector.reciprocal(out=rinv[:], in_=ssum[:]))
                                    eng.wait_ge(dv.h, dv_q(qt) + 7)
                                    dv.inc(nc.vector.tensor_scalar(
                                        out=Pn[qt % 2][:], in0=Pf32[qt % 2][:],
                                        scalar1=rinv[:], scalar2=None, op0=ALU.mult))
                                    for b in range(NB):
                                        eng.wait_ge(pe.h, pm[("T", qt, b)])
                                        src = ptp[(qt * NB + b) % 2].ap().rearrange(
                                            "p (t j) -> p t j", t=4)[:, 0:BSZ, :]
                                        dst = PnT.ap().rearrange(
                                            "p (kc q) -> p kc q", kc=KC)[
                                            :, b * BSZ:(b + 1) * BSZ,
                                            qt * 128:(qt + 1) * 128]
                                        dv.inc(nc.vector.tensor_copy(out=dst, in_=src))

                            def _scalar(eng):
                                for qt in range(QT):
                                    eng.wait_ge(dv.h, dv_q(qt) + 5)
                                    ac.inc(nc.scalar.mul(out=bias_q[:], in_=negmx[:],
                                                         mul=d.scale))
                                    if qt >= 2:
                                        eng.wait_ge(dv.h, dv_q(qt - 2) + 8)
                                    if qt >= 1:
                                        eng.wait_ge(dv.h, dv_q(qt - 1) + 6)
                                    eng.wait_ge(ac.h, am[("bias", qt)])
                                    for ncb in range(NCa):
                                        ac.inc(nc.scalar.activation(
                                            out=Pf32[qt % 2][:, ncb * TN:(ncb + 1) * TN],
                                            in_=pscore[:, ncb * TN:(ncb + 1) * TN],
                                            func=AF.Exp, bias=bias_q[:], scale=d.scale,
                                            accum_out=sums[:, ncb:ncb + 1]))
                                    if qt >= 1:
                                        eng.wait_ge(pe.h, pm[("PV", qt - 1)])
                                        dst = attn_out.ap().rearrange(
                                            "p (dc s) -> p dc s", dc=DC)[
                                            :, :, (qt - 1) * 128:qt * 128]
                                        ac.inc(nc.scalar.copy(
                                            out=dst,
                                            in_=ppv.ap().rearrange(
                                                "p (dc j) -> p dc j", dc=DC)))
                                eng.wait_ge(pe.h, pm[("PV", QT - 1)])
                                dst = attn_out.ap().rearrange(
                                    "p (dc s) -> p dc s", dc=DC)[
                                    :, :, (QT - 1) * 128:QT * 128]
                                ac.inc(nc.scalar.copy(
                                    out=dst,
                                    in_=ppv.ap().rearrange("p (dc j) -> p dc j", dc=DC)))

                            blk.tensor(_tensor)
                            blk.vector(_vector)
                            blk.scalar(_scalar)
                            assert pe.n == pe0 + cnt, (pe.n, pe0, cnt)
                            assert ac.n == ac0 + acnt, (ac.n, ac0, acnt)
                            assert dv.n == dv0 + QT * DVQ
                        barrier()

                with ExitStack() as S4:
                    xo = [S4.enter_context(sb(f"xo{i}", [128, S_own], F32)) for i in range(6)]
                    x2t = [S4.enter_context(sb(f"x2t{i}", [128, S_own], F32)) for i in range(4)]
                    sq4 = [S4.enter_context(sb(f"sq4_{i}", [128, S_own], BF16)) for i in range(4)]
                    x2b = [S4.enter_context(sb(f"x2b_{i}", [128, S_own], BF16)) for i in range(4)]
                    rowA2 = S4.enter_context(sb("rowA2", [1, S_own], F32))
                    rowB2 = S4.enter_context(sb("rowB2", [1, S_own], F32))
                    rowC2 = S4.enter_context(sb("rowC2", [1, S_own], F32))
                    rowD2 = S4.enter_context(sb("rowD2", [1, S_own], F32))

                    with ExitStack() as SP:
                        po4 = [SP.enter_context(ps(f"po4_{i}", [128, TN], F32))
                               for i in range(3)]
                        st2_s = [SP.enter_context(ps(f"st2s{i}", [1, TN], F32))
                                 for i in range(NCo)]
                        st2_q = [SP.enter_context(ps(f"st2q{i}", [1, TN], F32))
                                 for i in range(NCo)]

                        groups = [(fc, tc) for fc in range(OC) for tc in range(NCo)]
                        NG4 = len(groups)
                        pe0, dv0, ac0 = pe.n, dv.n, ac.n
                        lS.begin()
                        sS.begin()

                        def op_mark(g):
                            return pe0 + (g + 1 if g < 2 else 2 * g - 1)

                        def stats_mark(h):
                            if h < NG4 - 2:
                                return pe0 + 2 * h + 4
                            return pe0 + 2 * NG4 - 2 + (h - (NG4 - 2)) + 1

                        with nc.Block() as blk:
                            def _gpsimd(eng):
                                for fc in range(2, OC):
                                    if fc >= 3:
                                        eng.wait_ge(pe.h,
                                                    op_mark((fc - 3) * NCo + NCo - 1))
                                    wS.issue(nc.gpsimd.dma_start(
                                        out=wcol4[fc % 3][:],
                                        in_=wo_t[fc].rearrange("p dc j -> p (dc j)")), fc)

                            def _sync(eng):
                                for fc in range(OC):
                                    if fc >= 6:
                                        eng.wait_ge(dv.h,
                                                    dv0 + 2 * ((fc - 6) * NCo + NCo - 1) + 1)
                                    lS.issue(eng.dma_start(
                                        out=xo[fc % 6][:],
                                        in_=x_fm[fc * 128:(fc + 1) * 128, :]), fc)

                            def _tensor(eng):
                                for g, (fc, tc) in enumerate(groups):
                                    if tc == 0:
                                        wS.wait(eng, fc)
                                    if g >= 3:
                                        eng.wait_ge(dv.h, dv0 + 2 * (g - 3) + 1)
                                    last = None
                                    for dc in range(DC):
                                        last = nc.tensor.matmul(
                                            po4[g % 3][:],
                                            wcol4[fc % 3][:, dc * 128:(dc + 1) * 128],
                                            attn_out[:, dc * S_own + tc * TN:
                                                     dc * S_own + (tc + 1) * TN],
                                            start=(dc == 0), stop=(dc == DC - 1))
                                    pe.inc(last)
                                    if g >= 2:
                                        h = g - 2
                                        hfc, htc = groups[h]
                                        eng.wait_ge(ac.h, ac0 + h + 1)
                                        eng.wait_ge(dv.h, dv0 + 2 * h + 2)
                                        nc.tensor.matmul(
                                            st2_s[htc][:], ones128b[:],
                                            x2b[hfc % 4][:, htc * TN:(htc + 1) * TN],
                                            start=(hfc == 0), stop=(hfc == OC - 1))
                                        last = nc.tensor.matmul(
                                            st2_q[htc][:], ones128b[:],
                                            sq4[hfc % 4][:, htc * TN:(htc + 1) * TN],
                                            start=(hfc == 0), stop=(hfc == OC - 1))
                                        pe.inc(last)
                                for h in range(NG4 - 2, NG4):
                                    hfc, htc = groups[h]
                                    eng.wait_ge(ac.h, ac0 + h + 1)
                                    eng.wait_ge(dv.h, dv0 + 2 * h + 2)
                                    nc.tensor.matmul(
                                        st2_s[htc][:], ones128b[:],
                                        x2b[hfc % 4][:, htc * TN:(htc + 1) * TN],
                                        start=(hfc == 0), stop=(hfc == OC - 1))
                                    last = nc.tensor.matmul(
                                        st2_q[htc][:], ones128b[:],
                                        sq4[hfc % 4][:, htc * TN:(htc + 1) * TN],
                                        start=(hfc == 0), stop=(hfc == OC - 1))
                                    pe.inc(last)

                            def _vector(eng):
                                for g, (fc, tc) in enumerate(groups):
                                    eng.wait_ge(pe.h, op_mark(g))
                                    if tc == 0:
                                        lS.wait(eng, fc)
                                    if tc == 0 and fc >= 4:
                                        hlast = (fc - 4) * NCo + NCo - 1
                                        eng.wait_ge(ac.h, ac0 + hlast + 1)
                                        sS.wait(eng, fc - 4)
                                        eng.wait_ge(pe.h, stats_mark(hlast))
                                    dv.inc(nc.vector.scalar_tensor_tensor(
                                        out=x2t[fc % 4][:, tc * TN:(tc + 1) * TN],
                                        in0=po4[g % 3][:], scalar=bo_pp[:, fc:fc + 1],
                                        in1=xo[fc % 6][:, tc * TN:(tc + 1) * TN],
                                        op0=ALU.add, op1=ALU.add))
                                    eng.wait_ge(dv.h, dv0 + 2 * g + 1)
                                    dv.inc(nc.vector.tensor_copy(
                                        out=x2b[fc % 4][:, tc * TN:(tc + 1) * TN],
                                        in_=x2t[fc % 4][:, tc * TN:(tc + 1) * TN]))

                            def _scalar(eng):
                                for g, (fc, tc) in enumerate(groups):
                                    eng.wait_ge(dv.h, dv0 + 2 * g + 1)
                                    if tc == 0 and fc >= 4:
                                        eng.wait_ge(pe.h,
                                                    stats_mark((fc - 4) * NCo + NCo - 1))
                                    ac.inc(nc.scalar.activation(
                                        out=sq4[fc % 4][:, tc * TN:(tc + 1) * TN],
                                        in_=x2t[fc % 4][:, tc * TN:(tc + 1) * TN],
                                        func=AF.Square))
                                    if tc == NCo - 1:
                                        sS.issue(nc.scalar.dma_start(
                                            out=x2_spill[fc], in_=x2t[fc % 4][:]), fc)

                            blk.gpsimd(_gpsimd)
                            blk.sync(_sync)
                            blk.tensor(_tensor)
                            blk.vector(_vector)
                            blk.scalar(_scalar)
                        P4_PE = pe.n


                        with nc.Block() as blk:
                            def _vector(eng):
                                eng.wait_ge(pe.h, P4_PE)
                                for t in range(NCo):
                                    dv.inc(nc.vector.tensor_scalar(
                                        out=rowA2[0:1, t * TN:(t + 1) * TN],
                                        in0=st2_s[t][:], scalar1=1.0 / H, scalar2=None,
                                        op0=ALU.mult))
                                    dv.inc(nc.vector.tensor_scalar(
                                        out=rowB2[0:1, t * TN:(t + 1) * TN],
                                        in0=st2_q[t][:], scalar1=1.0 / H, scalar2=None,
                                        op0=ALU.mult))
                            blk.vector(_vector)

                    ln_finalize(rowA2, rowB2, rowC2, rowD2, r2b, negmr2b, S_own)
                    barrier()
        SQ.close()

        with ExitStack() as S6:
            x2n = S6.enter_context(sb("x2n", [128, HC * S_own], BF16))
            wcol8 = [S6.enter_context(sb(f"wcol8_{i}", [128, HC * 128], BF16))
                     for i in range(2)]
            z_sb = S6.enter_context(sb("z_sb", [ER, S_own], F32))
            zs = S6.enter_context(sb("zs", [ER, S_own], BF16))
            a_r = S6.enter_context(sb("a_r", [EA, S_own], F32))
            e_r = S6.enter_context(sb("e_r", [EA, S_own], F32))
            t_r = S6.enter_context(sb("t_r", [EA, S_own], F32))
            alpha = S6.enter_context(sb("alpha", [EA, S_own], F32))
            mrow = S6.enter_context(sb("mrow", [1, S_own], F32))
            rrow = S6.enter_context(sb("rrow", [1, S_own], F32))

            with ExitStack() as S6a:
                CH6 = 2 if HC % 2 == 0 else 1
                x2s = [S6a.enter_context(sb(f"x2s{i}", [128, CH6 * S_own], F32))
                       for i in range(3)]
                with ExitStack() as SPz:
                    paw = SPz.enter_context(ps("paw", [AW, S_own], F32))
                    dv0, pe0 = dv.n, pe.n
                    NCH6 = HC // CH6
                    lS.begin()
                    wS.begin()
                    xS.begin()
                    x2v = x2_spill.ap().rearrange("h p t -> p h t")
                    with nc.Block() as blk:
                        def _gpsimd(eng):
                            for ci in range(2):
                                wS.issue(nc.gpsimd.dma_start(
                                    out=wcol8[ci % 2][:, 0:HC * 128],
                                    in_=wb_t[ci].rearrange("p hc j -> p (hc j)")), ci)
                            for i in range(NCH6):
                                if i % 2 == 1:
                                    if i >= 3:
                                        eng.wait_ge(dv.h, dv0 + CH6 * (i - 3) + CH6)
                                    xS.issue(nc.gpsimd.dma_start(
                                        out=x2s[i % 3][:],
                                        in_=x2v[:, i * CH6:(i + 1) * CH6, :]), i)
                        blk.gpsimd(_gpsimd)

                        def _sync(eng):
                            for i in range(NCH6):
                                if i % 2 == 0:
                                    if i >= 3:
                                        eng.wait_ge(dv.h, dv0 + CH6 * (i - 3) + CH6)
                                    lS.issue(eng.dma_start(
                                        out=x2s[i % 3][:],
                                        in_=x2v[:, i * CH6:(i + 1) * CH6, :]), i)

                        def _vector(eng):
                            for i in range(NCH6):
                                if i % 2 == 0:
                                    lS.wait(eng, i)
                                else:
                                    xS.wait(eng, i)
                                for c in range(CH6):
                                    hcb = i * CH6 + c
                                    dv.inc(nc.vector.tensor_sub(
                                        out=x2n[:, hcb * S_own:(hcb + 1) * S_own],
                                        in0=x2s[i % 3][:, c * S_own:(c + 1) * S_own],
                                        in1=negmr2b[:]))

                        def _tensor(eng):
                            for hcb in range(HC):
                                eng.wait_ge(dv.h, dv0 + hcb + 1)
                                last = None
                                for t in range(NCo):
                                    last = nc.tensor.matmul(
                                        paw[:, t * TN:(t + 1) * TN],
                                        AW_T[:, hcb * AW:(hcb + 1) * AW],
                                        x2n[:, hcb * S_own + t * TN:
                                            hcb * S_own + (t + 1) * TN],
                                        start=(hcb == 0), stop=(hcb == HC - 1))
                                pe.inc(last)

                        blk.sync(_sync)
                        blk.vector(_vector)
                        blk.tensor(_tensor)
                    P6_PE = pe.n

                    with nc.Block() as blk:
                        def _vector(eng):
                            eng.wait_ge(pe.h, P6_PE)
                            dv.inc(nc.vector.tensor_copy(out=z_sb[:],
                                                          in_=paw[0:ER, :]))
                        blk.vector(_vector)
                    ZEV = dv.n

                    with nc.Block() as blk:
                        def _vector(eng):
                            eng.wait_ge(pe.h, P6_PE)
                            dv.inc(nc.vector.tensor_mul(out=paw[ER:AW, :],
                                                        in0=paw[ER:AW, :],
                                                        in1=r2b[0:EA, :]))
                        blk.vector(_vector)
                        LSC = dv.n

                        def _scalar(eng):
                            eng.wait_ge(dv.h, LSC)
                            ac.inc(nc.scalar.activation(out=a_r[:], in_=paw[ER:AW, :],
                                                        func=AF.Identity, bias=bias_r[:],
                                                        scale=1.0))
                        blk.scalar(_scalar)
                    AEV = ac.n

                LSC_DV = dv.n
                with ExitStack() as S8:
                    x2r = [S8.enter_context(sb(f"x2r{i}", [128, S_own], F32))
                           for i in range(2)]
                    out_t = [S8.enter_context(sb(f"out_t{i}", [128, S_own], F32))
                             for i in range(2)]
                    with ExitStack() as SP:
                        pys = [SP.enter_context(ps(f"py{i}", [128, TN], F32))
                               for i in range(4)]
                        pr1 = SP.enter_context(ps("pr1", [1, S_own], F32))
                        pr4 = SP.enter_context(ps("pr4", [ER, S_own], F32))

                        pe0, dvr0, ac0 = pe.n, dv.n, ac.n
                        dv8 = dvr0 + 5
                        NHOC = min(2, OC)
                        NGH = NHOC * NCo
                        lS.begin()
                        sS.begin()
                        lfence = [(x, x.n) for x in l01]
                        sfence = [(x, x.n) for x in s01]

                        NFILL = 9

                        def m_lora(g):
                            if g < NGH:
                                return pe0 + NFILL + g + 1
                            return pe0 + NFILL + NGH + (g - NGH) + 1

                        def emit_w_part(eng, oc, t, tile):
                            last = None
                            for hcb in range(HC):
                                last = nc.tensor.matmul(
                                    tile[:],
                                    wcol8[oc % 2][:, hcb * 128:(hcb + 1) * 128],
                                    x2n[:, hcb * S_own + t * TN:
                                        hcb * S_own + (t + 1) * TN],
                                    start=(hcb == 0), stop=False)
                            return last

                        with nc.Block() as blk:
                            def _tensor(eng):
                                eng.wait_ge(dv.h, LSC_DV)
                                eng.wait_ge(ac.h, AEV)
                                heads = [(oc, t) for oc in range(NHOC)
                                         for t in range(NCo)]
                                hi = 0

                                def fill(eng):
                                    nonlocal hi
                                    if hi < len(heads):
                                        oc, t = heads[hi]
                                        if t == 0:
                                            wS.wait(eng, oc)
                                        pe.inc(emit_w_part(eng, oc, t, pys[hi]))
                                    else:
                                        pe.inc(nc.tensor.matmul(
                                            pys[3][0:1, 0:1], ones128b[0:1, 0:1],
                                            ones128b[0:1, 0:1],
                                            start=True, stop=True))
                                    hi += 1

                                fill(eng)
                                for t in range(NCo):
                                    last = nc.tensor.matmul(
                                        pr1[0:1, t * TN:(t + 1) * TN], quart4[:],
                                        a_r[:, t * TN:(t + 1) * TN],
                                        start=True, stop=True)
                                pe.inc(last)
                                fill(eng)
                                eng.wait_ge(dv.h, dvr0 + 1)
                                for t in range(NCo):
                                    last = nc.tensor.matmul(
                                        pr4[0:EA, t * TN:(t + 1) * TN], ones1x4[:],
                                        mrow[0:1, t * TN:(t + 1) * TN],
                                        start=True, stop=True)
                                pe.inc(last)
                                fill(eng)
                                eng.wait_ge(ac.h, ac0 + 1)
                                for t in range(NCo):
                                    last = nc.tensor.matmul(
                                        pr1[0:1, t * TN:(t + 1) * TN], ones4[:],
                                        e_r[:, t * TN:(t + 1) * TN],
                                        start=True, stop=True)
                                pe.inc(last)
                                fill(eng)
                                eng.wait_ge(dv.h, dvr0 + 3)
                                for t in range(NCo):
                                    last = nc.tensor.matmul(
                                        pr4[0:EA, t * TN:(t + 1) * TN], ones1x4[:],
                                        rrow[0:1, t * TN:(t + 1) * TN],
                                        start=True, stop=True)
                                pe.inc(last)
                                eng.wait_ge(dv.h, dvr0 + 4)
                                for t in range(NCo):
                                    last = nc.tensor.matmul(
                                        pr4[0:ER, t * TN:(t + 1) * TN], r2m_s[:],
                                        alpha[:, t * TN:(t + 1) * TN],
                                        start=True, stop=True)
                                pe.inc(last)
                                eng.wait_ge(dv.h, dvr0 + 5)
                                for g in range(NGH):
                                    oc, t = heads[g]
                                    pe.inc(nc.tensor.matmul(
                                        pys[g][:], Bcat[:, oc * 128:(oc + 1) * 128],
                                        zs[:, t * TN:(t + 1) * TN],
                                        start=False, stop=True))
                                for oc in range(NHOC, OC):
                                    wS.wait(eng, oc)
                                    for t in range(NCo):
                                        g = oc * NCo + t
                                        if g >= 4:
                                            eng.wait_ge(dv.h, dv8 + 2 * (g - 4) + 2)
                                        for hcb in range(HC):
                                            nc.tensor.matmul(
                                                pys[g % 4][:],
                                                wcol8[oc % 2][:, hcb * 128:(hcb + 1) * 128],
                                                x2n[:, hcb * S_own + t * TN:
                                                    hcb * S_own + (t + 1) * TN],
                                                start=(hcb == 0), stop=False)
                                        last = nc.tensor.matmul(
                                            pys[g % 4][:], Bcat[:, oc * 128:(oc + 1) * 128],
                                            zs[:, t * TN:(t + 1) * TN],
                                            start=False, stop=True)
                                        pe.inc(last)

                            def _vector(eng):
                                eng.wait_ge(pe.h, pe0 + 2)
                                dv.inc(nc.vector.tensor_copy(out=mrow[:],
                                                             in_=pr1[0:1, :]))
                                eng.wait_ge(pe.h, pe0 + 4)
                                dv.inc(nc.vector.tensor_sub(out=t_r[:], in0=a_r[:],
                                                            in1=pr4[0:EA, :]))
                                eng.wait_ge(pe.h, pe0 + 6)
                                eng.wait_ge(dv.h, dvr0 + 2)
                                dv.inc(nc.vector.reciprocal(out=rrow[:],
                                                            in_=pr1[0:1, :]))
                                eng.wait_ge(pe.h, pe0 + 8)
                                dv.inc(nc.vector.tensor_mul(out=alpha[:], in0=e_r[:],
                                                            in1=pr4[0:EA, :]))
                                eng.wait_ge(pe.h, pe0 + 9)
                                eng.wait_ge(dv.h, ZEV)
                                dv.inc(nc.vector.tensor_mul(out=zs[:], in0=z_sb[:],
                                                            in1=pr4[0:ER, :]))
                                for oc in range(OC):
                                    for t in range(NCo):
                                        g = oc * NCo + t
                                        eng.wait_ge(pe.h, m_lora(g))
                                        if t == 0:
                                            lS.wait(eng, oc)
                                        if oc >= 2 and t == 0:
                                            sS.wait(eng, oc - 2)
                                        dv.inc(nc.vector.tensor_mul(
                                            out=pys[g % 4][:], in0=pys[g % 4][:],
                                            in1=r2b[:, t * TN:(t + 1) * TN]))
                                        eng.wait_ge(dv.h, dv8 + 2 * g + 1)
                                        dv.inc(nc.vector.scalar_tensor_tensor(
                                            out=out_t[oc % 2][:, t * TN:(t + 1) * TN],
                                            in0=pys[g % 4][:],
                                            scalar=bb_pp[:, oc:oc + 1],
                                            in1=x2r[oc % 2][:, t * TN:(t + 1) * TN],
                                            op0=ALU.add, op1=ALU.add))

                            def _scalar(eng):
                                eng.wait_ge(dv.h, dvr0 + 2)
                                ac.inc(nc.scalar.activation(out=e_r[:], in_=t_r[:],
                                                            func=AF.Exp))

                            def _sync(eng):
                                for (x, n) in lfence:
                                    eng.wait_ge(x.h, n)
                                for (x, n) in sfence:
                                    eng.wait_ge(x.h, n)
                                for oc in range(OC):
                                    if oc >= 2:
                                        eng.wait_ge(dv.h,
                                                    dv8 + 2 * ((oc - 2) * NCo + NCo))
                                    lS.issue(eng.dma_start(out=x2r[oc % 2][:],
                                                           in_=x2_spill[oc]), oc)
                                    eng.wait_ge(dv.h, dv8 + 2 * (oc * NCo + NCo))
                                    sS.issue(eng.dma_start(
                                        out=out[oc * 128:(oc + 1) * 128, :],
                                        in_=out_t[oc % 2][:]), oc)

                            def _gpsimd(eng):
                                for oc in range(2, OC):
                                    if oc == 2:
                                        eng.wait_ge(pe.h, pe0 + 3)
                                    elif oc == 3:
                                        eng.wait_ge(pe.h, pe0 + 7)
                                    else:
                                        eng.wait_ge(pe.h,
                                                    m_lora((oc - 2) * NCo + NCo - 1))
                                    wS.issue(nc.gpsimd.dma_start(
                                        out=wcol8[oc % 2][:, 0:HC * 128],
                                        in_=wb_t[oc].rearrange("p hc j -> p (hc j)")), oc)

                            blk.tensor(_tensor)
                            blk.vector(_vector)
                            blk.scalar(_scalar)
                            blk.sync(_sync)
                            blk.gpsimd(_gpsimd)

                    barrier()

    return nc


def prep_inputs(d: Dims, ins):
    x = np.asarray(ins["L0_attn_x"])
    Wqkv = np.asarray(ins["L0_attn_Wqkv"])
    bqkv = np.ascontiguousarray(np.asarray(ins["L0_attn_bqkv"]))
    Wo = np.asarray(ins["L0_attn_Wo"])
    bo = np.ascontiguousarray(np.asarray(ins["L0_attn_bo"]))
    Wb = np.asarray(ins["L0_ffn_W_base"])
    bb = np.ascontiguousarray(np.asarray(ins["L0_ffn_b_base"]))
    loraA = np.asarray(ins["L0_ffn_lora_A"])
    loraB = np.asarray(ins["L0_ffn_lora_B"])
    routerW = np.ascontiguousarray(np.asarray(ins["L0_ffn_router_W"]))
    routerb = np.asarray(ins["L0_ffn_router_b"])
    prior = np.asarray(ins["L0_ffn_domain_prior"])
    active = np.ascontiguousarray(np.asarray(ins["L0_ffn_active"]).astype(np.int32))

    H, S_own, S_all = d.H, d.S_own, d.S_all
    HC, DC, FC, OC = d.HC, d.DC, d.FC, d.OC

    wqkv_t = np.ascontiguousarray(Wqkv.reshape(HC, 128, FC, 128).transpose(2, 1, 0, 3))
    wo_t = np.ascontiguousarray(Wo.reshape(DC, 128, OC, 128).transpose(2, 1, 0, 3))
    wb_t = np.ascontiguousarray(Wb.reshape(OC, 128, HC, 128).transpose(0, 3, 2, 1))
    loraA_f = np.ascontiguousarray(loraA.reshape(d.E_tot * d.R, H))
    loraBrt_f = np.ascontiguousarray(loraB.transpose(0, 2, 1).reshape(d.E_tot * d.R, H))
    routerb_c = np.ascontiguousarray(routerb.reshape(d.E_tot, 1))
    prior_c = np.ascontiguousarray(prior.reshape(d.E_tot, 1))

    shared = {
        "wqkv_t": wqkv_t, "wo_t": wo_t, "wb_t": wb_t,
        "bqkv": bqkv, "bo": bo, "bb": bb,
        "loraA": loraA_f, "loraBrt": loraBrt_f,
        "routerW": routerW, "routerb": routerb_c, "prior": prior_c,
        "active": active,
    }
    in_maps = []
    halves = S_all // S_own
    n_cores = x.shape[0] * halves
    for core in range(n_cores):
        b = core // halves
        h = core % halves
        m = dict(shared)
        m["x_fm"] = np.ascontiguousarray(x[b][h * S_own:(h + 1) * S_own].T)
        in_maps.append(m)
    return in_maps


def assemble_output(d: Dims, results):
    halves = d.S_all // d.S_own
    B = len(results) // halves
    out = np.empty((B, d.S_all, d.H), dtype=np.float32)
    for core in range(len(results)):
        b = core // halves
        h = core % halves
        out[b, h * d.S_own:(h + 1) * d.S_own, :] = results[core]["out"].T
    return out


def kernel(**inputs):
    d = Dims()
    nc = build_program(d)
    in_maps = prep_inputs(d, inputs)
    res = run_bass_kernel_spmd(nc, in_maps, core_ids=list(range(N_CORES)))
    return assemble_output(d, res.results)


# version 30
# speedup vs baseline: 1.0245x; 1.0212x over previous
import numpy as np
import ml_dtypes
from contextlib import ExitStack

import concourse.bass as bass
import concourse.mybir as mybir
from concourse.bass_utils import run_bass_kernel_spmd

F32 = mybir.dt.float32
BF16 = mybir.dt.bfloat16
I32 = mybir.dt.int32
AF = mybir.ActivationFunctionType
ALU = mybir.AluOpType
AX = mybir.AxisListType

EPS_LN = 1e-5
EPS_PRIOR = 1e-12
SCALING = 2.0
N_CORES = 8


class Dims:
    def __init__(self, H=4096, D=1024, S_own=1024, S_all=2048, E_tot=16, E_act=4, R=8):
        self.H, self.D, self.S_own, self.S_all = H, D, S_own, S_all
        self.E_tot, self.E_act, self.R = E_tot, E_act, R
        self.ER = E_act * R
        self.HC = H // 128
        self.DC = D // 128
        self.FC = 3 * self.DC
        self.OC = self.HC
        self.TN = min(512, S_own)
        self.NCo = S_own // self.TN
        self.NCa = S_all // self.TN
        self.W1 = min(1024, S_all)
        self.N1 = S_all // self.W1
        self.QT = S_own // 128
        self.KC = S_all // 128
        self.scale = 1.0 / float(np.sqrt(float(D)))


class Sem:
    def __init__(self, nc, st, name):
        self.h = st.enter_context(nc.semaphore(name))
        self.n = 0

    def inc(self, inst, k=1):
        inst.then_inc(self.h, k)
        self.n += k
        return self.n


class DStream:

    def __init__(self, sems):
        self.sems = sems
        self.base = [s.n for s in sems]

    def begin(self):
        self.base = [s.n for s in self.sems]

    def issue(self, inst, i):
        s = self.sems[i % len(self.sems)]
        inst.then_inc(s.h, 16)
        s.n += 16

    def wait(self, eng, i):
        nb = len(self.sems)
        s = self.sems[i % nb]
        eng.wait_ge(s.h, self.base[i % nb] + 16 * (i // nb + 1))


def build_program(d: Dims):
    nc = bass.Bass("TRN2", target_bir_lowering=False)
    H, D, S_own, S_all = d.H, d.D, d.S_own, d.S_all
    HC, DC, FC, OC = d.HC, d.DC, d.FC, d.OC
    TN, NCo, NCa = d.TN, d.NCo, d.NCa
    QT, KC, ER = d.QT, d.KC, d.ER
    EA = d.E_act

    x_fm = nc.dram_tensor("x_fm", [H, S_own], F32, kind="ExternalInput")
    wqkv_t = nc.dram_tensor("wqkv_t", [FC, 128, HC, 128], F32, kind="ExternalInput")
    wo_t = nc.dram_tensor("wo_t", [OC, 128, DC, 128], F32, kind="ExternalInput")
    wb_t = nc.dram_tensor("wb_t", [OC, 128, HC, 128], F32, kind="ExternalInput")
    bqkv = nc.dram_tensor("bqkv", [3 * D], F32, kind="ExternalInput")
    bo = nc.dram_tensor("bo", [H], F32, kind="ExternalInput")
    bb = nc.dram_tensor("bb", [H], F32, kind="ExternalInput")
    loraA = nc.dram_tensor("loraA", [d.E_tot * d.R, H], F32, kind="ExternalInput")
    loraBrt = nc.dram_tensor("loraBrt", [d.E_tot * d.R, H], F32, kind="ExternalInput")
    routerW = nc.dram_tensor("routerW", [d.E_tot, H], F32, kind="ExternalInput")
    routerb = nc.dram_tensor("routerb", [d.E_tot, 1], F32, kind="ExternalInput")
    prior = nc.dram_tensor("prior", [d.E_tot, 1], F32, kind="ExternalInput")
    active = nc.dram_tensor("active", [EA], I32, kind="ExternalInput")
    out = nc.dram_tensor("out", [H, S_own], F32, kind="ExternalOutput")

    kvK_mine = nc.dram_tensor("kvK_mine", [DC * 128, S_own], BF16, kind="Internal")
    kvV_mine = nc.dram_tensor("kvV_mine", [DC * 128, S_own], BF16, kind="Internal")
    kvK_g = nc.dram_tensor("kvK_g", [2 * DC * 128, S_own], BF16, kind="Internal")
    kvV_g = nc.dram_tensor("kvV_g", [2 * DC * 128, S_own], BF16, kind="Internal")
    x2_spill = nc.dram_tensor("x2_spill", [HC, 128, S_own], F32, kind="Internal")

    eye_bf16_c = nc.inline_tensor(np.asarray(np.eye(128, dtype=ml_dtypes.bfloat16)), name="c_eye_b")
    eye_f32_c = nc.inline_tensor(np.eye(128, dtype=np.float32), name="c_eye_f")
    ones128_b_c = nc.inline_tensor(np.ones((128, 1), dtype=ml_dtypes.bfloat16), name="c_ones128b")
    ones_sq_c = nc.inline_tensor(np.ones((128, 128), dtype=ml_dtypes.bfloat16), name="c_ones_sq")
    ones_1x128_c = nc.inline_tensor(np.ones((1, 128), dtype=np.float32), name="c_ones1x128")
    quarter4_c = nc.inline_tensor(np.full((EA, 1), 1.0 / EA, dtype=np.float32), name="c_quart")
    ones4_c = nc.inline_tensor(np.ones((EA, 1), dtype=np.float32), name="c_ones4")
    ones_1x4_c = nc.inline_tensor(np.ones((1, EA), dtype=np.float32), name="c_ones1x4")
    rrep8 = np.zeros((EA, ER), dtype=np.float32)
    r2m = np.zeros((EA, ER), dtype=np.float32)
    for e in range(EA):
        rrep8[e, e * d.R:(e + 1) * d.R] = float(d.R)
        r2m[e, e * d.R:(e + 1) * d.R] = SCALING
    rrep8_c = nc.inline_tensor(rrep8, name="c_rrep8")
    r2m_c = nc.inline_tensor(r2m, name="c_r2m")
    modR_c = nc.inline_tensor((np.arange(ER, dtype=np.float32) % d.R).reshape(ER, 1), name="c_modR")
    cpool_c = nc.inline_tensor(np.tile(np.array([[EPS_LN, EPS_PRIOR]], dtype=np.float32), (128, 1)), name="c_cpool")

    G = ExitStack()
    with G:
        qd = Sem(nc, G, "qd")
        qg = Sem(nc, G, "qg")
        kld = Sem(nc, G, "kld")
        ccK = Sem(nc, G, "ccK")
        ccV = Sem(nc, G, "ccV")
        w01 = [Sem(nc, G, f"w{i}") for i in range(3)]
        l01 = [Sem(nc, G, f"l{i}") for i in range(6)]
        s01 = [Sem(nc, G, f"s{i}") for i in range(4)]
        x01 = [Sem(nc, G, f"x{i}") for i in range(8)]
        wS = DStream(w01)
        lS = DStream(l01)
        sS = DStream(s01)
        xS = DStream(x01)
        pe = Sem(nc, G, "pe")
        dv = Sem(nc, G, "dv")
        ac = Sem(nc, G, "ac")
        pp = Sem(nc, G, "pp")
        bar = Sem(nc, G, "bar")

        def barrier():
            with nc.Block() as blk:
                tgt = bar.n + 5

                def mk(fences):
                    def f(eng):
                        for (s, n) in fences:
                            eng.wait_ge(s.h, n)
                        eng.sem_inc(bar.h, 1)
                        eng.wait_ge(bar.h, tgt)
                    return f
                blk.sync(mk([(qd, qd.n), (kld, kld.n)] +
                            [(x, x.n) for x in l01] + [(x, x.n) for x in s01]))
                blk.tensor(mk([(pe, pe.n)]))
                blk.vector(mk([(dv, dv.n)]))
                blk.scalar(mk([(ac, ac.n)]))
                blk.gpsimd(mk([(qg, qg.n), (pp, pp.n)] +
                               [(x, x.n) for x in w01]))
            bar.n += 5

        sb = lambda *a, **k: nc.sbuf_tensor(*a, **k)
        ps = lambda *a, **k: nc.psum_tensor(*a, **k)

        AW = ER + EA
        AW_T = G.enter_context(sb("AW_T", [128, HC * AW], BF16))
        Bcat = G.enter_context(sb("Bcat", [ER, H], BF16))
        bqkv_pp = G.enter_context(sb("bqkv_pp", [128, FC], F32))
        bo_pp = G.enter_context(sb("bo_pp", [128, OC], F32))
        bb_pp = G.enter_context(sb("bb_pp", [128, OC], F32))
        eye_b = G.enter_context(sb("eye_b", [128, 128], BF16))
        eye_f = G.enter_context(sb("eye_f", [128, 128], F32))
        ones128b = G.enter_context(sb("ones128b", [128, 1], BF16))
        ones_sq = G.enter_context(sb("ones_sq", [128, 128], BF16))
        ones1x128 = G.enter_context(sb("ones1x128", [1, 128], F32))
        quart4 = G.enter_context(sb("quart4", [EA, 1], F32))
        ones4 = G.enter_context(sb("ones4", [EA, 1], F32))
        ones1x4 = G.enter_context(sb("ones1x4", [1, EA], F32))
        rrep8_s = G.enter_context(sb("rrep8_s", [EA, ER], F32))
        r2m_s = G.enter_context(sb("r2m_s", [EA, ER], F32))
        modR_s = G.enter_context(sb("modR_s", [ER, 1], F32))
        act_i = G.enter_context(sb("act_i", [EA, 1], I32))
        actf = G.enter_context(sb("actf", [EA, 1], F32))
        idxf = G.enter_context(sb("idxf", [ER, 1], F32))
        idx_i = G.enter_context(sb("idx_i", [ER, 1], I32))
        rb_g = G.enter_context(sb("rb_g", [EA, 1], F32))
        prior_g = G.enter_context(sb("prior_g", [EA, 1], F32))
        bias_r = G.enter_context(sb("bias_r", [EA, 1], F32))
        cpool = G.enter_context(sb("cpool", [128, 2], F32))
        nc.const_aps.register(F32, EPS_LN, cpool[:, 0:1])
        nc.const_aps.register(F32, EPS_PRIOR, cpool[:, 1:2])
        r2b = G.enter_context(sb("r2b", [128, S_own], F32))
        negmr2b = G.enter_context(sb("negmr2b", [128, S_own], F32))

        SQ = ExitStack()
        SQ.__enter__()
        Q_fm = SQ.enter_context(sb("Q_fm", [128, DC * S_own], BF16, side="right"))
        wcol = [SQ.enter_context(sb(f"wcol{i}", [128, HC * 128], BF16, side="right"))
                for i in range(2)]
        W1L = min(1024, S_own)
        if HC % 2 == 0 and S_own == W1L:
            CH1 = 2
        elif HC % 4 == 0 and S_own == W1L:
            CH1 = 4
        else:
            CH1 = 1
        wS.begin()
        xS.begin()
        col_order = list(range(DC, 3 * DC)) + list(range(DC))
        xv = x_fm.ap().rearrange("(c p) t -> p c t", p=128)
        K_fm_l = [None]
        SLn = ExitStack()
        SLn.__enter__()
        LNx = SLn.enter_context(sb("LNx", [128, HC * S_own], BF16))
        S1R = ExitStack()
        S1R.__enter__()
        xsq = [S1R.enter_context(sb(f"xsq{i}", [128, CH1 * W1L], BF16, side="right"))
               for i in range(2)]
        rowA = S1R.enter_context(sb("rowA", [1, S_own], F32, side="right"))
        rowB = S1R.enter_context(sb("rowB", [1, S_own], F32, side="right"))
        NSTAT = S_own // TN
        chunks = list(range(HC // CH1))
        NCHU = len(chunks)
        TPC = (CH1 * W1L) // TN

        with ExitStack() as S0:
            Acat = S0.enter_context(sb("Acat", [ER, H], F32))
            Bcat_f = S0.enter_context(sb("Bcat_f", [ER, H], F32))
            Wr_g = S0.enter_context(sb("Wr_g", [EA, H], F32))
            brow_q = S0.enter_context(sb("brow_q", [FC, 128], F32))
            brow_o = S0.enter_context(sb("brow_o", [OC, 128], F32))
            brow_b = S0.enter_context(sb("brow_b", [OC, 128], F32))
            with ExitStack() as SP:
                p0x = [SP.enter_context(ps(f"p0x{i}", [128, 64], F32)) for i in range(3)]
                psi = SP.enter_context(ps("psi", [ER, 1], F32))
                st_sum = [SP.enter_context(ps(f"st_s{i}", [1, TN], F32))
                          for i in range(NSTAT)]
                st_sq = [SP.enter_context(ps(f"st_q{i}", [1, TN], F32))
                         for i in range(NSTAT)]

                with nc.Block() as blk:
                    qd0 = qd.n

                    def _sync(eng):
                        for dst, src in [
                            (eye_b[:], eye_bf16_c.ap()), (eye_f[:], eye_f32_c.ap()),
                            (ones128b[:], ones128_b_c.ap()), (ones_sq[:], ones_sq_c.ap()),
                            (ones1x128[:], ones_1x128_c.ap()),
                            (quart4[:], quarter4_c.ap()), (ones4[:], ones4_c.ap()),
                            (ones1x4[:], ones_1x4_c.ap()), (rrep8_s[:], rrep8_c.ap()),
                            (r2m_s[:], r2m_c.ap()), (modR_s[:], modR_c.ap()),
                            (act_i[:], active.rearrange("(p o) -> p o", o=1)),
                            (brow_q[:], bqkv.rearrange("(p j) -> p j", j=128)),
                            (brow_o[:], bo.rearrange("(p j) -> p j", j=128)),
                            (brow_b[:], bb.rearrange("(p j) -> p j", j=128)),
                            (cpool[:], cpool_c.ap()),
                        ]:
                            eng.dma_start(out=dst, in_=src).then_inc(qd.h, 16)
                    blk.sync(_sync)
                    qd.n = qd0 + 16 * 16
                    LOADS0 = qd.n

                    dv0 = dv.n
                    pe0 = pe.n

                    def _vector(eng):
                        eng.wait_ge(qd.h, LOADS0)
                        dv.inc(nc.vector.tensor_copy(out=actf[:], in_=act_i[:]))
                        eng.wait_ge(pe.h, pe0 + 1)
                        dv.inc(nc.vector.tensor_scalar(out=idxf[:], in0=psi[:],
                                                       scalar1=modR_s[:], scalar2=None,
                                                       op0=ALU.add))
                        eng.wait_ge(dv.h, dv0 + 2)
                        dv.inc(nc.vector.tensor_copy(out=idx_i[:], in_=idxf[:]))
                    blk.vector(_vector)
                    ACTF, IDXI = dv0 + 1, dv0 + 3

                    def _tensor(eng):
                        eng.wait_ge(dv.h, ACTF)
                        pe.inc(nc.tensor.matmul(psi[:], rrep8_s[:], actf[:],
                                                start=True, stop=True))
                    blk.tensor(_tensor)

                    def _gpsimd(eng):
                        for i in range(min(2, HC // CH1)):
                            xS.issue(nc.gpsimd.dma_start(
                                out=LNx[:, i * CH1 * S_own:(i + 1) * CH1 * S_own],
                                in_=xv[:, i * CH1:(i + 1) * CH1, :]), i)
                        eng.wait_ge(dv.h, IDXI)
                        qg.inc(nc.gpsimd.indirect_dma_start(
                            out=Acat[:], out_offset=None, in_=loraA.ap(),
                            in_offset=bass.IndirectOffsetOnAxis(ap=idx_i[:], axis=0)), 16)
                        qg.inc(nc.gpsimd.indirect_dma_start(
                            out=Bcat_f[:], out_offset=None, in_=loraBrt.ap(),
                            in_offset=bass.IndirectOffsetOnAxis(ap=idx_i[:], axis=0)), 16)
                        eng.wait_ge(qd.h, LOADS0)
                        qg.inc(nc.gpsimd.indirect_dma_start(
                            out=Wr_g[:], out_offset=None, in_=routerW.ap(),
                            in_offset=bass.IndirectOffsetOnAxis(ap=act_i[:], axis=0)), 16)
                        qg.inc(nc.gpsimd.indirect_dma_start(
                            out=rb_g[:], out_offset=None, in_=routerb.ap(),
                            in_offset=bass.IndirectOffsetOnAxis(ap=act_i[:], axis=0)), 16)
                        qg.inc(nc.gpsimd.indirect_dma_start(
                            out=prior_g[:], out_offset=None, in_=prior.ap(),
                            in_offset=bass.IndirectOffsetOnAxis(ap=act_i[:], axis=0)), 16)
                    blk.gpsimd(_gpsimd)
                    GATH = qg.n

                jobs = [(brow_q[:, :], FC, bqkv_pp[:, :], F32),
                        (brow_o[:, :], OC, bo_pp[:, :], F32),
                        (brow_b[:, :], OC, bb_pp[:, :], F32)]
                for hcb in range(HC):
                    jobs.append((Acat[:, hcb * 128:(hcb + 1) * 128], ER,
                                 AW_T[:, hcb * AW:hcb * AW + ER], BF16))
                for hcb in range(HC):
                    jobs.append((Wr_g[:, hcb * 128:(hcb + 1) * 128], EA,
                                 AW_T[:, hcb * AW + ER:(hcb + 1) * AW], BF16))

                NJ = len(jobs)
                with nc.Block() as blk:
                    pe0 = pe.n
                    dv0 = dv.n
                    ac0 = ac.n
                    LN_AC = ac0 + 1

                    def _gpsimd(eng):
                        for i in range(2, HC // CH1):
                            xS.issue(nc.gpsimd.dma_start(
                                out=LNx[:, i * CH1 * S_own:(i + 1) * CH1 * S_own],
                                in_=xv[:, i * CH1:(i + 1) * CH1, :]), i)
                        for ci in range(2):
                            wS.issue(nc.gpsimd.dma_start(
                                out=wcol[ci % 2][:, 0:HC * 128],
                                in_=wqkv_t[col_order[ci]].rearrange(
                                    "p hc j -> p (hc j)")), ci)
                    blk.gpsimd(_gpsimd)

                    def _tensor(eng):
                        eng.wait_ge(qd.h, LOADS0)
                        for j, (src, npart, dst, dt_) in enumerate(jobs):
                            if j == 3:
                                eng.wait_ge(qg.h, GATH)
                            dstp = p0x[j % 3]
                            if j >= 3:
                                eng.wait_ge(dv.h, dv0 + j - 2)
                            pe.inc(nc.tensor.transpose(dstp[:, 0:npart], src,
                                                       eye_f[0:npart, 0:npart]))
                        for i in chunks:
                            xS.wait(eng, i)
                            eng.wait_ge(ac.h, ac0 + i + 2)
                            sl = LNx[:, i * CH1 * S_own:(i + 1) * CH1 * S_own]
                            last = None
                            for t in range(TPC):
                                nidx = t % NSTAT
                                nc.tensor.matmul(
                                    st_sum[nidx][:], ones128b[:],
                                    sl[:, t * TN:(t + 1) * TN],
                                    start=(i == 0 and t < NSTAT),
                                    stop=(i == NCHU - 1 and t >= TPC - NSTAT))
                                last = nc.tensor.matmul(
                                    st_sq[nidx][:], ones128b[:],
                                    xsq[i % 2][:, t * TN:(t + 1) * TN],
                                    start=(i == 0 and t < NSTAT),
                                    stop=(i == NCHU - 1 and t >= TPC - NSTAT))
                            pe.inc(last)
                    blk.tensor(_tensor)

                    def _scalar(eng):
                        eng.wait_ge(qd.h, LOADS0)
                        eng.wait_ge(qg.h, GATH)
                        ac.inc(nc.scalar.activation(out=prior_g[:], in_=prior_g[:],
                                                    func=AF.Ln, bias=EPS_PRIOR, scale=1.0))
                        for i in chunks:
                            xS.wait(eng, i)
                            if i >= 2:
                                eng.wait_ge(pe.h, pe0 + NJ + (i - 2) + 1)
                            sl = LNx[:, i * CH1 * S_own:(i + 1) * CH1 * S_own]
                            ac.inc(nc.scalar.activation(out=xsq[i % 2][:],
                                                        in_=sl, func=AF.Square))
                    blk.scalar(_scalar)

                    def _vector(eng):
                        for j, (src, npart, dst, dt_) in enumerate(jobs):
                            dstp = p0x[j % 3]
                            eng.wait_ge(pe.h, pe0 + j + 1)
                            dv.inc(nc.vector.tensor_copy(out=dst, in_=dstp[:, 0:npart]))
                        eng.wait_ge(qg.h, GATH)
                        dv.inc(nc.vector.tensor_copy(out=Bcat[:], in_=Bcat_f[:]))
                        eng.wait_ge(ac.h, LN_AC)
                        dv.inc(nc.vector.tensor_add(out=bias_r[:], in0=rb_g[:], in1=prior_g[:]))
                        eng.wait_ge(pe.h, pe0 + NJ + NCHU)
                        for nidx in range(NSTAT):
                            dv.inc(nc.vector.tensor_scalar(
                                out=rowA[0:1, nidx * TN:(nidx + 1) * TN],
                                in0=st_sum[nidx][:], scalar1=1.0 / H,
                                scalar2=None, op0=ALU.mult))
                            dv.inc(nc.vector.tensor_scalar(
                                out=rowB[0:1, nidx * TN:(nidx + 1) * TN],
                                in0=st_sq[nidx][:], scalar1=1.0 / H,
                                scalar2=None, op0=ALU.mult))
                    blk.vector(_vector)

            barrier()

        if True:
            with ExitStack() as SL:
                r1b = SL.enter_context(sb("r1b", [128, S_own], F32))
                with ExitStack() as SR:
                    negmr1b = SL.enter_context(sb("negmr1b", [128, S_own], F32))

                    N1L = S_own // W1L
                    with ExitStack() as S1:
                        rowC = S1.enter_context(sb("rowC", [1, S_own], F32))
                        rowD = S1.enter_context(sb("rowD", [1, S_own], F32))

                        def ln_finalize(rowA_, rowB_, rowC_, rowD_, dstR, dstM, width):
                            nslice = width // TN
                            with ExitStack() as SPb:
                                pbcR = [SPb.enter_context(ps(f"pbR_{id(rowA_)}_{i}",
                                                             [128, TN], F32))
                                        for i in range(2)]
                                pbcM = [SPb.enter_context(ps(f"pbM_{id(rowA_)}_{i}",
                                                             [128, TN], F32))
                                        for i in range(2)]
                                ac0, dv0, pe0 = ac.n, dv.n, pe.n
                                with nc.Block() as blk:
                                    def _scalar(eng):
                                        ac.inc(nc.scalar.activation(
                                            out=rowC_[0:1, 0:width], in_=rowA_[0:1, 0:width],
                                            func=AF.Square))
                                        eng.wait_ge(dv.h, dv0 + 1)
                                        ac.inc(nc.scalar.activation(
                                            out=rowC_[0:1, 0:width], in_=rowD_[0:1, 0:width],
                                            func=AF.Ln, bias=EPS_LN))
                                        eng.wait_ge(ac.h, ac0 + 2)
                                        ac.inc(nc.scalar.activation(
                                            out=rowD_[0:1, 0:width], in_=rowC_[0:1, 0:width],
                                            func=AF.Exp, scale=-0.5))

                                    def _vector(eng):
                                        eng.wait_ge(ac.h, ac0 + 1)
                                        dv.inc(nc.vector.tensor_sub(
                                            out=rowD_[0:1, 0:width], in0=rowB_[0:1, 0:width],
                                            in1=rowC_[0:1, 0:width]))
                                        for ss in range(nslice):
                                            eng.wait_ge(pe.h, pe0 + 2 * ss + 1)
                                            dv.inc(nc.vector.tensor_copy(
                                                out=dstR[:, ss * TN:(ss + 1) * TN],
                                                in_=pbcR[ss % 2][:]))
                                            eng.wait_ge(pe.h, pe0 + 2 * ss + 2)
                                            dv.inc(nc.vector.tensor_copy(
                                                out=dstM[:, ss * TN:(ss + 1) * TN],
                                                in_=pbcM[ss % 2][:]))

                                    def _tensor(eng):
                                        for ss in range(nslice):
                                            eng.wait_ge(ac.h, ac0 + 3)
                                            if ss >= 2:
                                                eng.wait_ge(dv.h, dv0 + 2 * (ss - 2) + 2)
                                            pe.inc(nc.tensor.matmul(
                                                pbcR[ss % 2][:], ones1x128[:],
                                                rowD_[0:1, ss * TN:(ss + 1) * TN],
                                                start=True, stop=True))
                                            if ss >= 2:
                                                eng.wait_ge(dv.h, dv0 + 2 * (ss - 2) + 3)
                                            pe.inc(nc.tensor.matmul(
                                                pbcM[ss % 2][:], ones1x128[:],
                                                rowA_[0:1, ss * TN:(ss + 1) * TN],
                                                start=True, stop=True))

                                    blk.scalar(_scalar)
                                    blk.vector(_vector)
                                    blk.tensor(_tensor)

                        ln_finalize(rowA, rowB, rowC, rowD, r1b, negmr1b, S_own)
                        barrier()
                        S1R.close()
                        K_fm_l[0] = SQ.enter_context(
                            sb("K_fm", [128, DC * S_all], BF16, side="right"))

                    APL = [0, 0]

                with ExitStack() as S2b:
                    kvbuf = [S2b.enter_context(sb(f"kvbuf{i}", [128, S_own], BF16,
                                                  side="right")) for i in range(2)]
                    with ExitStack() as SP:
                        pqk = [SP.enter_context(ps(f"pqk{i}", [128, TN], F32))
                               for i in range(2)]
                        cols = []
                        g = 0
                        for fc in col_order:
                            is_q = fc < DC
                            cols.append({"fc": fc, "is_q": is_q, "ncs": NCo, "g0": g})
                            g += NCo
                        kv_cols = [c for c in cols if not c["is_q"]]
                        for k, c in enumerate(kv_cols):
                            c["kv_idx"] = k
                        pe0, ac0, dvb0 = pe.n, ac.n, dv.n
                        sS.begin()

                        if NCo == 1:
                            gseq = [(ci, 0) for ci in range(len(cols))]
                        else:
                            gseq = [(0, 0), (1, 0), (0, 1), (1, 1)]
                            for ci in range(2, len(cols)):
                                for ncb in range(NCo):
                                    gseq.append((ci, ncb))
                        lastpos = {}
                        for p, (ci, ncb) in enumerate(gseq):
                            lastpos[ci] = p
                        multmark = [0] * len(gseq)

                        def pe_mark(ci):
                            return pe0 + lastpos[ci] + 1

                        with nc.Block() as blk:
                            rgroups = [[2 * i, 2 * i + 1] for i in range(N_CORES // 2)]

                            def _gpsimd(eng):
                                for ci, c in enumerate(cols):
                                    if ci >= 2:
                                        eng.wait_ge(pe.h, pe_mark(ci - 2))
                                        wS.issue(nc.gpsimd.dma_start(
                                            out=wcol[ci % 2][:, 0:HC * 128],
                                            in_=wqkv_t[c["fc"]].rearrange(
                                                "p hc j -> p (hc j)")), ci)
                                    if ci == DC + 2:
                                        for k in range(DC):
                                            sS.wait(eng, k)
                                        ccK.inc(nc.gpsimd.collective_compute(
                                            "AllGather", ALU.bypass,
                                            replica_groups=rgroups,
                                            ins=[kvK_mine.ap()], outs=[kvK_g.ap()]))
                                    if ci == min(2 * DC + 3, len(cols) - 1):
                                        for k in range(DC, 2 * DC):
                                            sS.wait(eng, k)
                                        ccV.inc(nc.gpsimd.collective_compute(
                                            "AllGather", ALU.bypass,
                                            replica_groups=rgroups,
                                            ins=[kvV_mine.ap()], outs=[kvV_g.ap()]))

                            def _vector(eng):
                                def center(ts_, h0, h1):
                                    for hcb in range(h0, h1):
                                        sl = LNx[:, hcb * S_own + ts_ * TN:
                                                 hcb * S_own + (ts_ + 1) * TN]
                                        dv.inc(nc.vector.tensor_sub(
                                            out=sl, in0=sl,
                                            in1=negmr1b[:, ts_ * TN:(ts_ + 1) * TN]))

                                def mult(p):
                                    ci, ncb = gseq[p]
                                    eng.wait_ge(pe.h, pe0 + p + 1)
                                    dv.inc(nc.vector.tensor_mul(
                                        out=pqk[p % 2][:], in0=pqk[p % 2][:],
                                        in1=r1b[:, ncb * TN:(ncb + 1) * TN]))
                                    multmark[p] = dv.n

                                center(0, 0, HC)
                                APL[0] = dv.n
                                p_start = 0
                                if NCo > 1:
                                    center(1, 0, HC // 2)
                                    mult(0)
                                    center(1, HC // 2, HC)
                                    APL[1] = dv.n
                                    mult(1)
                                    p_start = 2
                                for p in range(p_start, len(gseq)):
                                    mult(p)

                            def _tensor(eng):
                                for p, (ci, ncb) in enumerate(gseq):
                                    wS.wait(eng, ci)
                                    eng.wait_ge(dv.h, APL[ncb])
                                    if p >= 2:
                                        eng.wait_ge(ac.h, ac0 + p - 1)
                                    last = None
                                    for hcb in range(HC):
                                        last = nc.tensor.matmul(
                                            pqk[p % 2][:],
                                            wcol[ci % 2][:, hcb * 128:(hcb + 1) * 128],
                                            LNx[:, hcb * S_own + ncb * TN:
                                                hcb * S_own + (ncb + 1) * TN],
                                            start=(hcb == 0), stop=(hcb == HC - 1))
                                    pe.inc(last)

                            def _scalar(eng):
                                for p, (ci, ncb) in enumerate(gseq):
                                    c = cols[ci]
                                    eng.wait_ge(dv.h, multmark[p])
                                    if c["is_q"]:
                                        dst = Q_fm[:, c["fc"] * S_own + ncb * TN:
                                                   c["fc"] * S_own + (ncb + 1) * TN]
                                    else:
                                        k = c["kv_idx"]
                                        if ncb == 0 and k >= 2:
                                            sS.wait(eng, k - 2)
                                        dst = kvbuf[k % 2][:, ncb * TN:(ncb + 1) * TN]
                                    ac.inc(nc.scalar.activation(
                                        out=dst, in_=pqk[p % 2][:], func=AF.Identity,
                                        bias=bqkv_pp[:, c["fc"]:c["fc"] + 1], scale=1.0))

                            def _sync(eng):
                                for k, c in enumerate(kv_cols):
                                    eng.wait_ge(ac.h, ac0 + lastpos[k] + 1)
                                    if c["fc"] < 2 * DC:
                                        dst = kvK_mine[(c["fc"] - DC) * 128:
                                                       (c["fc"] - DC + 1) * 128, :]
                                    else:
                                        dst = kvV_mine[(c["fc"] - 2 * DC) * 128:
                                                       (c["fc"] - 2 * DC + 1) * 128, :]
                                    sS.issue(eng.dma_start(out=dst, in_=kvbuf[k % 2][:]),
                                             k)
                                eng.wait_ge(ccK.h, ccK.n)
                                for dc in range(DC):
                                    for r in range(2):
                                        kld.inc(eng.dma_start(
                                            out=K_fm_l[0][:, dc * S_all + r * S_own:
                                                     dc * S_all + (r + 1) * S_own],
                                            in_=kvK_g[r * DC * 128 + dc * 128:
                                                      r * DC * 128 + (dc + 1) * 128,
                                                      :]), 16)

                            blk.gpsimd(_gpsimd)
                            blk.vector(_vector)
                            blk.tensor(_tensor)
                            blk.scalar(_scalar)
                            blk.sync(_sync)
                        barrier()
            SLn.close()

            with ExitStack() as S3:
                attn_out = S3.enter_context(sb("attn_out", [128, DC * S_own], BF16))
                wcol4 = [S3.enter_context(sb(f"wcol4_{i}", [128, DC * 128], BF16))
                         for i in range(3)]
                with ExitStack() as S3t:
                    V_tm = S3t.enter_context(sb("V_tm", [128, KC * D], BF16))
                    PnT = S3t.enter_context(sb("PnT", [128, KC * S_own], BF16))
                    Pf32 = [S3t.enter_context(sb(f"Pf32_{i}", [128, S_all], F32))
                            for i in range(2)]
                    Pn = [S3t.enter_context(sb(f"Pn{i}", [128, S_all], BF16))
                          for i in range(2)]
                    vstage = [S3t.enter_context(sb(f"vstage{i}", [128, S_all], BF16))
                              for i in range(2)]
                    mx4 = S3t.enter_context(sb("mx4", [128, NCa], F32))
                    negmx = S3t.enter_context(sb("negmx", [128, 1], F32))
                    bias_q = S3t.enter_context(sb("bias_q", [128, 1], F32))
                    sums = S3t.enter_context(sb("sums", [128, NCa], F32))
                    ssum = S3t.enter_context(sb("ssum", [128, 1], F32))
                    rinv = S3t.enter_context(sb("rinv", [128, 1], F32))

                    with ExitStack() as SP:
                        CCV = ccV.n
                        KLOAD = kld.n
                        wS.begin()
                        with nc.Block() as blk:
                            def _gpsimd(eng):
                                for ci in range(2):
                                    wS.issue(nc.gpsimd.dma_start(
                                        out=wcol4[ci % 3][:],
                                        in_=wo_t[ci].rearrange("p dc j -> p (dc j)")), ci)
                            blk.gpsimd(_gpsimd)

                        BSZ = 4 if KC % 4 == 0 else 1
                        NB = KC // BSZ
                        pe0, dv0 = pe.n, dv.n
                        lS.begin()
                        SV = ExitStack()
                        SV.__enter__()
                        ptpV = [SV.enter_context(ps(f"ptpV{i}", [128, 512], BF16))
                                for i in range(4)]
                        with nc.Block() as blk:
                            def _sync(eng):
                                eng.wait_ge(ccV.h, CCV)
                                for dc in range(DC):
                                    if dc >= 2:
                                        eng.wait_ge(pe.h, pe0 + (dc - 1) * NB)
                                    for r in range(2):
                                        lS.issue(eng.dma_start(
                                            out=vstage[dc % 2][:, r * S_own:
                                                       (r + 1) * S_own],
                                            in_=kvV_g[r * DC * 128 + dc * 128:
                                                      r * DC * 128 + (dc + 1) * 128,
                                                      :]), dc * 2 + r)

                            def _tensor(eng):
                                for dc in range(DC):
                                    lS.wait(eng, dc * 2)
                                    lS.wait(eng, dc * 2 + 1)
                                    for b in range(NB):
                                        gi = dc * NB + b
                                        if gi >= 4:
                                            eng.wait_ge(dv.h, dv0 + gi - 3)
                                        last = None
                                        for t in range(BSZ):
                                            kc = b * BSZ + t
                                            last = nc.tensor.transpose(
                                                ptpV[gi % 4][:, t * 128:(t + 1) * 128],
                                                vstage[dc % 2][:, kc * 128:(kc + 1) * 128],
                                                eye_b[:, :])
                                        pe.inc(last)

                            def _vector(eng):
                                for dc in range(DC):
                                    for b in range(NB):
                                        gi = dc * NB + b
                                        eng.wait_ge(pe.h, pe0 + gi + 1)
                                        src = ptpV[gi % 4].ap().rearrange(
                                            "p (t j) -> p t j", t=4)[:, 0:BSZ, :]
                                        dst = V_tm.ap().rearrange(
                                            "p (kc dd) -> p kc dd", kc=KC)[
                                            :, b * BSZ:(b + 1) * BSZ,
                                            dc * 128:(dc + 1) * 128]
                                        dv.inc(nc.vector.tensor_copy(out=dst, in_=src))

                            blk.sync(_sync)
                            blk.tensor(_tensor)
                            blk.vector(_vector)
                        VT_DV = dv.n
                        SV.close()
                        pscore = SP.enter_context(ps("pscore", [128, S_all], F32))
                        ptp = [SP.enter_context(ps(f"ptp{i}", [128, 512], BF16))
                               for i in range(2)]
                        ppv = SP.enter_context(ps("ppv", [128, DC * 128], F32))

                        pe0, dv0, ac0 = pe.n, dv.n, ac.n
                        DVQ = 8 + NB

                        def dv_q(qt):
                            return dv0 + qt * DVQ

                        pm = {}
                        cnt = 0
                        for qt in range(QT):
                            cnt += 1
                            pm[("S", qt)] = pe0 + cnt
                            if qt >= 1:
                                cnt += 1
                                pm[("PV", qt - 1)] = pe0 + cnt
                            for b in range(NB):
                                cnt += 1
                                pm[("T", qt, b)] = pe0 + cnt
                        cnt += 1
                        pm[("PV", QT - 1)] = pe0 + cnt

                        am = {}
                        acnt = 0
                        for qt in range(QT):
                            acnt += 1
                            am[("bias", qt)] = ac0 + acnt
                            acnt += NCa
                            am[("exp", qt)] = ac0 + acnt
                            if qt >= 1:
                                acnt += 1
                                am[("pvev", qt - 1)] = ac0 + acnt
                        acnt += 1
                        am[("pvev", QT - 1)] = ac0 + acnt

                        def emit_pv(eng, qt):
                            if qt >= 1:
                                eng.wait_ge(ac.h, am[("pvev", qt - 1)])
                            if qt == 0:
                                eng.wait_ge(dv.h, VT_DV)
                            last = None
                            for kc in range(KC):
                                eng.wait_ge(dv.h, dv_q(qt) + 8 + (kc // BSZ) + 1)
                                for dc in range(DC):
                                    last = nc.tensor.matmul(
                                        ppv[:, dc * 128:(dc + 1) * 128],
                                        V_tm[:, kc * D + dc * 128:kc * D + (dc + 1) * 128],
                                        PnT[:, kc * S_own + qt * 128:
                                            kc * S_own + (qt + 1) * 128],
                                        start=(kc == 0 and dc % 4 == 0),
                                        stop=(kc == KC - 1 and
                                              (dc % 4 == 3 or dc == DC - 1)))
                            pe.inc(last)

                        with nc.Block() as blk:
                            def _tensor(eng):
                                for qt in range(QT):
                                    eng.wait_ge(kld.h, KLOAD)
                                    if qt > 0:
                                        eng.wait_ge(ac.h, am[("exp", qt - 1)])
                                    last = None
                                    for ncb in range(NCa):
                                        for dc in range(DC):
                                            last = nc.tensor.matmul(
                                                pscore[:, ncb * TN:(ncb + 1) * TN],
                                                Q_fm[:, dc * S_own + qt * 128:
                                                     dc * S_own + (qt + 1) * 128],
                                                K_fm_l[0][:, dc * S_all + ncb * TN:
                                                     dc * S_all + (ncb + 1) * TN],
                                                start=(dc == 0), stop=(dc == DC - 1))
                                    pe.inc(last)
                                    if qt >= 1:
                                        emit_pv(eng, qt - 1)
                                    eng.wait_ge(dv.h, dv_q(qt) + 8)
                                    for b in range(NB):
                                        gi = qt * NB + b
                                        if gi >= 2:
                                            pq, pb = (gi - 2) // NB, (gi - 2) % NB
                                            eng.wait_ge(dv.h, dv_q(pq) + 8 + pb + 1)
                                        else:
                                            eng.wait_ge(dv.h, VT_DV)
                                        last = None
                                        for t in range(BSZ):
                                            kc = b * BSZ + t
                                            last = nc.tensor.transpose(
                                                ptp[gi % 2][:, t * 128:(t + 1) * 128],
                                                Pn[qt % 2][:, kc * 128:(kc + 1) * 128],
                                                eye_b[:, :])
                                        pe.inc(last)
                                emit_pv(eng, QT - 1)

                            def _vector(eng):
                                for qt in range(QT):
                                    eng.wait_ge(pe.h, pm[("S", qt)])
                                    if qt >= 2:
                                        eng.wait_ge(pe.h, pm[("T", qt - 2, NB - 1)])
                                    if qt >= 1:
                                        eng.wait_ge(ac.h, am[("bias", qt - 1)])
                                    for ncb in range(NCa):
                                        dv.inc(nc.vector.reduce_max(
                                            out=mx4[:, ncb:ncb + 1],
                                            in_=pscore[:, ncb * TN:(ncb + 1) * TN],
                                            axis=AX.X, negate=True))
                                    eng.wait_ge(dv.h, dv_q(qt) + NCa)
                                    dv.inc(nc.vector.tensor_reduce(
                                        out=negmx[:], in_=mx4[:], axis=AX.X, op=ALU.min))
                                    while dv.n < dv_q(qt) + 5:
                                        dv.inc(nc.vector.engine_nop())
                                    eng.wait_ge(ac.h, am[("exp", qt)])
                                    dv.inc(nc.vector.reduce_sum(out=ssum[:], in_=sums[:],
                                                                axis=AX.X))
                                    eng.wait_ge(dv.h, dv_q(qt) + 6)
                                    dv.inc(nc.vector.reciprocal(out=rinv[:], in_=ssum[:]))
                                    eng.wait_ge(dv.h, dv_q(qt) + 7)
                                    dv.inc(nc.vector.tensor_scalar(
                                        out=Pn[qt % 2][:], in0=Pf32[qt % 2][:],
                                        scalar1=rinv[:], scalar2=None, op0=ALU.mult))
                                    for b in range(NB):
                                        eng.wait_ge(pe.h, pm[("T", qt, b)])
                                        src = ptp[(qt * NB + b) % 2].ap().rearrange(
                                            "p (t j) -> p t j", t=4)[:, 0:BSZ, :]
                                        dst = PnT.ap().rearrange(
                                            "p (kc q) -> p kc q", kc=KC)[
                                            :, b * BSZ:(b + 1) * BSZ,
                                            qt * 128:(qt + 1) * 128]
                                        dv.inc(nc.vector.tensor_copy(out=dst, in_=src))

                            def _scalar(eng):
                                for qt in range(QT):
                                    eng.wait_ge(dv.h, dv_q(qt) + 5)
                                    ac.inc(nc.scalar.mul(out=bias_q[:], in_=negmx[:],
                                                         mul=d.scale))
                                    if qt >= 2:
                                        eng.wait_ge(dv.h, dv_q(qt - 2) + 8)
                                    if qt >= 1:
                                        eng.wait_ge(dv.h, dv_q(qt - 1) + 6)
                                    eng.wait_ge(ac.h, am[("bias", qt)])
                                    for ncb in range(NCa):
                                        ac.inc(nc.scalar.activation(
                                            out=Pf32[qt % 2][:, ncb * TN:(ncb + 1) * TN],
                                            in_=pscore[:, ncb * TN:(ncb + 1) * TN],
                                            func=AF.Exp, bias=bias_q[:], scale=d.scale,
                                            accum_out=sums[:, ncb:ncb + 1]))
                                    if qt >= 1:
                                        eng.wait_ge(pe.h, pm[("PV", qt - 1)])
                                        dst = attn_out.ap().rearrange(
                                            "p (dc s) -> p dc s", dc=DC)[
                                            :, :, (qt - 1) * 128:qt * 128]
                                        ac.inc(nc.scalar.copy(
                                            out=dst,
                                            in_=ppv.ap().rearrange(
                                                "p (dc j) -> p dc j", dc=DC)))
                                eng.wait_ge(pe.h, pm[("PV", QT - 1)])
                                dst = attn_out.ap().rearrange(
                                    "p (dc s) -> p dc s", dc=DC)[
                                    :, :, (QT - 1) * 128:QT * 128]
                                ac.inc(nc.scalar.copy(
                                    out=dst,
                                    in_=ppv.ap().rearrange("p (dc j) -> p dc j", dc=DC)))

                            blk.tensor(_tensor)
                            blk.vector(_vector)
                            blk.scalar(_scalar)
                            assert pe.n == pe0 + cnt, (pe.n, pe0, cnt)
                            assert ac.n == ac0 + acnt, (ac.n, ac0, acnt)
                            assert dv.n == dv0 + QT * DVQ
                        barrier()

                with ExitStack() as S4:
                    xo = [S4.enter_context(sb(f"xo{i}", [128, S_own], F32)) for i in range(6)]
                    x2t = [S4.enter_context(sb(f"x2t{i}", [128, S_own], F32)) for i in range(4)]
                    sq4 = [S4.enter_context(sb(f"sq4_{i}", [128, S_own], BF16)) for i in range(4)]
                    x2b = [S4.enter_context(sb(f"x2b_{i}", [128, S_own], BF16)) for i in range(4)]
                    rowA2 = S4.enter_context(sb("rowA2", [1, S_own], F32))
                    rowB2 = S4.enter_context(sb("rowB2", [1, S_own], F32))
                    rowC2 = S4.enter_context(sb("rowC2", [1, S_own], F32))
                    rowD2 = S4.enter_context(sb("rowD2", [1, S_own], F32))

                    with ExitStack() as SP:
                        po4 = [SP.enter_context(ps(f"po4_{i}", [128, TN], F32))
                               for i in range(3)]
                        st2_s = [SP.enter_context(ps(f"st2s{i}", [1, TN], F32))
                                 for i in range(NCo)]
                        st2_q = [SP.enter_context(ps(f"st2q{i}", [1, TN], F32))
                                 for i in range(NCo)]

                        groups = [(fc, tc) for fc in range(OC) for tc in range(NCo)]
                        NG4 = len(groups)
                        pe0, dv0, ac0 = pe.n, dv.n, ac.n
                        lS.begin()
                        sS.begin()

                        def op_mark(g):
                            return pe0 + (g + 1 if g < 2 else 2 * g - 1)

                        def stats_mark(h):
                            if h < NG4 - 2:
                                return pe0 + 2 * h + 4
                            return pe0 + 2 * NG4 - 2 + (h - (NG4 - 2)) + 1

                        with nc.Block() as blk:
                            def _gpsimd(eng):
                                for fc in range(2, OC):
                                    if fc >= 3:
                                        eng.wait_ge(pe.h,
                                                    op_mark((fc - 3) * NCo + NCo - 1))
                                    wS.issue(nc.gpsimd.dma_start(
                                        out=wcol4[fc % 3][:],
                                        in_=wo_t[fc].rearrange("p dc j -> p (dc j)")), fc)

                            def _sync(eng):
                                for fc in range(OC):
                                    if fc >= 6:
                                        eng.wait_ge(dv.h,
                                                    dv0 + 2 * ((fc - 6) * NCo + NCo - 1) + 1)
                                    lS.issue(eng.dma_start(
                                        out=xo[fc % 6][:],
                                        in_=x_fm[fc * 128:(fc + 1) * 128, :]), fc)

                            def _tensor(eng):
                                for g, (fc, tc) in enumerate(groups):
                                    if tc == 0:
                                        wS.wait(eng, fc)
                                    if g >= 3:
                                        eng.wait_ge(dv.h, dv0 + 2 * (g - 3) + 1)
                                    last = None
                                    for dc in range(DC):
                                        last = nc.tensor.matmul(
                                            po4[g % 3][:],
                                            wcol4[fc % 3][:, dc * 128:(dc + 1) * 128],
                                            attn_out[:, dc * S_own + tc * TN:
                                                     dc * S_own + (tc + 1) * TN],
                                            start=(dc == 0), stop=(dc == DC - 1))
                                    pe.inc(last)
                                    if g >= 2:
                                        h = g - 2
                                        hfc, htc = groups[h]
                                        eng.wait_ge(ac.h, ac0 + h + 1)
                                        eng.wait_ge(dv.h, dv0 + 2 * h + 2)
                                        nc.tensor.matmul(
                                            st2_s[htc][:], ones128b[:],
                                            x2b[hfc % 4][:, htc * TN:(htc + 1) * TN],
                                            start=(hfc == 0), stop=(hfc == OC - 1))
                                        last = nc.tensor.matmul(
                                            st2_q[htc][:], ones128b[:],
                                            sq4[hfc % 4][:, htc * TN:(htc + 1) * TN],
                                            start=(hfc == 0), stop=(hfc == OC - 1))
                                        pe.inc(last)
                                for h in range(NG4 - 2, NG4):
                                    hfc, htc = groups[h]
                                    eng.wait_ge(ac.h, ac0 + h + 1)
                                    eng.wait_ge(dv.h, dv0 + 2 * h + 2)
                                    nc.tensor.matmul(
                                        st2_s[htc][:], ones128b[:],
                                        x2b[hfc % 4][:, htc * TN:(htc + 1) * TN],
                                        start=(hfc == 0), stop=(hfc == OC - 1))
                                    last = nc.tensor.matmul(
                                        st2_q[htc][:], ones128b[:],
                                        sq4[hfc % 4][:, htc * TN:(htc + 1) * TN],
                                        start=(hfc == 0), stop=(hfc == OC - 1))
                                    pe.inc(last)

                            def _vector(eng):
                                for g, (fc, tc) in enumerate(groups):
                                    eng.wait_ge(pe.h, op_mark(g))
                                    if tc == 0:
                                        lS.wait(eng, fc)
                                    if tc == 0 and fc >= 4:
                                        hlast = (fc - 4) * NCo + NCo - 1
                                        eng.wait_ge(ac.h, ac0 + hlast + 1)
                                        sS.wait(eng, fc - 4)
                                        eng.wait_ge(pe.h, stats_mark(hlast))
                                    dv.inc(nc.vector.scalar_tensor_tensor(
                                        out=x2t[fc % 4][:, tc * TN:(tc + 1) * TN],
                                        in0=po4[g % 3][:], scalar=bo_pp[:, fc:fc + 1],
                                        in1=xo[fc % 6][:, tc * TN:(tc + 1) * TN],
                                        op0=ALU.add, op1=ALU.add))
                                    eng.wait_ge(dv.h, dv0 + 2 * g + 1)
                                    dv.inc(nc.vector.tensor_copy(
                                        out=x2b[fc % 4][:, tc * TN:(tc + 1) * TN],
                                        in_=x2t[fc % 4][:, tc * TN:(tc + 1) * TN]))

                            def _scalar(eng):
                                for g, (fc, tc) in enumerate(groups):
                                    eng.wait_ge(dv.h, dv0 + 2 * g + 1)
                                    if tc == 0 and fc >= 4:
                                        eng.wait_ge(pe.h,
                                                    stats_mark((fc - 4) * NCo + NCo - 1))
                                    ac.inc(nc.scalar.activation(
                                        out=sq4[fc % 4][:, tc * TN:(tc + 1) * TN],
                                        in_=x2t[fc % 4][:, tc * TN:(tc + 1) * TN],
                                        func=AF.Square))
                                    if tc == NCo - 1:
                                        sS.issue(nc.scalar.dma_start(
                                            out=x2_spill[fc], in_=x2t[fc % 4][:]), fc)

                            blk.gpsimd(_gpsimd)
                            blk.sync(_sync)
                            blk.tensor(_tensor)
                            blk.vector(_vector)
                            blk.scalar(_scalar)
                        P4_PE = pe.n


                        with nc.Block() as blk:
                            def _vector(eng):
                                eng.wait_ge(pe.h, P4_PE)
                                for t in range(NCo):
                                    dv.inc(nc.vector.tensor_scalar(
                                        out=rowA2[0:1, t * TN:(t + 1) * TN],
                                        in0=st2_s[t][:], scalar1=1.0 / H, scalar2=None,
                                        op0=ALU.mult))
                                    dv.inc(nc.vector.tensor_scalar(
                                        out=rowB2[0:1, t * TN:(t + 1) * TN],
                                        in0=st2_q[t][:], scalar1=1.0 / H, scalar2=None,
                                        op0=ALU.mult))
                            blk.vector(_vector)

                    ln_finalize(rowA2, rowB2, rowC2, rowD2, r2b, negmr2b, S_own)
                    barrier()
        SQ.close()

        with ExitStack() as S6:
            x2n = S6.enter_context(sb("x2n", [128, HC * S_own], BF16))
            wcol8 = [S6.enter_context(sb(f"wcol8_{i}", [128, HC * 128], BF16))
                     for i in range(2)]
            z_sb = S6.enter_context(sb("z_sb", [ER, S_own], F32))
            zs = S6.enter_context(sb("zs", [ER, S_own], BF16))
            a_r = S6.enter_context(sb("a_r", [EA, S_own], F32))
            e_r = S6.enter_context(sb("e_r", [EA, S_own], F32))
            t_r = S6.enter_context(sb("t_r", [EA, S_own], F32))
            alpha = S6.enter_context(sb("alpha", [EA, S_own], F32))
            mrow = S6.enter_context(sb("mrow", [1, S_own], F32))
            rrow = S6.enter_context(sb("rrow", [1, S_own], F32))

            with ExitStack() as S6a:
                CH6 = 2 if HC % 2 == 0 else 1
                x2s = [S6a.enter_context(sb(f"x2s{i}", [128, CH6 * S_own], F32))
                       for i in range(3)]
                with ExitStack() as SPz:
                    paw = SPz.enter_context(ps("paw", [AW, S_own], F32))
                    dv0, pe0 = dv.n, pe.n
                    NCH6 = HC // CH6
                    lS.begin()
                    wS.begin()
                    xS.begin()
                    x2v = x2_spill.ap().rearrange("h p t -> p h t")
                    with nc.Block() as blk:
                        def _gpsimd(eng):
                            for ci in range(2):
                                wS.issue(nc.gpsimd.dma_start(
                                    out=wcol8[ci % 2][:, 0:HC * 128],
                                    in_=wb_t[ci].rearrange("p hc j -> p (hc j)")), ci)
                            for i in range(NCH6):
                                if i % 2 == 1:
                                    if i >= 3:
                                        eng.wait_ge(dv.h, dv0 + CH6 * (i - 3) + CH6)
                                    xS.issue(nc.gpsimd.dma_start(
                                        out=x2s[i % 3][:],
                                        in_=x2v[:, i * CH6:(i + 1) * CH6, :]), i)
                        blk.gpsimd(_gpsimd)

                        def _sync(eng):
                            for i in range(NCH6):
                                if i % 2 == 0:
                                    if i >= 3:
                                        eng.wait_ge(dv.h, dv0 + CH6 * (i - 3) + CH6)
                                    lS.issue(eng.dma_start(
                                        out=x2s[i % 3][:],
                                        in_=x2v[:, i * CH6:(i + 1) * CH6, :]), i)

                        def _vector(eng):
                            for i in range(NCH6):
                                if i % 2 == 0:
                                    lS.wait(eng, i)
                                else:
                                    xS.wait(eng, i)
                                for c in range(CH6):
                                    hcb = i * CH6 + c
                                    dv.inc(nc.vector.tensor_sub(
                                        out=x2n[:, hcb * S_own:(hcb + 1) * S_own],
                                        in0=x2s[i % 3][:, c * S_own:(c + 1) * S_own],
                                        in1=negmr2b[:]))

                        def _tensor(eng):
                            for hcb in range(HC):
                                eng.wait_ge(dv.h, dv0 + hcb + 1)
                                last = None
                                for t in range(NCo):
                                    last = nc.tensor.matmul(
                                        paw[:, t * TN:(t + 1) * TN],
                                        AW_T[:, hcb * AW:(hcb + 1) * AW],
                                        x2n[:, hcb * S_own + t * TN:
                                            hcb * S_own + (t + 1) * TN],
                                        start=(hcb == 0), stop=(hcb == HC - 1))
                                pe.inc(last)

                        blk.sync(_sync)
                        blk.vector(_vector)
                        blk.tensor(_tensor)
                    P6_PE = pe.n

                    with nc.Block() as blk:
                        def _vector(eng):
                            eng.wait_ge(pe.h, P6_PE)
                            dv.inc(nc.vector.tensor_copy(out=z_sb[:],
                                                          in_=paw[0:ER, :]))
                        blk.vector(_vector)
                    ZEV = dv.n

                    with nc.Block() as blk:
                        def _vector(eng):
                            eng.wait_ge(pe.h, P6_PE)
                            dv.inc(nc.vector.tensor_mul(out=paw[ER:AW, :],
                                                        in0=paw[ER:AW, :],
                                                        in1=r2b[0:EA, :]))
                        blk.vector(_vector)
                        LSC = dv.n

                        def _scalar(eng):
                            eng.wait_ge(dv.h, LSC)
                            ac.inc(nc.scalar.activation(out=a_r[:], in_=paw[ER:AW, :],
                                                        func=AF.Identity, bias=bias_r[:],
                                                        scale=1.0))
                        blk.scalar(_scalar)
                    AEV = ac.n

                LSC_DV = dv.n
                with ExitStack() as S8:
                    x2r = [S8.enter_context(sb(f"x2r{i}", [128, S_own], F32))
                           for i in range(2)]
                    out_t = [S8.enter_context(sb(f"out_t{i}", [128, S_own], F32))
                             for i in range(2)]
                    with ExitStack() as SP:
                        pys = [SP.enter_context(ps(f"py{i}", [128, TN], F32))
                               for i in range(4)]
                        pr1 = SP.enter_context(ps("pr1", [1, S_own], F32))
                        pr4 = SP.enter_context(ps("pr4", [ER, S_own], F32))

                        pe0, dvr0, ac0 = pe.n, dv.n, ac.n
                        dv8 = dvr0 + 5
                        NHOC = min(2, OC)
                        NGH = NHOC * NCo
                        lS.begin()
                        sS.begin()
                        lfence = [(x, x.n) for x in l01]
                        sfence = [(x, x.n) for x in s01]

                        NFILL = 9

                        def m_lora(g):
                            if g < NGH:
                                return pe0 + NFILL + g + 1
                            return pe0 + NFILL + NGH + (g - NGH) + 1

                        def emit_w_part(eng, oc, t, tile):
                            last = None
                            for hcb in range(HC):
                                last = nc.tensor.matmul(
                                    tile[:],
                                    wcol8[oc % 2][:, hcb * 128:(hcb + 1) * 128],
                                    x2n[:, hcb * S_own + t * TN:
                                        hcb * S_own + (t + 1) * TN],
                                    start=(hcb == 0), stop=False)
                            return last

                        with nc.Block() as blk:
                            def _tensor(eng):
                                eng.wait_ge(dv.h, LSC_DV)
                                eng.wait_ge(ac.h, AEV)
                                heads = [(oc, t) for oc in range(NHOC)
                                         for t in range(NCo)]
                                hi = 0

                                def fill(eng):
                                    nonlocal hi
                                    if hi < len(heads):
                                        oc, t = heads[hi]
                                        if t == 0:
                                            wS.wait(eng, oc)
                                        pe.inc(emit_w_part(eng, oc, t, pys[hi]))
                                    else:
                                        pe.inc(nc.tensor.matmul(
                                            pys[3][0:1, 0:1], ones128b[0:1, 0:1],
                                            ones128b[0:1, 0:1],
                                            start=True, stop=True))
                                    hi += 1

                                fill(eng)
                                for t in range(NCo):
                                    last = nc.tensor.matmul(
                                        pr1[0:1, t * TN:(t + 1) * TN], quart4[:],
                                        a_r[:, t * TN:(t + 1) * TN],
                                        start=True, stop=True)
                                pe.inc(last)
                                fill(eng)
                                eng.wait_ge(dv.h, dvr0 + 1)
                                for t in range(NCo):
                                    last = nc.tensor.matmul(
                                        pr4[0:EA, t * TN:(t + 1) * TN], ones1x4[:],
                                        mrow[0:1, t * TN:(t + 1) * TN],
                                        start=True, stop=True)
                                pe.inc(last)
                                fill(eng)
                                eng.wait_ge(ac.h, ac0 + 1)
                                for t in range(NCo):
                                    last = nc.tensor.matmul(
                                        pr1[0:1, t * TN:(t + 1) * TN], ones4[:],
                                        e_r[:, t * TN:(t + 1) * TN],
                                        start=True, stop=True)
                                pe.inc(last)
                                fill(eng)
                                eng.wait_ge(dv.h, dvr0 + 3)
                                for t in range(NCo):
                                    last = nc.tensor.matmul(
                                        pr4[0:EA, t * TN:(t + 1) * TN], ones1x4[:],
                                        rrow[0:1, t * TN:(t + 1) * TN],
                                        start=True, stop=True)
                                pe.inc(last)
                                eng.wait_ge(dv.h, dvr0 + 4)
                                for t in range(NCo):
                                    last = nc.tensor.matmul(
                                        pr4[0:ER, t * TN:(t + 1) * TN], r2m_s[:],
                                        alpha[:, t * TN:(t + 1) * TN],
                                        start=True, stop=True)
                                pe.inc(last)
                                eng.wait_ge(dv.h, dvr0 + 5)
                                for g in range(NGH):
                                    oc, t = heads[g]
                                    pe.inc(nc.tensor.matmul(
                                        pys[g][:], Bcat[:, oc * 128:(oc + 1) * 128],
                                        zs[:, t * TN:(t + 1) * TN],
                                        start=False, stop=True))
                                for oc in range(NHOC, OC):
                                    wS.wait(eng, oc)
                                    for t in range(NCo):
                                        g = oc * NCo + t
                                        if g >= 4:
                                            eng.wait_ge(dv.h, dv8 + 2 * (g - 4) + 2)
                                        for hcb in range(HC):
                                            nc.tensor.matmul(
                                                pys[g % 4][:],
                                                wcol8[oc % 2][:, hcb * 128:(hcb + 1) * 128],
                                                x2n[:, hcb * S_own + t * TN:
                                                    hcb * S_own + (t + 1) * TN],
                                                start=(hcb == 0), stop=False)
                                        last = nc.tensor.matmul(
                                            pys[g % 4][:], Bcat[:, oc * 128:(oc + 1) * 128],
                                            zs[:, t * TN:(t + 1) * TN],
                                            start=False, stop=True)
                                        pe.inc(last)

                            def _vector(eng):
                                eng.wait_ge(pe.h, pe0 + 2)
                                dv.inc(nc.vector.tensor_copy(out=mrow[:],
                                                             in_=pr1[0:1, :]))
                                eng.wait_ge(pe.h, pe0 + 4)
                                dv.inc(nc.vector.tensor_sub(out=t_r[:], in0=a_r[:],
                                                            in1=pr4[0:EA, :]))
                                eng.wait_ge(pe.h, pe0 + 6)
                                eng.wait_ge(dv.h, dvr0 + 2)
                                dv.inc(nc.vector.reciprocal(out=rrow[:],
                                                            in_=pr1[0:1, :]))
                                eng.wait_ge(pe.h, pe0 + 8)
                                dv.inc(nc.vector.tensor_mul(out=alpha[:], in0=e_r[:],
                                                            in1=pr4[0:EA, :]))
                                eng.wait_ge(pe.h, pe0 + 9)
                                eng.wait_ge(dv.h, ZEV)
                                dv.inc(nc.vector.tensor_mul(out=zs[:], in0=z_sb[:],
                                                            in1=pr4[0:ER, :]))
                                for oc in range(OC):
                                    for t in range(NCo):
                                        g = oc * NCo + t
                                        eng.wait_ge(pe.h, m_lora(g))
                                        if t == 0:
                                            lS.wait(eng, oc)
                                        if oc >= 2 and t == 0:
                                            sS.wait(eng, oc - 2)
                                        dv.inc(nc.vector.tensor_mul(
                                            out=pys[g % 4][:], in0=pys[g % 4][:],
                                            in1=r2b[:, t * TN:(t + 1) * TN]))
                                        eng.wait_ge(dv.h, dv8 + 2 * g + 1)
                                        dv.inc(nc.vector.scalar_tensor_tensor(
                                            out=out_t[oc % 2][:, t * TN:(t + 1) * TN],
                                            in0=pys[g % 4][:],
                                            scalar=bb_pp[:, oc:oc + 1],
                                            in1=x2r[oc % 2][:, t * TN:(t + 1) * TN],
                                            op0=ALU.add, op1=ALU.add))

                            def _scalar(eng):
                                eng.wait_ge(dv.h, dvr0 + 2)
                                ac.inc(nc.scalar.activation(out=e_r[:], in_=t_r[:],
                                                            func=AF.Exp))

                            def _sync(eng):
                                for (x, n) in lfence:
                                    eng.wait_ge(x.h, n)
                                for (x, n) in sfence:
                                    eng.wait_ge(x.h, n)
                                for oc in range(OC):
                                    if oc >= 2:
                                        eng.wait_ge(dv.h,
                                                    dv8 + 2 * ((oc - 2) * NCo + NCo))
                                    lS.issue(eng.dma_start(out=x2r[oc % 2][:],
                                                           in_=x2_spill[oc]), oc)
                                    eng.wait_ge(dv.h, dv8 + 2 * (oc * NCo + NCo))
                                    sS.issue(eng.dma_start(
                                        out=out[oc * 128:(oc + 1) * 128, :],
                                        in_=out_t[oc % 2][:]), oc)

                            def _gpsimd(eng):
                                for oc in range(2, OC):
                                    if oc == 2:
                                        eng.wait_ge(pe.h, pe0 + 3)
                                    elif oc == 3:
                                        eng.wait_ge(pe.h, pe0 + 7)
                                    else:
                                        eng.wait_ge(pe.h,
                                                    m_lora((oc - 2) * NCo + NCo - 1))
                                    wS.issue(nc.gpsimd.dma_start(
                                        out=wcol8[oc % 2][:, 0:HC * 128],
                                        in_=wb_t[oc].rearrange("p hc j -> p (hc j)")), oc)

                            blk.tensor(_tensor)
                            blk.vector(_vector)
                            blk.scalar(_scalar)
                            blk.sync(_sync)
                            blk.gpsimd(_gpsimd)

                    barrier()

    return nc


def prep_inputs(d: Dims, ins):
    x = np.asarray(ins["L0_attn_x"])
    Wqkv = np.asarray(ins["L0_attn_Wqkv"])
    bqkv = np.ascontiguousarray(np.asarray(ins["L0_attn_bqkv"]))
    Wo = np.asarray(ins["L0_attn_Wo"])
    bo = np.ascontiguousarray(np.asarray(ins["L0_attn_bo"]))
    Wb = np.asarray(ins["L0_ffn_W_base"])
    bb = np.ascontiguousarray(np.asarray(ins["L0_ffn_b_base"]))
    loraA = np.asarray(ins["L0_ffn_lora_A"])
    loraB = np.asarray(ins["L0_ffn_lora_B"])
    routerW = np.ascontiguousarray(np.asarray(ins["L0_ffn_router_W"]))
    routerb = np.asarray(ins["L0_ffn_router_b"])
    prior = np.asarray(ins["L0_ffn_domain_prior"])
    active = np.ascontiguousarray(np.asarray(ins["L0_ffn_active"]).astype(np.int32))

    H, S_own, S_all = d.H, d.S_own, d.S_all
    HC, DC, FC, OC = d.HC, d.DC, d.FC, d.OC

    wqkv_t = np.ascontiguousarray(Wqkv.reshape(HC, 128, FC, 128).transpose(2, 1, 0, 3))
    wo_t = np.ascontiguousarray(Wo.reshape(DC, 128, OC, 128).transpose(2, 1, 0, 3))
    wb_t = np.ascontiguousarray(Wb.reshape(OC, 128, HC, 128).transpose(0, 3, 2, 1))
    loraA_f = np.ascontiguousarray(loraA.reshape(d.E_tot * d.R, H))
    loraBrt_f = np.ascontiguousarray(loraB.transpose(0, 2, 1).reshape(d.E_tot * d.R, H))
    routerb_c = np.ascontiguousarray(routerb.reshape(d.E_tot, 1))
    prior_c = np.ascontiguousarray(prior.reshape(d.E_tot, 1))

    shared = {
        "wqkv_t": wqkv_t, "wo_t": wo_t, "wb_t": wb_t,
        "bqkv": bqkv, "bo": bo, "bb": bb,
        "loraA": loraA_f, "loraBrt": loraBrt_f,
        "routerW": routerW, "routerb": routerb_c, "prior": prior_c,
        "active": active,
    }
    in_maps = []
    halves = S_all // S_own
    n_cores = x.shape[0] * halves
    for core in range(n_cores):
        b = core // halves
        h = core % halves
        m = dict(shared)
        m["x_fm"] = np.ascontiguousarray(x[b][h * S_own:(h + 1) * S_own].T)
        in_maps.append(m)
    return in_maps


def assemble_output(d: Dims, results):
    halves = d.S_all // d.S_own
    B = len(results) // halves
    out = np.empty((B, d.S_all, d.H), dtype=np.float32)
    for core in range(len(results)):
        b = core // halves
        h = core % halves
        out[b, h * d.S_own:(h + 1) * d.S_own, :] = results[core]["out"].T
    return out


def kernel(**inputs):
    d = Dims()
    nc = build_program(d)
    in_maps = prep_inputs(d, inputs)
    res = run_bass_kernel_spmd(nc, in_maps, core_ids=list(range(N_CORES)))
    return assemble_output(d, res.results)


# version 31 (final)
# speedup vs baseline: 1.0303x; 1.0057x over previous
import numpy as np
import ml_dtypes
from contextlib import ExitStack

import concourse.bass as bass
import concourse.mybir as mybir
from concourse.bass_utils import run_bass_kernel_spmd

F32 = mybir.dt.float32
BF16 = mybir.dt.bfloat16
I32 = mybir.dt.int32
AF = mybir.ActivationFunctionType
ALU = mybir.AluOpType
AX = mybir.AxisListType

EPS_LN = 1e-5
EPS_PRIOR = 1e-12
SCALING = 2.0
N_CORES = 8


class Dims:
    def __init__(self, H=4096, D=1024, S_own=1024, S_all=2048, E_tot=16, E_act=4, R=8):
        self.H, self.D, self.S_own, self.S_all = H, D, S_own, S_all
        self.E_tot, self.E_act, self.R = E_tot, E_act, R
        self.ER = E_act * R
        self.HC = H // 128
        self.DC = D // 128
        self.FC = 3 * self.DC
        self.OC = self.HC
        self.TN = min(512, S_own)
        self.NCo = S_own // self.TN
        self.NCa = S_all // self.TN
        self.W1 = min(1024, S_all)
        self.N1 = S_all // self.W1
        self.QT = S_own // 128
        self.KC = S_all // 128
        self.scale = 1.0 / float(np.sqrt(float(D)))


class Sem:
    def __init__(self, nc, st, name):
        self.h = st.enter_context(nc.semaphore(name))
        self.n = 0

    def inc(self, inst, k=1):
        inst.then_inc(self.h, k)
        self.n += k
        return self.n


class DStream:

    def __init__(self, sems):
        self.sems = sems
        self.base = [s.n for s in sems]

    def begin(self):
        self.base = [s.n for s in self.sems]

    def issue(self, inst, i):
        s = self.sems[i % len(self.sems)]
        inst.then_inc(s.h, 16)
        s.n += 16

    def wait(self, eng, i):
        nb = len(self.sems)
        s = self.sems[i % nb]
        eng.wait_ge(s.h, self.base[i % nb] + 16 * (i // nb + 1))


def build_program(d: Dims):
    nc = bass.Bass("TRN2", target_bir_lowering=False)
    H, D, S_own, S_all = d.H, d.D, d.S_own, d.S_all
    HC, DC, FC, OC = d.HC, d.DC, d.FC, d.OC
    TN, NCo, NCa = d.TN, d.NCo, d.NCa
    QT, KC, ER = d.QT, d.KC, d.ER
    EA = d.E_act

    x_fm = nc.dram_tensor("x_fm", [H, S_own], F32, kind="ExternalInput")
    wqkv_t = nc.dram_tensor("wqkv_t", [FC, 128, HC, 128], F32, kind="ExternalInput")
    wo_t = nc.dram_tensor("wo_t", [OC, 128, DC, 128], F32, kind="ExternalInput")
    wb_t = nc.dram_tensor("wb_t", [OC, 128, HC, 128], F32, kind="ExternalInput")
    bqkv = nc.dram_tensor("bqkv", [3 * D], F32, kind="ExternalInput")
    bo = nc.dram_tensor("bo", [H], F32, kind="ExternalInput")
    bb = nc.dram_tensor("bb", [H], F32, kind="ExternalInput")
    loraA = nc.dram_tensor("loraA", [d.E_tot * d.R, H], F32, kind="ExternalInput")
    loraBrt = nc.dram_tensor("loraBrt", [d.E_tot * d.R, H], F32, kind="ExternalInput")
    routerW = nc.dram_tensor("routerW", [d.E_tot, H], F32, kind="ExternalInput")
    routerb = nc.dram_tensor("routerb", [d.E_tot, 1], F32, kind="ExternalInput")
    prior = nc.dram_tensor("prior", [d.E_tot, 1], F32, kind="ExternalInput")
    active = nc.dram_tensor("active", [EA], I32, kind="ExternalInput")
    out = nc.dram_tensor("out", [H, S_own], F32, kind="ExternalOutput")

    kvK_mine = nc.dram_tensor("kvK_mine", [DC * 128, S_own], BF16, kind="Internal")
    kvV_mine = nc.dram_tensor("kvV_mine", [DC * 128, S_own], BF16, kind="Internal")
    kvK_g = nc.dram_tensor("kvK_g", [2 * DC * 128, S_own], BF16, kind="Internal")
    kvV_g = nc.dram_tensor("kvV_g", [2 * DC * 128, S_own], BF16, kind="Internal")
    x2_spill = nc.dram_tensor("x2_spill", [HC, 128, S_own], F32, kind="Internal")

    eye_bf16_c = nc.inline_tensor(np.asarray(np.eye(128, dtype=ml_dtypes.bfloat16)), name="c_eye_b")
    eye_f32_c = nc.inline_tensor(np.eye(128, dtype=np.float32), name="c_eye_f")
    ones128_b_c = nc.inline_tensor(np.ones((128, 1), dtype=ml_dtypes.bfloat16), name="c_ones128b")
    ones_sq_c = nc.inline_tensor(np.ones((128, 128), dtype=ml_dtypes.bfloat16), name="c_ones_sq")
    ones_1x128_c = nc.inline_tensor(np.ones((1, 128), dtype=np.float32), name="c_ones1x128")
    quarter4_c = nc.inline_tensor(np.full((EA, 1), 1.0 / EA, dtype=np.float32), name="c_quart")
    ones4_c = nc.inline_tensor(np.ones((EA, 1), dtype=np.float32), name="c_ones4")
    ones_1x4_c = nc.inline_tensor(np.ones((1, EA), dtype=np.float32), name="c_ones1x4")
    rrep8 = np.zeros((EA, ER), dtype=np.float32)
    r2m = np.zeros((EA, ER), dtype=np.float32)
    for e in range(EA):
        rrep8[e, e * d.R:(e + 1) * d.R] = float(d.R)
        r2m[e, e * d.R:(e + 1) * d.R] = SCALING
    rrep8_c = nc.inline_tensor(rrep8, name="c_rrep8")
    r2m_c = nc.inline_tensor(r2m, name="c_r2m")
    modR_c = nc.inline_tensor((np.arange(ER, dtype=np.float32) % d.R).reshape(ER, 1), name="c_modR")
    cpool_c = nc.inline_tensor(np.tile(np.array([[EPS_LN, EPS_PRIOR]], dtype=np.float32), (128, 1)), name="c_cpool")

    G = ExitStack()
    with G:
        qd = Sem(nc, G, "qd")
        qg = Sem(nc, G, "qg")
        kld = Sem(nc, G, "kld")
        ccK = Sem(nc, G, "ccK")
        ccV = Sem(nc, G, "ccV")
        w01 = [Sem(nc, G, f"w{i}") for i in range(3)]
        l01 = [Sem(nc, G, f"l{i}") for i in range(6)]
        s01 = [Sem(nc, G, f"s{i}") for i in range(4)]
        x01 = [Sem(nc, G, f"x{i}") for i in range(8)]
        wS = DStream(w01)
        lS = DStream(l01)
        sS = DStream(s01)
        xS = DStream(x01)
        pe = Sem(nc, G, "pe")
        dv = Sem(nc, G, "dv")
        ac = Sem(nc, G, "ac")
        pp = Sem(nc, G, "pp")
        bar = Sem(nc, G, "bar")

        def barrier():
            with nc.Block() as blk:
                tgt = bar.n + 5

                def mk(fences):
                    def f(eng):
                        for (s, n) in fences:
                            eng.wait_ge(s.h, n)
                        eng.sem_inc(bar.h, 1)
                        eng.wait_ge(bar.h, tgt)
                    return f
                blk.sync(mk([(qd, qd.n), (kld, kld.n)] +
                            [(x, x.n) for x in l01] + [(x, x.n) for x in s01]))
                blk.tensor(mk([(pe, pe.n)]))
                blk.vector(mk([(dv, dv.n)]))
                blk.scalar(mk([(ac, ac.n)]))
                blk.gpsimd(mk([(qg, qg.n), (pp, pp.n)] +
                               [(x, x.n) for x in w01]))
            bar.n += 5

        sb = lambda *a, **k: nc.sbuf_tensor(*a, **k)
        ps = lambda *a, **k: nc.psum_tensor(*a, **k)

        AW = ER + EA
        AW_T = G.enter_context(sb("AW_T", [128, HC * AW], BF16))
        Bcat = G.enter_context(sb("Bcat", [ER, H], BF16))
        bqkv_pp = G.enter_context(sb("bqkv_pp", [128, FC], F32))
        bo_pp = G.enter_context(sb("bo_pp", [128, OC], F32))
        bb_pp = G.enter_context(sb("bb_pp", [128, OC], F32))
        eye_b = G.enter_context(sb("eye_b", [128, 128], BF16))
        eye_f = G.enter_context(sb("eye_f", [128, 128], F32))
        ones128b = G.enter_context(sb("ones128b", [128, 1], BF16))
        ones_sq = G.enter_context(sb("ones_sq", [128, 128], BF16))
        ones1x128 = G.enter_context(sb("ones1x128", [1, 128], F32))
        quart4 = G.enter_context(sb("quart4", [EA, 1], F32))
        ones4 = G.enter_context(sb("ones4", [EA, 1], F32))
        ones1x4 = G.enter_context(sb("ones1x4", [1, EA], F32))
        rrep8_s = G.enter_context(sb("rrep8_s", [EA, ER], F32))
        r2m_s = G.enter_context(sb("r2m_s", [EA, ER], F32))
        modR_s = G.enter_context(sb("modR_s", [ER, 1], F32))
        act_i = G.enter_context(sb("act_i", [EA, 1], I32))
        actf = G.enter_context(sb("actf", [EA, 1], F32))
        idxf = G.enter_context(sb("idxf", [ER, 1], F32))
        idx_i = G.enter_context(sb("idx_i", [ER, 1], I32))
        rb_g = G.enter_context(sb("rb_g", [EA, 1], F32))
        prior_g = G.enter_context(sb("prior_g", [EA, 1], F32))
        bias_r = G.enter_context(sb("bias_r", [EA, 1], F32))
        cpool = G.enter_context(sb("cpool", [128, 2], F32))
        nc.const_aps.register(F32, EPS_LN, cpool[:, 0:1])
        nc.const_aps.register(F32, EPS_PRIOR, cpool[:, 1:2])
        r2b = G.enter_context(sb("r2b", [128, S_own], F32))
        negmr2b = G.enter_context(sb("negmr2b", [128, S_own], F32))

        SQ = ExitStack()
        SQ.__enter__()
        Q_fm = SQ.enter_context(sb("Q_fm", [128, DC * S_own], BF16, side="right"))
        wcol = [SQ.enter_context(sb(f"wcol{i}", [128, HC * 128], BF16, side="right"))
                for i in range(2)]
        W1L = min(1024, S_own)
        if HC % 2 == 0 and S_own == W1L:
            CH1 = 2
        elif HC % 4 == 0 and S_own == W1L:
            CH1 = 4
        else:
            CH1 = 1
        wS.begin()
        xS.begin()
        col_order = list(range(DC, 3 * DC)) + list(range(DC))
        xv = x_fm.ap().rearrange("(c p) t -> p c t", p=128)
        K_fm_l = [None]
        SKF = ExitStack()
        SKF.__enter__()
        SLn = ExitStack()
        SLn.__enter__()
        LNx = SLn.enter_context(sb("LNx", [128, HC * S_own], BF16))
        S1R = ExitStack()
        S1R.__enter__()
        xsq = [S1R.enter_context(sb(f"xsq{i}", [128, CH1 * W1L], BF16, side="right"))
               for i in range(2)]
        rowA = S1R.enter_context(sb("rowA", [1, S_own], F32, side="right"))
        rowB = S1R.enter_context(sb("rowB", [1, S_own], F32, side="right"))
        NSTAT = S_own // TN
        chunks = list(range(HC // CH1))
        NCHU = len(chunks)
        TPC = (CH1 * W1L) // TN

        with ExitStack() as S0:
            Acat = S0.enter_context(sb("Acat", [ER, H], F32))
            Bcat_f = S0.enter_context(sb("Bcat_f", [ER, H], F32))
            Wr_g = S0.enter_context(sb("Wr_g", [EA, H], F32))
            brow_q = S0.enter_context(sb("brow_q", [FC, 128], F32))
            brow_o = S0.enter_context(sb("brow_o", [OC, 128], F32))
            brow_b = S0.enter_context(sb("brow_b", [OC, 128], F32))
            with ExitStack() as SP:
                p0x = [SP.enter_context(ps(f"p0x{i}", [128, 64], F32)) for i in range(3)]
                psi = SP.enter_context(ps("psi", [ER, 1], F32))
                st_sum = [SP.enter_context(ps(f"st_s{i}", [1, TN], F32))
                          for i in range(NSTAT)]
                st_sq = [SP.enter_context(ps(f"st_q{i}", [1, TN], F32))
                         for i in range(NSTAT)]

                with nc.Block() as blk:
                    qd0 = qd.n

                    def _sync(eng):
                        for dst, src in [
                            (eye_b[:], eye_bf16_c.ap()), (eye_f[:], eye_f32_c.ap()),
                            (ones128b[:], ones128_b_c.ap()), (ones_sq[:], ones_sq_c.ap()),
                            (ones1x128[:], ones_1x128_c.ap()),
                            (quart4[:], quarter4_c.ap()), (ones4[:], ones4_c.ap()),
                            (ones1x4[:], ones_1x4_c.ap()), (rrep8_s[:], rrep8_c.ap()),
                            (r2m_s[:], r2m_c.ap()), (modR_s[:], modR_c.ap()),
                            (act_i[:], active.rearrange("(p o) -> p o", o=1)),
                            (brow_q[:], bqkv.rearrange("(p j) -> p j", j=128)),
                            (brow_o[:], bo.rearrange("(p j) -> p j", j=128)),
                            (brow_b[:], bb.rearrange("(p j) -> p j", j=128)),
                            (cpool[:], cpool_c.ap()),
                        ]:
                            eng.dma_start(out=dst, in_=src).then_inc(qd.h, 16)
                    blk.sync(_sync)
                    qd.n = qd0 + 16 * 16
                    LOADS0 = qd.n

                    dv0 = dv.n
                    pe0 = pe.n

                    def _vector(eng):
                        eng.wait_ge(qd.h, LOADS0)
                        dv.inc(nc.vector.tensor_copy(out=actf[:], in_=act_i[:]))
                        eng.wait_ge(pe.h, pe0 + 1)
                        dv.inc(nc.vector.tensor_scalar(out=idxf[:], in0=psi[:],
                                                       scalar1=modR_s[:], scalar2=None,
                                                       op0=ALU.add))
                        eng.wait_ge(dv.h, dv0 + 2)
                        dv.inc(nc.vector.tensor_copy(out=idx_i[:], in_=idxf[:]))
                    blk.vector(_vector)
                    ACTF, IDXI = dv0 + 1, dv0 + 3

                    def _tensor(eng):
                        eng.wait_ge(dv.h, ACTF)
                        pe.inc(nc.tensor.matmul(psi[:], rrep8_s[:], actf[:],
                                                start=True, stop=True))
                    blk.tensor(_tensor)

                    def _gpsimd(eng):
                        for i in range(min(2, HC // CH1)):
                            xS.issue(nc.gpsimd.dma_start(
                                out=LNx[:, i * CH1 * S_own:(i + 1) * CH1 * S_own],
                                in_=xv[:, i * CH1:(i + 1) * CH1, :]), i)
                        eng.wait_ge(dv.h, IDXI)
                        qg.inc(nc.gpsimd.indirect_dma_start(
                            out=Acat[:], out_offset=None, in_=loraA.ap(),
                            in_offset=bass.IndirectOffsetOnAxis(ap=idx_i[:], axis=0)), 16)
                        qg.inc(nc.gpsimd.indirect_dma_start(
                            out=Bcat_f[:], out_offset=None, in_=loraBrt.ap(),
                            in_offset=bass.IndirectOffsetOnAxis(ap=idx_i[:], axis=0)), 16)
                        eng.wait_ge(qd.h, LOADS0)
                        qg.inc(nc.gpsimd.indirect_dma_start(
                            out=Wr_g[:], out_offset=None, in_=routerW.ap(),
                            in_offset=bass.IndirectOffsetOnAxis(ap=act_i[:], axis=0)), 16)
                        qg.inc(nc.gpsimd.indirect_dma_start(
                            out=rb_g[:], out_offset=None, in_=routerb.ap(),
                            in_offset=bass.IndirectOffsetOnAxis(ap=act_i[:], axis=0)), 16)
                        qg.inc(nc.gpsimd.indirect_dma_start(
                            out=prior_g[:], out_offset=None, in_=prior.ap(),
                            in_offset=bass.IndirectOffsetOnAxis(ap=act_i[:], axis=0)), 16)
                    blk.gpsimd(_gpsimd)
                    GATH = qg.n

                jobs = [(brow_q[:, :], FC, bqkv_pp[:, :], F32),
                        (brow_o[:, :], OC, bo_pp[:, :], F32),
                        (brow_b[:, :], OC, bb_pp[:, :], F32)]
                for hcb in range(HC):
                    jobs.append((Acat[:, hcb * 128:(hcb + 1) * 128], ER,
                                 AW_T[:, hcb * AW:hcb * AW + ER], BF16))
                for hcb in range(HC):
                    jobs.append((Wr_g[:, hcb * 128:(hcb + 1) * 128], EA,
                                 AW_T[:, hcb * AW + ER:(hcb + 1) * AW], BF16))

                NJ = len(jobs)
                with nc.Block() as blk:
                    pe0 = pe.n
                    dv0 = dv.n
                    ac0 = ac.n
                    LN_AC = ac0 + 1

                    def _gpsimd(eng):
                        for i in range(2, HC // CH1):
                            xS.issue(nc.gpsimd.dma_start(
                                out=LNx[:, i * CH1 * S_own:(i + 1) * CH1 * S_own],
                                in_=xv[:, i * CH1:(i + 1) * CH1, :]), i)
                        for ci in range(2):
                            wS.issue(nc.gpsimd.dma_start(
                                out=wcol[ci % 2][:, 0:HC * 128],
                                in_=wqkv_t[col_order[ci]].rearrange(
                                    "p hc j -> p (hc j)")), ci)
                    blk.gpsimd(_gpsimd)

                    def _tensor(eng):
                        eng.wait_ge(qd.h, LOADS0)
                        for j, (src, npart, dst, dt_) in enumerate(jobs):
                            if j == 3:
                                eng.wait_ge(qg.h, GATH)
                            dstp = p0x[j % 3]
                            if j >= 3:
                                eng.wait_ge(dv.h, dv0 + j - 2)
                            pe.inc(nc.tensor.transpose(dstp[:, 0:npart], src,
                                                       eye_f[0:npart, 0:npart]))
                        for i in chunks:
                            xS.wait(eng, i)
                            eng.wait_ge(ac.h, ac0 + i + 2)
                            sl = LNx[:, i * CH1 * S_own:(i + 1) * CH1 * S_own]
                            last = None
                            for t in range(TPC):
                                nidx = t % NSTAT
                                nc.tensor.matmul(
                                    st_sum[nidx][:], ones128b[:],
                                    sl[:, t * TN:(t + 1) * TN],
                                    start=(i == 0 and t < NSTAT),
                                    stop=(i == NCHU - 1 and t >= TPC - NSTAT))
                                last = nc.tensor.matmul(
                                    st_sq[nidx][:], ones128b[:],
                                    xsq[i % 2][:, t * TN:(t + 1) * TN],
                                    start=(i == 0 and t < NSTAT),
                                    stop=(i == NCHU - 1 and t >= TPC - NSTAT))
                            pe.inc(last)
                    blk.tensor(_tensor)

                    def _scalar(eng):
                        eng.wait_ge(qd.h, LOADS0)
                        eng.wait_ge(qg.h, GATH)
                        ac.inc(nc.scalar.activation(out=prior_g[:], in_=prior_g[:],
                                                    func=AF.Ln, bias=EPS_PRIOR, scale=1.0))
                        for i in chunks:
                            xS.wait(eng, i)
                            if i >= 2:
                                eng.wait_ge(pe.h, pe0 + NJ + (i - 2) + 1)
                            sl = LNx[:, i * CH1 * S_own:(i + 1) * CH1 * S_own]
                            ac.inc(nc.scalar.activation(out=xsq[i % 2][:],
                                                        in_=sl, func=AF.Square))
                    blk.scalar(_scalar)

                    def _vector(eng):
                        for j, (src, npart, dst, dt_) in enumerate(jobs):
                            dstp = p0x[j % 3]
                            eng.wait_ge(pe.h, pe0 + j + 1)
                            dv.inc(nc.vector.tensor_copy(out=dst, in_=dstp[:, 0:npart]))
                        eng.wait_ge(qg.h, GATH)
                        dv.inc(nc.vector.tensor_copy(out=Bcat[:], in_=Bcat_f[:]))
                        eng.wait_ge(ac.h, LN_AC)
                        dv.inc(nc.vector.tensor_add(out=bias_r[:], in0=rb_g[:], in1=prior_g[:]))
                        eng.wait_ge(pe.h, pe0 + NJ + NCHU)
                        for nidx in range(NSTAT):
                            dv.inc(nc.vector.tensor_scalar(
                                out=rowA[0:1, nidx * TN:(nidx + 1) * TN],
                                in0=st_sum[nidx][:], scalar1=1.0 / H,
                                scalar2=None, op0=ALU.mult))
                            dv.inc(nc.vector.tensor_scalar(
                                out=rowB[0:1, nidx * TN:(nidx + 1) * TN],
                                in0=st_sq[nidx][:], scalar1=1.0 / H,
                                scalar2=None, op0=ALU.mult))
                    blk.vector(_vector)

            barrier()

        if True:
            with ExitStack() as SL:
                r1b = SL.enter_context(sb("r1b", [128, S_own], F32))
                with ExitStack() as SR:
                    negmr1b = SL.enter_context(sb("negmr1b", [128, S_own], F32))

                    N1L = S_own // W1L
                    with ExitStack() as S1:
                        rowC = S1.enter_context(sb("rowC", [1, S_own], F32))
                        rowD = S1.enter_context(sb("rowD", [1, S_own], F32))

                        def ln_finalize(rowA_, rowB_, rowC_, rowD_, dstR, dstM, width,
                                        pre_g=None):
                            nslice = width // TN
                            with ExitStack() as SPb:
                                pbcR = [SPb.enter_context(ps(f"pbR_{id(rowA_)}_{i}",
                                                             [128, TN], F32))
                                        for i in range(2)]
                                pbcM = [SPb.enter_context(ps(f"pbM_{id(rowA_)}_{i}",
                                                             [128, TN], F32))
                                        for i in range(2)]
                                ac0, dv0, pe0 = ac.n, dv.n, pe.n
                                with nc.Block() as blk:
                                    def _scalar(eng):
                                        ac.inc(nc.scalar.activation(
                                            out=rowC_[0:1, 0:width], in_=rowA_[0:1, 0:width],
                                            func=AF.Square))
                                        eng.wait_ge(dv.h, dv0 + 1)
                                        ac.inc(nc.scalar.activation(
                                            out=rowC_[0:1, 0:width], in_=rowD_[0:1, 0:width],
                                            func=AF.Ln, bias=EPS_LN))
                                        eng.wait_ge(ac.h, ac0 + 2)
                                        ac.inc(nc.scalar.activation(
                                            out=rowD_[0:1, 0:width], in_=rowC_[0:1, 0:width],
                                            func=AF.Exp, scale=-0.5))

                                    def _vector(eng):
                                        eng.wait_ge(ac.h, ac0 + 1)
                                        dv.inc(nc.vector.tensor_sub(
                                            out=rowD_[0:1, 0:width], in0=rowB_[0:1, 0:width],
                                            in1=rowC_[0:1, 0:width]))
                                        for ss in range(nslice):
                                            eng.wait_ge(pe.h, pe0 + 2 * ss + 1)
                                            dv.inc(nc.vector.tensor_copy(
                                                out=dstR[:, ss * TN:(ss + 1) * TN],
                                                in_=pbcR[ss % 2][:]))
                                            eng.wait_ge(pe.h, pe0 + 2 * ss + 2)
                                            dv.inc(nc.vector.tensor_copy(
                                                out=dstM[:, ss * TN:(ss + 1) * TN],
                                                in_=pbcM[ss % 2][:]))

                                    def _tensor(eng):
                                        for ss in range(nslice):
                                            eng.wait_ge(ac.h, ac0 + 3)
                                            if ss >= 2:
                                                eng.wait_ge(dv.h, dv0 + 2 * (ss - 2) + 2)
                                            pe.inc(nc.tensor.matmul(
                                                pbcR[ss % 2][:], ones1x128[:],
                                                rowD_[0:1, ss * TN:(ss + 1) * TN],
                                                start=True, stop=True))
                                            if ss >= 2:
                                                eng.wait_ge(dv.h, dv0 + 2 * (ss - 2) + 3)
                                            pe.inc(nc.tensor.matmul(
                                                pbcM[ss % 2][:], ones1x128[:],
                                                rowA_[0:1, ss * TN:(ss + 1) * TN],
                                                start=True, stop=True))

                                    blk.scalar(_scalar)
                                    blk.vector(_vector)
                                    blk.tensor(_tensor)
                                    if pre_g is not None:
                                        blk.gpsimd(pre_g)

                        ln_finalize(rowA, rowB, rowC, rowD, r1b, negmr1b, S_own)
                        barrier()
                        S1R.close()
                        K_fm_l[0] = SKF.enter_context(
                            sb("K_fm", [128, DC * S_all], BF16, side="right"))

                    APL = [0, 0]

                with ExitStack() as S2b:
                    kvbuf = [S2b.enter_context(sb(f"kvbuf{i}", [128, S_own], BF16,
                                                  side="right")) for i in range(2)]
                    with ExitStack() as SP:
                        pqk = [SP.enter_context(ps(f"pqk{i}", [128, TN], F32))
                               for i in range(2)]
                        cols = []
                        g = 0
                        for fc in col_order:
                            is_q = fc < DC
                            cols.append({"fc": fc, "is_q": is_q, "ncs": NCo, "g0": g})
                            g += NCo
                        kv_cols = [c for c in cols if not c["is_q"]]
                        for k, c in enumerate(kv_cols):
                            c["kv_idx"] = k
                        pe0, ac0, dvb0 = pe.n, ac.n, dv.n
                        sS.begin()

                        if NCo == 1:
                            gseq = [(ci, 0) for ci in range(len(cols))]
                        else:
                            gseq = [(0, 0), (1, 0), (0, 1), (1, 1)]
                            for ci in range(2, len(cols)):
                                for ncb in range(NCo):
                                    gseq.append((ci, ncb))
                        lastpos = {}
                        for p, (ci, ncb) in enumerate(gseq):
                            lastpos[ci] = p
                        multmark = [0] * len(gseq)

                        def pe_mark(ci):
                            return pe0 + lastpos[ci] + 1

                        with nc.Block() as blk:
                            rgroups = [[2 * i, 2 * i + 1] for i in range(N_CORES // 2)]

                            def _gpsimd(eng):
                                for ci, c in enumerate(cols):
                                    if ci >= 2:
                                        eng.wait_ge(pe.h, pe_mark(ci - 2))
                                        wS.issue(nc.gpsimd.dma_start(
                                            out=wcol[ci % 2][:, 0:HC * 128],
                                            in_=wqkv_t[c["fc"]].rearrange(
                                                "p hc j -> p (hc j)")), ci)
                                    if ci == DC + 2:
                                        for k in range(DC):
                                            sS.wait(eng, k)
                                        ccK.inc(nc.gpsimd.collective_compute(
                                            "AllGather", ALU.bypass,
                                            replica_groups=rgroups,
                                            ins=[kvK_mine.ap()], outs=[kvK_g.ap()]))
                                    if ci == min(2 * DC + 3, len(cols) - 1):
                                        for k in range(DC, 2 * DC):
                                            sS.wait(eng, k)
                                        ccV.inc(nc.gpsimd.collective_compute(
                                            "AllGather", ALU.bypass,
                                            replica_groups=rgroups,
                                            ins=[kvV_mine.ap()], outs=[kvV_g.ap()]))

                            def _vector(eng):
                                def center(ts_, h0, h1):
                                    for hcb in range(h0, h1):
                                        sl = LNx[:, hcb * S_own + ts_ * TN:
                                                 hcb * S_own + (ts_ + 1) * TN]
                                        dv.inc(nc.vector.tensor_sub(
                                            out=sl, in0=sl,
                                            in1=negmr1b[:, ts_ * TN:(ts_ + 1) * TN]))

                                def mult(p):
                                    ci, ncb = gseq[p]
                                    eng.wait_ge(pe.h, pe0 + p + 1)
                                    dv.inc(nc.vector.tensor_mul(
                                        out=pqk[p % 2][:], in0=pqk[p % 2][:],
                                        in1=r1b[:, ncb * TN:(ncb + 1) * TN]))
                                    multmark[p] = dv.n

                                center(0, 0, HC)
                                APL[0] = dv.n
                                p_start = 0
                                if NCo > 1:
                                    center(1, 0, HC // 2)
                                    mult(0)
                                    center(1, HC // 2, HC)
                                    APL[1] = dv.n
                                    mult(1)
                                    p_start = 2
                                for p in range(p_start, len(gseq)):
                                    mult(p)

                            def _tensor(eng):
                                for p, (ci, ncb) in enumerate(gseq):
                                    wS.wait(eng, ci)
                                    eng.wait_ge(dv.h, APL[ncb])
                                    if p >= 2:
                                        eng.wait_ge(ac.h, ac0 + p - 1)
                                    last = None
                                    for hcb in range(HC):
                                        last = nc.tensor.matmul(
                                            pqk[p % 2][:],
                                            wcol[ci % 2][:, hcb * 128:(hcb + 1) * 128],
                                            LNx[:, hcb * S_own + ncb * TN:
                                                hcb * S_own + (ncb + 1) * TN],
                                            start=(hcb == 0), stop=(hcb == HC - 1))
                                    pe.inc(last)

                            def _scalar(eng):
                                for p, (ci, ncb) in enumerate(gseq):
                                    c = cols[ci]
                                    eng.wait_ge(dv.h, multmark[p])
                                    if c["is_q"]:
                                        dst = Q_fm[:, c["fc"] * S_own + ncb * TN:
                                                   c["fc"] * S_own + (ncb + 1) * TN]
                                    else:
                                        k = c["kv_idx"]
                                        if ncb == 0 and k >= 2:
                                            sS.wait(eng, k - 2)
                                        dst = kvbuf[k % 2][:, ncb * TN:(ncb + 1) * TN]
                                    ac.inc(nc.scalar.activation(
                                        out=dst, in_=pqk[p % 2][:], func=AF.Identity,
                                        bias=bqkv_pp[:, c["fc"]:c["fc"] + 1], scale=1.0))

                            def _sync(eng):
                                for k, c in enumerate(kv_cols):
                                    eng.wait_ge(ac.h, ac0 + lastpos[k] + 1)
                                    if c["fc"] < 2 * DC:
                                        dst = kvK_mine[(c["fc"] - DC) * 128:
                                                       (c["fc"] - DC + 1) * 128, :]
                                    else:
                                        dst = kvV_mine[(c["fc"] - 2 * DC) * 128:
                                                       (c["fc"] - 2 * DC + 1) * 128, :]
                                    sS.issue(eng.dma_start(out=dst, in_=kvbuf[k % 2][:]),
                                             k)
                                eng.wait_ge(ccK.h, ccK.n)
                                for dc in range(DC):
                                    for r in range(2):
                                        kld.inc(eng.dma_start(
                                            out=K_fm_l[0][:, dc * S_all + r * S_own:
                                                     dc * S_all + (r + 1) * S_own],
                                            in_=kvK_g[r * DC * 128 + dc * 128:
                                                      r * DC * 128 + (dc + 1) * 128,
                                                      :]), 16)

                            blk.gpsimd(_gpsimd)
                            blk.vector(_vector)
                            blk.tensor(_tensor)
                            blk.scalar(_scalar)
                            blk.sync(_sync)
                        barrier()
            SLn.close()

            with ExitStack() as S3:
                attn_out = S3.enter_context(sb("attn_out", [128, DC * S_own], BF16))
                wcol4 = [S3.enter_context(sb(f"wcol4_{i}", [128, DC * 128], BF16))
                         for i in range(3)]
                with ExitStack() as S3t:
                    V_tm = S3t.enter_context(sb("V_tm", [128, KC * D], BF16))
                    PnT = S3t.enter_context(sb("PnT", [128, KC * S_own], BF16))
                    Pf32 = [S3t.enter_context(sb(f"Pf32_{i}", [128, S_all], F32))
                            for i in range(2)]
                    Pn = [S3t.enter_context(sb(f"Pn{i}", [128, S_all], BF16))
                          for i in range(2)]
                    vstage = [S3t.enter_context(sb(f"vstage{i}", [128, S_all], BF16))
                              for i in range(2)]
                    mx4 = S3t.enter_context(sb("mx4", [128, NCa], F32))
                    negmx = S3t.enter_context(sb("negmx", [128, 1], F32))
                    bias_q = S3t.enter_context(sb("bias_q", [128, 1], F32))
                    sums = S3t.enter_context(sb("sums", [128, NCa], F32))
                    ssum = S3t.enter_context(sb("ssum", [128, 1], F32))
                    rinv = S3t.enter_context(sb("rinv", [128, 1], F32))

                    with ExitStack() as SP:
                        CCV = ccV.n
                        KLOAD = kld.n
                        wS.begin()
                        with nc.Block() as blk:
                            def _gpsimd(eng):
                                for ci in range(2):
                                    wS.issue(nc.gpsimd.dma_start(
                                        out=wcol4[ci % 3][:],
                                        in_=wo_t[ci].rearrange("p dc j -> p (dc j)")), ci)
                            blk.gpsimd(_gpsimd)

                        BSZ = 4 if KC % 4 == 0 else 1
                        NB = KC // BSZ
                        pe0, dv0 = pe.n, dv.n
                        lS.begin()
                        SV = ExitStack()
                        SV.__enter__()
                        ptpV = [SV.enter_context(ps(f"ptpV{i}", [128, 512], BF16))
                                for i in range(4)]
                        with nc.Block() as blk:
                            def _sync(eng):
                                eng.wait_ge(ccV.h, CCV)
                                for dc in range(DC):
                                    if dc >= 2:
                                        eng.wait_ge(pe.h, pe0 + (dc - 1) * NB)
                                    for r in range(2):
                                        lS.issue(eng.dma_start(
                                            out=vstage[dc % 2][:, r * S_own:
                                                       (r + 1) * S_own],
                                            in_=kvV_g[r * DC * 128 + dc * 128:
                                                      r * DC * 128 + (dc + 1) * 128,
                                                      :]), dc * 2 + r)

                            def _tensor(eng):
                                for dc in range(DC):
                                    lS.wait(eng, dc * 2)
                                    lS.wait(eng, dc * 2 + 1)
                                    for b in range(NB):
                                        gi = dc * NB + b
                                        if gi >= 4:
                                            eng.wait_ge(dv.h, dv0 + gi - 3)
                                        last = None
                                        for t in range(BSZ):
                                            kc = b * BSZ + t
                                            last = nc.tensor.transpose(
                                                ptpV[gi % 4][:, t * 128:(t + 1) * 128],
                                                vstage[dc % 2][:, kc * 128:(kc + 1) * 128],
                                                eye_b[:, :])
                                        pe.inc(last)

                            def _vector(eng):
                                for dc in range(DC):
                                    for b in range(NB):
                                        gi = dc * NB + b
                                        eng.wait_ge(pe.h, pe0 + gi + 1)
                                        src = ptpV[gi % 4].ap().rearrange(
                                            "p (t j) -> p t j", t=4)[:, 0:BSZ, :]
                                        dst = V_tm.ap().rearrange(
                                            "p (kc dd) -> p kc dd", kc=KC)[
                                            :, b * BSZ:(b + 1) * BSZ,
                                            dc * 128:(dc + 1) * 128]
                                        dv.inc(nc.vector.tensor_copy(out=dst, in_=src))

                            blk.sync(_sync)
                            blk.tensor(_tensor)
                            blk.vector(_vector)
                        VT_DV = dv.n
                        SV.close()
                        pscore = SP.enter_context(ps("pscore", [128, S_all], F32))
                        ptp = [SP.enter_context(ps(f"ptp{i}", [128, 512], BF16))
                               for i in range(2)]
                        ppv = SP.enter_context(ps("ppv", [128, DC * 128], F32))

                        pe0, dv0, ac0 = pe.n, dv.n, ac.n
                        DVQ = 8 + NB

                        def dv_q(qt):
                            return dv0 + qt * DVQ

                        pm = {}
                        cnt = 0
                        for qt in range(QT):
                            cnt += 1
                            pm[("S", qt)] = pe0 + cnt
                            if qt >= 1:
                                cnt += 1
                                pm[("PV", qt - 1)] = pe0 + cnt
                            for b in range(NB):
                                cnt += 1
                                pm[("T", qt, b)] = pe0 + cnt
                        cnt += 1
                        pm[("PV", QT - 1)] = pe0 + cnt

                        am = {}
                        acnt = 0
                        for qt in range(QT):
                            acnt += 1
                            am[("bias", qt)] = ac0 + acnt
                            acnt += NCa
                            am[("exp", qt)] = ac0 + acnt
                            if qt >= 1:
                                acnt += 1
                                am[("pvev", qt - 1)] = ac0 + acnt
                        acnt += 1
                        am[("pvev", QT - 1)] = ac0 + acnt

                        def emit_pv(eng, qt):
                            if qt >= 1:
                                eng.wait_ge(ac.h, am[("pvev", qt - 1)])
                            if qt == 0:
                                eng.wait_ge(dv.h, VT_DV)
                            last = None
                            for kc in range(KC):
                                eng.wait_ge(dv.h, dv_q(qt) + 8 + (kc // BSZ) + 1)
                                for dc in range(DC):
                                    last = nc.tensor.matmul(
                                        ppv[:, dc * 128:(dc + 1) * 128],
                                        V_tm[:, kc * D + dc * 128:kc * D + (dc + 1) * 128],
                                        PnT[:, kc * S_own + qt * 128:
                                            kc * S_own + (qt + 1) * 128],
                                        start=(kc == 0 and dc % 4 == 0),
                                        stop=(kc == KC - 1 and
                                              (dc % 4 == 3 or dc == DC - 1)))
                            pe.inc(last)

                        with nc.Block() as blk:
                            def _tensor(eng):
                                for qt in range(QT):
                                    eng.wait_ge(kld.h, KLOAD)
                                    if qt > 0:
                                        eng.wait_ge(ac.h, am[("exp", qt - 1)])
                                    last = None
                                    for ncb in range(NCa):
                                        for dc in range(DC):
                                            last = nc.tensor.matmul(
                                                pscore[:, ncb * TN:(ncb + 1) * TN],
                                                Q_fm[:, dc * S_own + qt * 128:
                                                     dc * S_own + (qt + 1) * 128],
                                                K_fm_l[0][:, dc * S_all + ncb * TN:
                                                     dc * S_all + (ncb + 1) * TN],
                                                start=(dc == 0), stop=(dc == DC - 1))
                                    pe.inc(last)
                                    if qt >= 1:
                                        emit_pv(eng, qt - 1)
                                    eng.wait_ge(dv.h, dv_q(qt) + 8)
                                    for b in range(NB):
                                        gi = qt * NB + b
                                        if gi >= 2:
                                            pq, pb = (gi - 2) // NB, (gi - 2) % NB
                                            eng.wait_ge(dv.h, dv_q(pq) + 8 + pb + 1)
                                        else:
                                            eng.wait_ge(dv.h, VT_DV)
                                        last = None
                                        for t in range(BSZ):
                                            kc = b * BSZ + t
                                            last = nc.tensor.transpose(
                                                ptp[gi % 2][:, t * 128:(t + 1) * 128],
                                                Pn[qt % 2][:, kc * 128:(kc + 1) * 128],
                                                eye_b[:, :])
                                        pe.inc(last)
                                emit_pv(eng, QT - 1)

                            def _vector(eng):
                                for qt in range(QT):
                                    eng.wait_ge(pe.h, pm[("S", qt)])
                                    if qt >= 2:
                                        eng.wait_ge(pe.h, pm[("T", qt - 2, NB - 1)])
                                    if qt >= 1:
                                        eng.wait_ge(ac.h, am[("bias", qt - 1)])
                                    for ncb in range(NCa):
                                        dv.inc(nc.vector.reduce_max(
                                            out=mx4[:, ncb:ncb + 1],
                                            in_=pscore[:, ncb * TN:(ncb + 1) * TN],
                                            axis=AX.X, negate=True))
                                    eng.wait_ge(dv.h, dv_q(qt) + NCa)
                                    dv.inc(nc.vector.tensor_reduce(
                                        out=negmx[:], in_=mx4[:], axis=AX.X, op=ALU.min))
                                    while dv.n < dv_q(qt) + 5:
                                        dv.inc(nc.vector.engine_nop())
                                    eng.wait_ge(ac.h, am[("exp", qt)])
                                    dv.inc(nc.vector.reduce_sum(out=ssum[:], in_=sums[:],
                                                                axis=AX.X))
                                    eng.wait_ge(dv.h, dv_q(qt) + 6)
                                    dv.inc(nc.vector.reciprocal(out=rinv[:], in_=ssum[:]))
                                    eng.wait_ge(dv.h, dv_q(qt) + 7)
                                    dv.inc(nc.vector.tensor_scalar(
                                        out=Pn[qt % 2][:], in0=Pf32[qt % 2][:],
                                        scalar1=rinv[:], scalar2=None, op0=ALU.mult))
                                    for b in range(NB):
                                        eng.wait_ge(pe.h, pm[("T", qt, b)])
                                        src = ptp[(qt * NB + b) % 2].ap().rearrange(
                                            "p (t j) -> p t j", t=4)[:, 0:BSZ, :]
                                        dst = PnT.ap().rearrange(
                                            "p (kc q) -> p kc q", kc=KC)[
                                            :, b * BSZ:(b + 1) * BSZ,
                                            qt * 128:(qt + 1) * 128]
                                        dv.inc(nc.vector.tensor_copy(out=dst, in_=src))

                            def _scalar(eng):
                                for qt in range(QT):
                                    eng.wait_ge(dv.h, dv_q(qt) + 5)
                                    ac.inc(nc.scalar.mul(out=bias_q[:], in_=negmx[:],
                                                         mul=d.scale))
                                    if qt >= 2:
                                        eng.wait_ge(dv.h, dv_q(qt - 2) + 8)
                                    if qt >= 1:
                                        eng.wait_ge(dv.h, dv_q(qt - 1) + 6)
                                    eng.wait_ge(ac.h, am[("bias", qt)])
                                    for ncb in range(NCa):
                                        ac.inc(nc.scalar.activation(
                                            out=Pf32[qt % 2][:, ncb * TN:(ncb + 1) * TN],
                                            in_=pscore[:, ncb * TN:(ncb + 1) * TN],
                                            func=AF.Exp, bias=bias_q[:], scale=d.scale,
                                            accum_out=sums[:, ncb:ncb + 1]))
                                    if qt >= 1:
                                        eng.wait_ge(pe.h, pm[("PV", qt - 1)])
                                        dst = attn_out.ap().rearrange(
                                            "p (dc s) -> p dc s", dc=DC)[
                                            :, :, (qt - 1) * 128:qt * 128]
                                        ac.inc(nc.scalar.copy(
                                            out=dst,
                                            in_=ppv.ap().rearrange(
                                                "p (dc j) -> p dc j", dc=DC)))
                                eng.wait_ge(pe.h, pm[("PV", QT - 1)])
                                dst = attn_out.ap().rearrange(
                                    "p (dc s) -> p dc s", dc=DC)[
                                    :, :, (QT - 1) * 128:QT * 128]
                                ac.inc(nc.scalar.copy(
                                    out=dst,
                                    in_=ppv.ap().rearrange("p (dc j) -> p dc j", dc=DC)))

                            blk.tensor(_tensor)
                            blk.vector(_vector)
                            blk.scalar(_scalar)
                            assert pe.n == pe0 + cnt, (pe.n, pe0, cnt)
                            assert ac.n == ac0 + acnt, (ac.n, ac0, acnt)
                            assert dv.n == dv0 + QT * DVQ
                        barrier()

                with ExitStack() as S4:
                    xo = [S4.enter_context(sb(f"xo{i}", [128, S_own], F32)) for i in range(6)]
                    x2t = [S4.enter_context(sb(f"x2t{i}", [128, S_own], F32)) for i in range(4)]
                    sq4 = [S4.enter_context(sb(f"sq4_{i}", [128, S_own], BF16)) for i in range(4)]
                    x2b = [S4.enter_context(sb(f"x2b_{i}", [128, S_own], BF16)) for i in range(4)]
                    rowA2 = S4.enter_context(sb("rowA2", [1, S_own], F32))
                    rowB2 = S4.enter_context(sb("rowB2", [1, S_own], F32))
                    rowC2 = S4.enter_context(sb("rowC2", [1, S_own], F32))
                    rowD2 = S4.enter_context(sb("rowD2", [1, S_own], F32))

                    with ExitStack() as SP:
                        po4 = [SP.enter_context(ps(f"po4_{i}", [128, TN], F32))
                               for i in range(3)]
                        st2_s = [SP.enter_context(ps(f"st2s{i}", [1, TN], F32))
                                 for i in range(NCo)]
                        st2_q = [SP.enter_context(ps(f"st2q{i}", [1, TN], F32))
                                 for i in range(NCo)]

                        groups = [(fc, tc) for fc in range(OC) for tc in range(NCo)]
                        NG4 = len(groups)
                        pe0, dv0, ac0 = pe.n, dv.n, ac.n
                        lS.begin()
                        sS.begin()

                        def op_mark(g):
                            return pe0 + (g + 1 if g < 2 else 2 * g - 1)

                        def stats_mark(h):
                            if h < NG4 - 2:
                                return pe0 + 2 * h + 4
                            return pe0 + 2 * NG4 - 2 + (h - (NG4 - 2)) + 1

                        with nc.Block() as blk:
                            def _gpsimd(eng):
                                for fc in range(2, OC):
                                    if fc >= 3:
                                        eng.wait_ge(pe.h,
                                                    op_mark((fc - 3) * NCo + NCo - 1))
                                    wS.issue(nc.gpsimd.dma_start(
                                        out=wcol4[fc % 3][:],
                                        in_=wo_t[fc].rearrange("p dc j -> p (dc j)")), fc)

                            def _sync(eng):
                                for fc in range(OC):
                                    if fc >= 6:
                                        eng.wait_ge(dv.h,
                                                    dv0 + 2 * ((fc - 6) * NCo + NCo - 1) + 1)
                                    lS.issue(eng.dma_start(
                                        out=xo[fc % 6][:],
                                        in_=x_fm[fc * 128:(fc + 1) * 128, :]), fc)

                            def _tensor(eng):
                                for g, (fc, tc) in enumerate(groups):
                                    if tc == 0:
                                        wS.wait(eng, fc)
                                    if g >= 3:
                                        eng.wait_ge(dv.h, dv0 + 2 * (g - 3) + 1)
                                    last = None
                                    for dc in range(DC):
                                        last = nc.tensor.matmul(
                                            po4[g % 3][:],
                                            wcol4[fc % 3][:, dc * 128:(dc + 1) * 128],
                                            attn_out[:, dc * S_own + tc * TN:
                                                     dc * S_own + (tc + 1) * TN],
                                            start=(dc == 0), stop=(dc == DC - 1))
                                    pe.inc(last)
                                    if g >= 2:
                                        h = g - 2
                                        hfc, htc = groups[h]
                                        eng.wait_ge(ac.h, ac0 + h + 1)
                                        eng.wait_ge(dv.h, dv0 + 2 * h + 2)
                                        nc.tensor.matmul(
                                            st2_s[htc][:], ones128b[:],
                                            x2b[hfc % 4][:, htc * TN:(htc + 1) * TN],
                                            start=(hfc == 0), stop=(hfc == OC - 1))
                                        last = nc.tensor.matmul(
                                            st2_q[htc][:], ones128b[:],
                                            sq4[hfc % 4][:, htc * TN:(htc + 1) * TN],
                                            start=(hfc == 0), stop=(hfc == OC - 1))
                                        pe.inc(last)
                                for h in range(NG4 - 2, NG4):
                                    hfc, htc = groups[h]
                                    eng.wait_ge(ac.h, ac0 + h + 1)
                                    eng.wait_ge(dv.h, dv0 + 2 * h + 2)
                                    nc.tensor.matmul(
                                        st2_s[htc][:], ones128b[:],
                                        x2b[hfc % 4][:, htc * TN:(htc + 1) * TN],
                                        start=(hfc == 0), stop=(hfc == OC - 1))
                                    last = nc.tensor.matmul(
                                        st2_q[htc][:], ones128b[:],
                                        sq4[hfc % 4][:, htc * TN:(htc + 1) * TN],
                                        start=(hfc == 0), stop=(hfc == OC - 1))
                                    pe.inc(last)

                            def _vector(eng):
                                for g, (fc, tc) in enumerate(groups):
                                    eng.wait_ge(pe.h, op_mark(g))
                                    if tc == 0:
                                        lS.wait(eng, fc)
                                    if tc == 0 and fc >= 4:
                                        hlast = (fc - 4) * NCo + NCo - 1
                                        eng.wait_ge(ac.h, ac0 + hlast + 1)
                                        sS.wait(eng, fc - 4)
                                        eng.wait_ge(pe.h, stats_mark(hlast))
                                    dv.inc(nc.vector.scalar_tensor_tensor(
                                        out=x2t[fc % 4][:, tc * TN:(tc + 1) * TN],
                                        in0=po4[g % 3][:], scalar=bo_pp[:, fc:fc + 1],
                                        in1=xo[fc % 6][:, tc * TN:(tc + 1) * TN],
                                        op0=ALU.add, op1=ALU.add))
                                    eng.wait_ge(dv.h, dv0 + 2 * g + 1)
                                    dv.inc(nc.vector.tensor_copy(
                                        out=x2b[fc % 4][:, tc * TN:(tc + 1) * TN],
                                        in_=x2t[fc % 4][:, tc * TN:(tc + 1) * TN]))

                            def _scalar(eng):
                                for g, (fc, tc) in enumerate(groups):
                                    eng.wait_ge(dv.h, dv0 + 2 * g + 1)
                                    if tc == 0 and fc >= 4:
                                        eng.wait_ge(pe.h,
                                                    stats_mark((fc - 4) * NCo + NCo - 1))
                                    ac.inc(nc.scalar.activation(
                                        out=sq4[fc % 4][:, tc * TN:(tc + 1) * TN],
                                        in_=x2t[fc % 4][:, tc * TN:(tc + 1) * TN],
                                        func=AF.Square))
                                    if tc == NCo - 1:
                                        sS.issue(nc.scalar.dma_start(
                                            out=x2_spill[fc], in_=x2t[fc % 4][:]), fc)

                            blk.gpsimd(_gpsimd)
                            blk.sync(_sync)
                            blk.tensor(_tensor)
                            blk.vector(_vector)
                            blk.scalar(_scalar)
                        P4_PE = pe.n


                        with nc.Block() as blk:
                            def _vector(eng):
                                eng.wait_ge(pe.h, P4_PE)
                                for t in range(NCo):
                                    dv.inc(nc.vector.tensor_scalar(
                                        out=rowA2[0:1, t * TN:(t + 1) * TN],
                                        in0=st2_s[t][:], scalar1=1.0 / H, scalar2=None,
                                        op0=ALU.mult))
                                    dv.inc(nc.vector.tensor_scalar(
                                        out=rowB2[0:1, t * TN:(t + 1) * TN],
                                        in0=st2_q[t][:], scalar1=1.0 / H, scalar2=None,
                                        op0=ALU.mult))
                            blk.vector(_vector)

                    wS.begin()

                    def _preg(eng):
                        eng.wait_ge(pe.h, P4_PE)
                        for ci in range(2):
                            wS.issue(nc.gpsimd.dma_start(
                                out=wcol[ci % 2][:, 0:HC * 128],
                                in_=wb_t[ci].rearrange("p hc j -> p (hc j)")), ci)
                    ln_finalize(rowA2, rowB2, rowC2, rowD2, r2b, negmr2b, S_own,
                                pre_g=_preg)
                    barrier()
        SKF.close()

        with ExitStack() as S6:
            x2n = S6.enter_context(sb("x2n", [128, HC * S_own], BF16))
            wcol8 = wcol
            z_sb = S6.enter_context(sb("z_sb", [ER, S_own], F32))
            zs = S6.enter_context(sb("zs", [ER, S_own], BF16))
            a_r = S6.enter_context(sb("a_r", [EA, S_own], F32))
            e_r = S6.enter_context(sb("e_r", [EA, S_own], F32))
            t_r = S6.enter_context(sb("t_r", [EA, S_own], F32))
            alpha = S6.enter_context(sb("alpha", [EA, S_own], F32))
            mrow = S6.enter_context(sb("mrow", [1, S_own], F32))
            rrow = S6.enter_context(sb("rrow", [1, S_own], F32))

            with ExitStack() as S6a:
                CH6 = 2 if HC % 2 == 0 else 1
                x2s = [S6a.enter_context(sb(f"x2s{i}", [128, CH6 * S_own], F32))
                       for i in range(3)]
                with ExitStack() as SPz:
                    paw = SPz.enter_context(ps("paw", [AW, S_own], F32))
                    dv0, pe0 = dv.n, pe.n
                    NCH6 = HC // CH6
                    lS.begin()
                    xS.begin()
                    x2v = x2_spill.ap().rearrange("h p t -> p h t")
                    with nc.Block() as blk:
                        def _gpsimd(eng):
                            for i in range(NCH6):
                                if i % 2 == 1:
                                    if i >= 3:
                                        eng.wait_ge(dv.h, dv0 + CH6 * (i - 3) + CH6)
                                    xS.issue(nc.gpsimd.dma_start(
                                        out=x2s[i % 3][:],
                                        in_=x2v[:, i * CH6:(i + 1) * CH6, :]), i)
                        blk.gpsimd(_gpsimd)

                        def _sync(eng):
                            for i in range(NCH6):
                                if i % 2 == 0:
                                    if i >= 3:
                                        eng.wait_ge(dv.h, dv0 + CH6 * (i - 3) + CH6)
                                    lS.issue(eng.dma_start(
                                        out=x2s[i % 3][:],
                                        in_=x2v[:, i * CH6:(i + 1) * CH6, :]), i)

                        def _vector(eng):
                            for i in range(NCH6):
                                if i % 2 == 0:
                                    lS.wait(eng, i)
                                else:
                                    xS.wait(eng, i)
                                for c in range(CH6):
                                    hcb = i * CH6 + c
                                    dv.inc(nc.vector.tensor_sub(
                                        out=x2n[:, hcb * S_own:(hcb + 1) * S_own],
                                        in0=x2s[i % 3][:, c * S_own:(c + 1) * S_own],
                                        in1=negmr2b[:]))

                        def _tensor(eng):
                            for hcb in range(HC):
                                eng.wait_ge(dv.h, dv0 + hcb + 1)
                                last = None
                                for t in range(NCo):
                                    last = nc.tensor.matmul(
                                        paw[:, t * TN:(t + 1) * TN],
                                        AW_T[:, hcb * AW:(hcb + 1) * AW],
                                        x2n[:, hcb * S_own + t * TN:
                                            hcb * S_own + (t + 1) * TN],
                                        start=(hcb == 0), stop=(hcb == HC - 1))
                                pe.inc(last)

                        blk.sync(_sync)
                        blk.vector(_vector)
                        blk.tensor(_tensor)
                    P6_PE = pe.n

                    with nc.Block() as blk:
                        def _vector(eng):
                            eng.wait_ge(pe.h, P6_PE)
                            dv.inc(nc.vector.tensor_copy(out=z_sb[:],
                                                          in_=paw[0:ER, :]))
                        blk.vector(_vector)
                    ZEV = dv.n

                    with nc.Block() as blk:
                        def _vector(eng):
                            eng.wait_ge(pe.h, P6_PE)
                            dv.inc(nc.vector.tensor_mul(out=paw[ER:AW, :],
                                                        in0=paw[ER:AW, :],
                                                        in1=r2b[0:EA, :]))
                        blk.vector(_vector)
                        LSC = dv.n

                        def _scalar(eng):
                            eng.wait_ge(dv.h, LSC)
                            ac.inc(nc.scalar.activation(out=a_r[:], in_=paw[ER:AW, :],
                                                        func=AF.Identity, bias=bias_r[:],
                                                        scale=1.0))
                        blk.scalar(_scalar)
                    AEV = ac.n

                LSC_DV = dv.n
                with ExitStack() as S8:
                    x2r = [S8.enter_context(sb(f"x2r{i}", [128, S_own], F32))
                           for i in range(2)]
                    out_t = [S8.enter_context(sb(f"out_t{i}", [128, S_own], F32))
                             for i in range(2)]
                    with ExitStack() as SP:
                        pys = [SP.enter_context(ps(f"py{i}", [128, TN], F32))
                               for i in range(4)]
                        pr1 = SP.enter_context(ps("pr1", [1, S_own], F32))
                        pr4 = SP.enter_context(ps("pr4", [ER, S_own], F32))

                        pe0, dvr0, ac0 = pe.n, dv.n, ac.n
                        dv8 = dvr0 + 5
                        NHOC = min(2, OC)
                        NGH = NHOC * NCo
                        lS.begin()
                        sS.begin()
                        lfence = [(x, x.n) for x in l01]
                        sfence = [(x, x.n) for x in s01]

                        NFILL = 9

                        def m_lora(g):
                            if g < NGH:
                                return pe0 + NFILL + g + 1
                            return pe0 + NFILL + NGH + (g - NGH) + 1

                        def emit_w_part(eng, oc, t, tile):
                            last = None
                            for hcb in range(HC):
                                last = nc.tensor.matmul(
                                    tile[:],
                                    wcol8[oc % 2][:, hcb * 128:(hcb + 1) * 128],
                                    x2n[:, hcb * S_own + t * TN:
                                        hcb * S_own + (t + 1) * TN],
                                    start=(hcb == 0), stop=False)
                            return last

                        with nc.Block() as blk:
                            def _tensor(eng):
                                eng.wait_ge(dv.h, LSC_DV)
                                eng.wait_ge(ac.h, AEV)
                                heads = [(oc, t) for oc in range(NHOC)
                                         for t in range(NCo)]
                                hi = 0

                                def fill(eng):
                                    nonlocal hi
                                    if hi < len(heads):
                                        oc, t = heads[hi]
                                        if t == 0:
                                            wS.wait(eng, oc)
                                        pe.inc(emit_w_part(eng, oc, t, pys[hi]))
                                    else:
                                        pe.inc(nc.tensor.matmul(
                                            pys[3][0:1, 0:1], ones128b[0:1, 0:1],
                                            ones128b[0:1, 0:1],
                                            start=True, stop=True))
                                    hi += 1

                                fill(eng)
                                for t in range(NCo):
                                    last = nc.tensor.matmul(
                                        pr1[0:1, t * TN:(t + 1) * TN], quart4[:],
                                        a_r[:, t * TN:(t + 1) * TN],
                                        start=True, stop=True)
                                pe.inc(last)
                                fill(eng)
                                eng.wait_ge(dv.h, dvr0 + 1)
                                for t in range(NCo):
                                    last = nc.tensor.matmul(
                                        pr4[0:EA, t * TN:(t + 1) * TN], ones1x4[:],
                                        mrow[0:1, t * TN:(t + 1) * TN],
                                        start=True, stop=True)
                                pe.inc(last)
                                fill(eng)
                                eng.wait_ge(ac.h, ac0 + 1)
                                for t in range(NCo):
                                    last = nc.tensor.matmul(
                                        pr1[0:1, t * TN:(t + 1) * TN], ones4[:],
                                        e_r[:, t * TN:(t + 1) * TN],
                                        start=True, stop=True)
                                pe.inc(last)
                                fill(eng)
                                eng.wait_ge(dv.h, dvr0 + 3)
                                for t in range(NCo):
                                    last = nc.tensor.matmul(
                                        pr4[0:EA, t * TN:(t + 1) * TN], ones1x4[:],
                                        rrow[0:1, t * TN:(t + 1) * TN],
                                        start=True, stop=True)
                                pe.inc(last)
                                eng.wait_ge(dv.h, dvr0 + 4)
                                for t in range(NCo):
                                    last = nc.tensor.matmul(
                                        pr4[0:ER, t * TN:(t + 1) * TN], r2m_s[:],
                                        alpha[:, t * TN:(t + 1) * TN],
                                        start=True, stop=True)
                                pe.inc(last)
                                eng.wait_ge(dv.h, dvr0 + 5)
                                for g in range(NGH):
                                    oc, t = heads[g]
                                    pe.inc(nc.tensor.matmul(
                                        pys[g][:], Bcat[:, oc * 128:(oc + 1) * 128],
                                        zs[:, t * TN:(t + 1) * TN],
                                        start=False, stop=True))
                                for oc in range(NHOC, OC):
                                    wS.wait(eng, oc)
                                    for t in range(NCo):
                                        g = oc * NCo + t
                                        if g >= 4:
                                            eng.wait_ge(dv.h, dv8 + 2 * (g - 4) + 2)
                                        for hcb in range(HC):
                                            nc.tensor.matmul(
                                                pys[g % 4][:],
                                                wcol8[oc % 2][:, hcb * 128:(hcb + 1) * 128],
                                                x2n[:, hcb * S_own + t * TN:
                                                    hcb * S_own + (t + 1) * TN],
                                                start=(hcb == 0), stop=False)
                                        last = nc.tensor.matmul(
                                            pys[g % 4][:], Bcat[:, oc * 128:(oc + 1) * 128],
                                            zs[:, t * TN:(t + 1) * TN],
                                            start=False, stop=True)
                                        pe.inc(last)

                            def _vector(eng):
                                eng.wait_ge(pe.h, pe0 + 2)
                                dv.inc(nc.vector.tensor_copy(out=mrow[:],
                                                             in_=pr1[0:1, :]))
                                eng.wait_ge(pe.h, pe0 + 4)
                                dv.inc(nc.vector.tensor_sub(out=t_r[:], in0=a_r[:],
                                                            in1=pr4[0:EA, :]))
                                eng.wait_ge(pe.h, pe0 + 6)
                                eng.wait_ge(dv.h, dvr0 + 2)
                                dv.inc(nc.vector.reciprocal(out=rrow[:],
                                                            in_=pr1[0:1, :]))
                                eng.wait_ge(pe.h, pe0 + 8)
                                dv.inc(nc.vector.tensor_mul(out=alpha[:], in0=e_r[:],
                                                            in1=pr4[0:EA, :]))
                                eng.wait_ge(pe.h, pe0 + 9)
                                eng.wait_ge(dv.h, ZEV)
                                dv.inc(nc.vector.tensor_mul(out=zs[:], in0=z_sb[:],
                                                            in1=pr4[0:ER, :]))
                                for oc in range(OC):
                                    for t in range(NCo):
                                        g = oc * NCo + t
                                        eng.wait_ge(pe.h, m_lora(g))
                                        if t == 0:
                                            lS.wait(eng, oc)
                                        if oc >= 2 and t == 0:
                                            sS.wait(eng, oc - 2)
                                        dv.inc(nc.vector.tensor_mul(
                                            out=pys[g % 4][:], in0=pys[g % 4][:],
                                            in1=r2b[:, t * TN:(t + 1) * TN]))
                                        eng.wait_ge(dv.h, dv8 + 2 * g + 1)
                                        dv.inc(nc.vector.scalar_tensor_tensor(
                                            out=out_t[oc % 2][:, t * TN:(t + 1) * TN],
                                            in0=pys[g % 4][:],
                                            scalar=bb_pp[:, oc:oc + 1],
                                            in1=x2r[oc % 2][:, t * TN:(t + 1) * TN],
                                            op0=ALU.add, op1=ALU.add))

                            def _scalar(eng):
                                eng.wait_ge(dv.h, dvr0 + 2)
                                ac.inc(nc.scalar.activation(out=e_r[:], in_=t_r[:],
                                                            func=AF.Exp))

                            def _sync(eng):
                                for (x, n) in lfence:
                                    eng.wait_ge(x.h, n)
                                for (x, n) in sfence:
                                    eng.wait_ge(x.h, n)
                                for oc in range(OC):
                                    if oc >= 2:
                                        eng.wait_ge(dv.h,
                                                    dv8 + 2 * ((oc - 2) * NCo + NCo))
                                    lS.issue(eng.dma_start(out=x2r[oc % 2][:],
                                                           in_=x2_spill[oc]), oc)
                                    eng.wait_ge(dv.h, dv8 + 2 * (oc * NCo + NCo))
                                    sS.issue(eng.dma_start(
                                        out=out[oc * 128:(oc + 1) * 128, :],
                                        in_=out_t[oc % 2][:]), oc)

                            def _gpsimd(eng):
                                for oc in range(2, OC):
                                    if oc == 2:
                                        eng.wait_ge(pe.h, pe0 + 3)
                                    elif oc == 3:
                                        eng.wait_ge(pe.h, pe0 + 7)
                                    else:
                                        eng.wait_ge(pe.h,
                                                    m_lora((oc - 2) * NCo + NCo - 1))
                                    wS.issue(nc.gpsimd.dma_start(
                                        out=wcol8[oc % 2][:, 0:HC * 128],
                                        in_=wb_t[oc].rearrange("p hc j -> p (hc j)")), oc)

                            blk.tensor(_tensor)
                            blk.vector(_vector)
                            blk.scalar(_scalar)
                            blk.sync(_sync)
                            blk.gpsimd(_gpsimd)

                    barrier()
        SQ.close()

    return nc


def prep_inputs(d: Dims, ins):
    x = np.asarray(ins["L0_attn_x"])
    Wqkv = np.asarray(ins["L0_attn_Wqkv"])
    bqkv = np.ascontiguousarray(np.asarray(ins["L0_attn_bqkv"]))
    Wo = np.asarray(ins["L0_attn_Wo"])
    bo = np.ascontiguousarray(np.asarray(ins["L0_attn_bo"]))
    Wb = np.asarray(ins["L0_ffn_W_base"])
    bb = np.ascontiguousarray(np.asarray(ins["L0_ffn_b_base"]))
    loraA = np.asarray(ins["L0_ffn_lora_A"])
    loraB = np.asarray(ins["L0_ffn_lora_B"])
    routerW = np.ascontiguousarray(np.asarray(ins["L0_ffn_router_W"]))
    routerb = np.asarray(ins["L0_ffn_router_b"])
    prior = np.asarray(ins["L0_ffn_domain_prior"])
    active = np.ascontiguousarray(np.asarray(ins["L0_ffn_active"]).astype(np.int32))

    H, S_own, S_all = d.H, d.S_own, d.S_all
    HC, DC, FC, OC = d.HC, d.DC, d.FC, d.OC

    wqkv_t = np.ascontiguousarray(Wqkv.reshape(HC, 128, FC, 128).transpose(2, 1, 0, 3))
    wo_t = np.ascontiguousarray(Wo.reshape(DC, 128, OC, 128).transpose(2, 1, 0, 3))
    wb_t = np.ascontiguousarray(Wb.reshape(OC, 128, HC, 128).transpose(0, 3, 2, 1))
    loraA_f = np.ascontiguousarray(loraA.reshape(d.E_tot * d.R, H))
    loraBrt_f = np.ascontiguousarray(loraB.transpose(0, 2, 1).reshape(d.E_tot * d.R, H))
    routerb_c = np.ascontiguousarray(routerb.reshape(d.E_tot, 1))
    prior_c = np.ascontiguousarray(prior.reshape(d.E_tot, 1))

    shared = {
        "wqkv_t": wqkv_t, "wo_t": wo_t, "wb_t": wb_t,
        "bqkv": bqkv, "bo": bo, "bb": bb,
        "loraA": loraA_f, "loraBrt": loraBrt_f,
        "routerW": routerW, "routerb": routerb_c, "prior": prior_c,
        "active": active,
    }
    in_maps = []
    halves = S_all // S_own
    n_cores = x.shape[0] * halves
    for core in range(n_cores):
        b = core // halves
        h = core % halves
        m = dict(shared)
        m["x_fm"] = np.ascontiguousarray(x[b][h * S_own:(h + 1) * S_own].T)
        in_maps.append(m)
    return in_maps


def assemble_output(d: Dims, results):
    halves = d.S_all // d.S_own
    B = len(results) // halves
    out = np.empty((B, d.S_all, d.H), dtype=np.float32)
    for core in range(len(results)):
        b = core // halves
        h = core % halves
        out[b, h * d.S_own:(h + 1) * d.S_own, :] = results[core]["out"].T
    return out


def kernel(**inputs):
    d = Dims()
    nc = build_program(d)
    in_maps = prep_inputs(d, inputs)
    res = run_bass_kernel_spmd(nc, in_maps, core_ids=list(range(N_CORES)))
    return assemble_output(d, res.results)
